# version 18
# baseline (speedup 1.0000x reference)
import numpy as np
import ml_dtypes
from contextlib import ExitStack

import concourse.bass as bass
import concourse.mybir as mybir
from concourse.bass_utils import run_bass_kernel_spmd

F32 = mybir.dt.float32
BF16 = mybir.dt.bfloat16
I32 = mybir.dt.int32
ALU = mybir.AluOpType
AX = mybir.AxisListType
AF = mybir.ActivationFunctionType
BF = ml_dtypes.bfloat16

D = 2048
S = 2048
M4 = 4
DI = 4096
NCK = 16
NCI = 32
OWN = 1024
CH = 512
ATT_SCALE = float(128) ** -0.5
LN_EPS = 1e-5
ATT_EPS = 1e-6
Q_EPS = 1e-5

LOCAL_ORDER = {0: [0, 3, 1, 2], 1: [1, 2, 0, 3]}
PAIRS = [(0, 0, "diag", None), (0, 2, "bias", 0), (1, 0, "bias", 1),
         (1, 1, "diag", None), (1, 2, "bias", 2), (1, 3, "bias", 3)]


class Sem:
    def __init__(self, nc, es, name):
        self.h = es.enter_context(nc.semaphore(name))
        self.n = 0


class K:
    ENGINES = ("sync", "gpsimd", "scalar", "vector", "tensor")

    def __init__(self):
        self.ops = {e: [] for e in self.ENGINES}
        self.waited = {e: {} for e in self.ENGINES}
        self.fence = []
        self.chain = {}

    def emit(self, eng, fn, waits=(), inc=None, inc_k=1):
        wl = []
        for wv in list(self.fence) + list(waits):
            if wv is None:
                continue
            s, v = wv
            if s is None or v is None or v <= 0:
                continue
            if v > self.waited[eng].get(id(s), 0):
                wl.append((s.h, v))
                self.waited[eng][id(s)] = v
        if inc is not None:
            inc.n += inc_k

        def g(e, fn=fn, wl=wl, inc=inc, inc_k=inc_k):
            for (h, v) in wl:
                e.wait_ge(h, v)
            r = fn(e)
            if inc is not None:
                r.then_inc(inc.h, inc_k)
        self.ops[eng].append(g)

    def drain(self, eng, inc=None):
        self.emit(eng, lambda e: e.drain(), inc=inc)

    def ch(self, sem):
        return [(sem, self.chain.get(id(sem), 0))]

    def chend(self, sem):
        self.chain[id(sem)] = sem.n


PH = {"A0": 0, "A1": 1, "A2": 2, "B0": 3, "B1": 4, "B2": 5, "B3": 6,
      "C1": 7, "C2": 8, "C3": 9, "C4": 10, "C5": 11}
class Overlay:

    def __init__(self, nc):
        self.nc = nc
        self.base = (int(nc.sbuf_base) + 31) // 32 * 32
        self.total = int(nc.sbuf_top) - self.base
        self.live = []
        self.peak = 0

    def tensor(self, name, shape, dtype, p0, p1, at=None):
        per = 1
        for s_ in shape[1:]:
            per *= s_
        per = (per * mybir.dt.size(dtype) + 31) // 32 * 32
        if at is not None:
            self.live.append((at, per, p1))
            if not hasattr(self, "addr"):
                self.addr = {}
            self.addr[name] = at
            return self.nc.alloc_sbuf_tensor_at(
                name, list(shape), dtype, offset=self.base + at)
        self.live = [t for t in self.live if t[2] >= p0]
        self.live.sort()
        gaps = []
        prev = 0
        for (st, sz, _e) in self.live:
            if st - prev >= per:
                gaps.append((st - prev, prev))
            prev = max(prev, st + sz)
        if self.total - prev >= per:
            gaps.append((self.total - prev, prev))
        assert gaps, (
            f"overlay overflow: {name} needs {per}B, live "
            f"{[(s, z, e) for (s, z, e) in self.live]}")
        if per < 16384:
            gsz, gst = max(gaps, key=lambda g: g[1])
            addr = gst + gsz - per
        else:
            addr = min(gaps)[1]
        self.live.append((addr, per, p1))
        self.peak = max(self.peak, addr + per)
        if not hasattr(self, "addr"):
            self.addr = {}
        self.addr[name] = addr
        return self.nc.alloc_sbuf_tensor_at(
            name, list(shape), dtype, offset=self.base + addr)


def build_program(dbg=False):
    nc = bass.Bass("TRN2", target_bir_lowering=False)
    dt = nc.dram_tensor
    xT = dt("xT", [D, S], F32, kind="ExternalInput")
    x_own = dt("x_own", [OWN, D], F32, kind="ExternalInput")
    wqT = dt("wqT", [D, D], F32, kind="ExternalInput")
    wkT = dt("wkT", [D, D], F32, kind="ExternalInput")
    wvT = dt("wvT", [D, D], F32, kind="ExternalInput")
    woT = dt("woT", [D, D], F32, kind="ExternalInput")
    mem = dt("mem", [M4 * D, D], F32, kind="ExternalInput")
    mnT = dt("mnT", [D, M4], F32, kind="ExternalInput")
    wupT = dt("wupT", [D, DI], F32, kind="ExternalInput")
    wdnT = dt("wdnT", [DI, D], F32, kind="ExternalInput")
    gate = dt("gate", [1, 1], F32, kind="ExternalInput")
    tri = dt("tri", [128, 4 * CH], BF16, kind="ExternalInput")
    biasc = dt("biasc", [128, 4], F32, kind="ExternalInput")
    identb = dt("identb", [128, 128], BF16, kind="ExternalInput")
    onescb = dt("onescb", [128, 1], BF16, kind="ExternalInput")
    onesrf = dt("onesrf", [1, 128], F32, kind="ExternalInput")
    onescf = dt("onescf", [128, 1], F32, kind="ExternalInput")
    out = dt("out", [OWN, D], F32, kind="ExternalOutput")
    dbgt = {}
    if dbg:
        for nm, shp, dty in (("d_qT", [128, NCK * OWN], BF16),
                             ("d_sigT", [128, NCK * OWN], BF16),
                             ("d_sbar", [128, NCK], F32),
                             ("d_rel", [1, 16], F32),
                             ("d_coef", [128, 8 * M4], F32),
                             ("d_lsT", [128, NCK * OWN], BF16),
                             ("d_memtok", [128, 8 * D], BF16),
                             ("d_x2", [128, 8 * D], BF16),
                             ("d_rt", [128, 8 * DI], BF16),
                             ("d_y2", [128, 8 * D], BF16),
                             ("d_sw", [1, 8], F32)):
            dbgt[nm] = dt(nm, shp, dty, kind="ExternalOutput")

    root = ExitStack()
    with root:
        k = K()
        sems = root.enter_context(ExitStack())

        def mksem(name):
            return Sem(nc, sems, name)

        gld = mksem("gld")
        sld = mksem("sld")
        sst = mksem("sst")
        tmm = mksem("tmm")
        vve = mksem("vve")
        sse = mksem("sse")
        s_kv = mksem("s_kv")
        s_me = [mksem(f"s_me{i}") for i in range(2)]
        s_m2 = [mksem(f"s_m2{i}") for i in range(2)]
        s_wo = [mksem(f"s_wo{i}") for i in range(2)]
        s_up = mksem("s_up")
        s_dn = [mksem(f"s_dn{i}") for i in range(2)]
        s_xo = [mksem(f"s_xo{i}") for i in range(2)]
        s_sw = [mksem(f"s_sw{i}") for i in range(2)]
        gcc = mksem("gcc")
        sdbg = mksem("sdbg")

        ov = Overlay(nc)

        def endphase():
            k.fence = [(tmm, tmm.n), (vve, vve.n), (sse, sse.n),
                       (sst, sst.n), (sdbg, sdbg.n)]

        P = lambda name, p, c, d_: ov.tensor(name, [p, c], d_, 0, 11)
        biasb = P("biasb", 128, 4, F32)
        idb = P("idb", 128, 128, BF16)
        onecb = P("onecb", 128, 1, BF16)
        onerf = P("onerf", 1, 128, F32)
        onecf = P("onecf", 128, 1, F32)
        sbarP = P("sbarP", 128, 2 * NCK, F32)
        sbarO = P("sbarO", 128, NCK, F32)
        sbarF = P("sbarF", 128, NCK, F32)
        gb = P("gb", 128, 2, F32)
        wtilb = P("wtilb", 128, M4, F32)
        coefb = P("coefb", 128, 8 * M4, F32)
        rr = P("rr", 1, 16, F32)
        swupb = P("swupb", 128, 1, F32)
        swdnb = P("swdnb", 128, 1, F32)
        thrup = P("thrup", 128, 1, F32)
        thrdn = P("thrdn", 128, 1, F32)
        swpart2 = P("swpart2", 128, 128, F32)
        srow = [P(f"srow{i}", 128, 1, F32) for i in range(8)]
        f1col = P("f1col", 128, 8, F32)
        f2col = P("f2col", 128, 8, F32)
        swr = P("swr", 1, 8, F32)
        scq = ov.tensor("scq", [128, 2048], F32, 0, 11)
        sciT = ov.tensor("sciT", [128, 2048], I32, 0, 11)
        adjb = ov.tensor("adjb", [128, 2048], BF16, 0, 11)

        ps_all = root.enter_context(nc.psum_tensor("ps_all", [128, 4096], F32))
        ps = [ps_all[:, i * 512:(i + 1) * 512] for i in range(8)]
        psb_all = ps_all.bitcast(BF16)
        psb = [psb_all[:, i * 1024:(i + 1) * 1024] for i in range(8)]
        psc = [ps_all[:, 0:1024], ps_all[:, 3072:4096]]
        bank_guard = [None] * 8

        def bank_w(b):
            return [bank_guard[b]] if bank_guard[b] is not None else []

        qT = ov.tensor("qT", [128, NCK * OWN], BF16, 0, 3)
        mntf = ov.tensor("mntf", [128, NCK * M4], F32, 0, 3)
        mntb = ov.tensor("mntb", [128, NCK * M4], BF16, 0, 3)
        xTb = ov.tensor("xTb", [128, NCK * S], BF16, 0, 2)
        wqb = ov.tensor("wqb", [128, NCK * D], BF16, 0, 1)
        swst = [ov.tensor("swst0", [128, 1024], F32, 0, 2, at=152736),
                ov.tensor("swst1", [128, 1024], F32, 0, 2, at=131072)]
        scrA1 = ov.tensor("scrA1", [128, 1024], F32, 0, 1)
        scrA2 = ov.tensor("scrA2", [128, 1024], F32, 0, 1)

        xchunk_ld = []
        for lc in range(4):
            for dc in range(NCK):
                k.emit("gpsimd", lambda g, lc=lc, dc=dc: g.dma_start(
                    xTb[:, dc * S + lc * CH: dc * S + (lc + 1) * CH],
                    xT.ap()[dc * 128:(dc + 1) * 128, lc * CH:(lc + 1) * CH]),
                    waits=k.ch(gld) if dc == 0 else (), inc=gld, inc_k=16)
            xchunk_ld.append(gld.n)
            k.chend(gld)
        for dst, src in ((biasb, biasc), (idb, identb), (onecb, onescb),
                         (onerf, onesrf), (onecf, onescf)):
            k.emit("sync", lambda sy, dst=dst, src=src: sy.dma_start(
                dst[:], src.ap()), waits=k.ch(sld), inc=sld, inc_k=16)
            k.chend(sld)
        k.emit("sync", lambda sy: sy.dma_start(
            mntf[:].rearrange("p (c m) -> p c m", m=M4),
            mnT.ap().rearrange("(c p) m -> p c m", p=128)),
            waits=k.ch(sld), inc=sld, inc_k=16)
        k.chend(sld)
        k.emit("gpsimd", lambda g: g.dma_start(
            mntb[:].rearrange("p (c m) -> p c m", m=M4),
            mnT.ap().rearrange("(c p) m -> p c m", p=128)),
            waits=k.ch(gld), inc=gld, inc_k=16)
        k.chend(gld)
        k.emit("sync", lambda sy: sy.dma_start(gb[0:1, 0:1], gate.ap()),
               waits=k.ch(sld), inc=sld, inc_k=16)
        k.chend(sld)
        consts_ld = sld.n
        k.emit("gpsimd", lambda g: g.dma_start(
            wqb[:, 0:8 * D].rearrange("p (c w) -> p c w", w=D),
            wqT.ap()[0:1024, :].rearrange("(c p) w -> p c w", p=128)),
            waits=k.ch(gld), inc=gld, inc_k=16)
        wq_h0 = gld.n
        k.chend(gld)
        k.emit("gpsimd", lambda g: g.dma_start(
            wqb[:, 8 * D:16 * D].rearrange("p (c w) -> p c w", w=D),
            wqT.ap()[1024:2048, :].rearrange("(c p) w -> p c w", p=128)),
            waits=k.ch(gld), inc=gld, inc_k=16)
        wq_ld = gld.n
        k.chend(gld)

        swjobs = []
        for wi, w_dram in enumerate((wupT, wdnT)):
            nrb = (D if wi == 0 else DI) // 128
            ncol = (DI if wi == 0 else D)
            for rb in range(nrb):
                for chb in range(ncol // 1024):
                    swjobs.append((w_dram, rb, chb, len(swjobs)))
        sw_guard = [None, None]
        sw_abs_prev = [0]

        def emit_swjob2(j):
            w_dram, rb, chb, idx = j
            buf = idx % 2
            k.emit("sync", lambda sy, w_dram=w_dram, rb=rb, chb=chb, buf=buf: sy.dma_start(
                swst[buf][:],
                w_dram.ap()[rb * 128:(rb + 1) * 128,
                            chb * 1024:(chb + 1) * 1024]),
                waits=[sw_guard[buf]] + k.ch(s_sw[buf]),
                inc=s_sw[buf], inc_k=16)
            k.chend(s_sw[buf])
            ldv = s_sw[buf].n
            k.emit("vector", lambda v, buf=buf, idx=idx: v.reduce_sum(
                swpart2[:, idx:idx + 1], swst[buf][:], AX.X,
                apply_absolute_value=True),
                waits=[(s_sw[buf], ldv)], inc=vve)
            sw_guard[buf] = (vve, vve.n)

        qPscr = ov.tensor("qPscr", [128, 1024], F32, 0, 1)
        sigch = {"exp": 0, "v": 0}
        swq = list(swjobs)
        for oc in range(NCK):
            for lc in range(4):
                bank = lc
                wts = [(gld, max(wq_h0, xchunk_ld[lc]))] + bank_w(bank)
                for dc in range(NCK):
                    k.emit("tensor", lambda t, oc=oc, lc=lc, dc=dc, bank=bank: t.matmul(
                        ps[bank][:, 0:CH],
                        wqb[:, dc * D + oc * 128: dc * D + (oc + 1) * 128],
                        xTb[:, dc * S + lc * CH: dc * S + (lc + 1) * CH],
                        start=(dc == 0), stop=(dc == NCK - 1)),
                        waits=wts if dc == 0
                        else ([(gld, wq_ld)] if dc == 8 else ()),
                        inc=tmm if dc == NCK - 1 else None)
                mmv = tmm.n
                if lc < 2:
                    k.emit("vector", lambda v, oc=oc, lc=lc, bank=bank: v.tensor_copy(
                        qT[:, oc * OWN + lc * CH: oc * OWN + (lc + 1) * CH],
                        ps[bank][:, 0:CH]),
                        waits=[(tmm, mmv), (vve, vve.n)], inc=vve)
                    bank_guard[bank] = (vve, vve.n)
                else:
                    par = lc - 2
                    k.emit("vector", lambda v, bank=bank, par=par: v.tensor_copy(
                        qPscr[:, par * CH:(par + 1) * CH], ps[bank][:, 0:CH]),
                        waits=[(tmm, mmv), (vve, sigch["v"])], inc=vve)
                    qcp = vve.n
                    bank_guard[bank] = (vve, qcp)
                    k.emit("scalar", lambda s, par=par: s.activation(
                        scrA1[:, par * CH:(par + 1) * CH],
                        qPscr[:, par * CH:(par + 1) * CH], AF.Relu,
                        scale=-1.0),
                        waits=[(vve, qcp), (sse, sigch["exp"])], inc=sse)
                    k.emit("scalar", lambda s, par=par: s.activation(
                        scrA2[:, par * CH:(par + 1) * CH],
                        scrA1[:, par * CH:(par + 1) * CH], AF.Exp,
                        scale=-1.0), waits=[(sse, sse.n)], inc=sse)
                    sigch["exp"] = sse.n
                    k.emit("vector", lambda v, par=par: v.scalar_tensor_tensor(
                        scrA1[:, par * CH:(par + 1) * CH],
                        qPscr[:, par * CH:(par + 1) * CH],
                        0.0, scrA2[:, par * CH:(par + 1) * CH],
                        ALU.max, ALU.add),
                        waits=[(sse, sigch["exp"]), (vve, vve.n)], inc=vve)
                    k.emit("vector", lambda v, oc=oc, par=par: v.reduce_sum(
                        sbarP[:, par * NCK + oc: par * NCK + oc + 1],
                        scrA1[:, par * CH:(par + 1) * CH], AX.X),
                        waits=[(vve, vve.n)], inc=vve)
                    sigch["v"] = vve.n
        a1_v = vve.n
        a1_t = tmm.n
        endphase()

        lsT = ov.tensor("lsT", [128, NCK * OWN], BF16, 2, 6)
        khh = ov.tensor("khh", [128, 2 * S], BF16, 2, 2)
        vpp = ov.tensor("vpp", [128, 16 * 256], BF16, 2, 2)
        kw = ov.tensor("kw", [128, NCK * 256], BF16, 2, 2)
        vw = ov.tensor("vw", [128, NCK * 256], BF16, 2, 2)
        expm = ov.tensor("expm", [128, 2 * 1024], BF16, 2, 2)
        trib = ov.tensor("trib", [128, 4 * CH], BF16, 2, 2)
        denr = ov.tensor("denr", [1, CH], F32, 2, 2)
        denb = ov.tensor("denb", [128, CH], F32, 2, 2)

        k.emit("sync", lambda sy: sy.dma_start(trib[:], tri.ap()),
               waits=k.ch(sld), inc=sld, inc_k=16)
        tri_ld = sld.n
        k.chend(sld)

        k.emit("scalar", lambda s: s.activation(rr[0:1, 15:16], gb[0:1, 0:1],
                                                AF.Sigmoid),
               waits=[(sld, consts_ld)], inc=sse)
        gsig = sse.n
        wts = [(sse, gsig)] + bank_w(2)
        k.emit("tensor", lambda t: t.matmul(ps[2][:, 0:1], onerf[:, :],
                                            rr[0:1, 15:16], start=True,
                                            stop=True), waits=wts, inc=tmm)
        k.emit("vector", lambda v: v.tensor_copy(gb[:, 0:1], ps[2][:, 0:1]),
               waits=[(tmm, tmm.n)], inc=vve)
        bank_guard[2] = (vve, vve.n)
        k.drain("vector", inc=vve)
        k.emit("vector", lambda v: v.tensor_scalar(gb[:, 1:2], gb[:, 0:1],
                                                   -1.0, 1.0, ALU.mult,
                                                   ALU.add), inc=vve)
        k.drain("vector", inc=vve)
        gb_v = vve.n

        pairs_by_o = {0: [p for p in PAIRS if p[0] == 0],
                      1: [p for p in PAIRS if p[0] == 1]}
        kw_t = 0
        vw_t = 0
        att_t = a1_t
        sc_guard = [[], []]
        expm_rd = {}
        expp_rd = {}
        grp_guard = [[], []]
        den_bc_t = [0]
        finq = []

        def emit_group_fin(info):
            h, o, par, pvb, dnb, pv_bank, dn_bank, att_mm = info
            k.emit("vector", lambda v, dnb=dnb: v.reciprocal(
                denr[0:1, 0:CH], dnb[0:1, 0:CH]),
                waits=[(tmm, att_mm), (vve, gb_v),
                       (tmm, den_bc_t[0])], inc=vve)
            dn_guard_v = vve.n
            k.drain("vector", inc=vve)
            k.emit("vector", lambda v: v.tensor_single_scalar(
                denr[0:1, 0:CH], denr[0:1, 0:CH], gb[0:1, 1:2],
                ALU.mult), inc=vve)
            k.drain("vector", inc=vve)
            drv = vve.n
            wts = [(vve, drv)] + bank_w(6)
            k.emit("tensor", lambda t: t.matmul(
                ps[6][:, 0:CH], onerf[:, :], denr[0:1, 0:CH],
                start=True, stop=True), waits=wts, inc=tmm)
            den_bc_t[0] = tmm.n
            k.emit("vector", lambda v: v.tensor_copy(
                denb[:], ps[6][:, 0:CH]), waits=[(tmm, den_bc_t[0])],
                inc=vve)
            bank_guard[6] = (vve, vve.n)
            k.drain("vector", inc=vve)
            k.emit("vector", lambda v, h=h, o=o, pvb=pvb: v.tensor_mul(
                lsT[:, h * OWN + o * CH: h * OWN + (o + 1) * CH],
                pvb[:, 0:CH], denb[:]), inc=vve)
            grp_guard[par] = [(vve, vve.n)]
            bank_guard[pv_bank] = (vve, vve.n)
            bank_guard[dn_bank] = (vve, dn_guard_v)

        gidx = 0
        for hp in range(8):
            k.emit("gpsimd", lambda g, hp=hp: g.dma_start(
                kw[:].rearrange("p (c w) -> p c w", w=256),
                wkT.ap()[:, hp * 256:(hp + 1) * 256].rearrange(
                    "(c p) w -> p c w", p=128)),
                waits=[(tmm, kw_t)] + k.ch(s_kv), inc=s_kv, inc_k=16)
            kld = s_kv.n
            k.chend(s_kv)
            k.emit("gpsimd", lambda g, hp=hp: g.dma_start(
                vw[:].rearrange("p (c w) -> p c w", w=256),
                wvT.ap()[:, hp * 256:(hp + 1) * 256].rearrange(
                    "(c p) w -> p c w", p=128)),
                waits=[(tmm, vw_t)] + k.ch(s_kv), inc=s_kv, inc_k=16)
            vld = s_kv.n
            k.chend(s_kv)
            for hip in range(2):
                for st in range(4):
                    bank = (hip * 4 + st) % 2
                    wts = [(s_kv, kld), (gld, wq_ld)] + bank_w(bank)
                    for dc in range(NCK):
                        k.emit("tensor", lambda t, hip=hip, st=st, dc=dc, bank=bank: t.matmul(
                            ps[bank][:, 0:CH],
                            kw[:, dc * 256 + hip * 128: dc * 256 + hip * 128 + 128],
                            xTb[:, dc * S + st * CH: dc * S + (st + 1) * CH],
                            start=(dc == 0), stop=(dc == NCK - 1)),
                            waits=wts if dc == 0 else (),
                            inc=tmm if dc == NCK - 1 else None)
                    k.emit("scalar", lambda s, hip=hip, st=st, bank=bank: s.activation(
                        khh[:, hip * S + st * CH: hip * S + (st + 1) * CH],
                        ps[bank][:, 0:CH], AF.Copy),
                        waits=[(tmm, max(tmm.n, att_t)), (sse, sse.n)],
                        inc=sse)
                    bank_guard[bank] = (sse, sse.n)
            kw_t = tmm.n
            while finq:
                emit_group_fin(finq.pop(0))
            for ss in range(16):
                bank = 2 + ss % 2
                wts = [(s_kv, vld)] + bank_w(bank)
                for dc in range(NCK):
                    k.emit("tensor", lambda t, ss=ss, dc=dc, bank=bank: t.matmul(
                        ps[bank][:, 0:256],
                        xTb[:, dc * S + ss * 128: dc * S + (ss + 1) * 128],
                        vw[:, dc * 256:(dc + 1) * 256],
                        start=(dc == 0), stop=(dc == NCK - 1)),
                        waits=wts if dc == 0 else (),
                        inc=tmm if dc == NCK - 1 else None)
                k.emit("scalar", lambda s, ss=ss, bank=bank: s.activation(
                    vpp[:, ss * 256:(ss + 1) * 256], ps[bank][:, 0:256],
                    AF.Copy),
                    waits=[(tmm, max(tmm.n, att_t)), (sse, sse.n)], inc=sse)
                bank_guard[bank] = (sse, sse.n)
            kvgen_v = (sse, sse.n)
            vw_t = tmm.n

            for hip in range(2):
                h = hp * 2 + hip
                for o in range(2):
                    plist = pairs_by_o[o]
                    blocks = [(pc, kb, kind, bi)
                              for (po, pc, kind, bi) in plist
                              for kb in range(4)]
                    bpairs = [tuple(blocks[2 * j:2 * j + 2])
                              for j in range(len(blocks) // 2)]
                    nblk = len(blocks)
                    par = gidx % 2
                    pvb = ps[4] if par == 0 else ps[2]
                    dnb = ps[5] if par == 0 else ps[3]
                    pv_bank = 4 if par == 0 else 2
                    dn_bank = 5 if par == 0 else 3

                    pending = None

                    def emit_denpv(pend):
                        (j, bp, rdy) = pend
                        slot = j % 2
                        for half, (pc, kb, kind, bi) in enumerate(bp):
                            bi0 = 2 * j + half
                            first = (bi0 == 0)
                            last = (bi0 == nblk - 1)
                            dwts = [rdy]
                            if first:
                                dwts += (grp_guard[par] + bank_w(pv_bank)
                                         + bank_w(dn_bank))
                            k.emit("tensor", lambda t, slot=slot, half=half, first=first, last=last, dnb=dnb: t.matmul(
                                dnb[0:1, 0:CH], onecb[:, 0:1],
                                expm[:, slot * 1024 + half * CH:
                                     slot * 1024 + (half + 1) * CH],
                                start=first, stop=last), waits=dwts, inc=tmm)
                            k.emit("tensor", lambda t, hip=hip, pc=pc, kb=kb, slot=slot, half=half, first=first, last=last, pvb=pvb: t.matmul(
                                pvb[:, 0:CH],
                                vpp[:, (pc * 4 + kb) * 256 + hip * 128:
                                    (pc * 4 + kb) * 256 + hip * 128 + 128],
                                expm[:, slot * 1024 + half * CH:
                                     slot * 1024 + (half + 1) * CH],
                                start=first, stop=last), inc=tmm)
                        expm_rd[slot] = tmm.n

                    for j, bp in enumerate(bpairs):
                        scb = j % 2
                        scbanks = (0, 1) if scb == 0 else (6, 7)
                        for half, (pc, kb, kind, bi) in enumerate(bp):
                            wts = [kvgen_v, (sld, tri_ld)]
                            if half == 0:
                                wts += (sc_guard[scb] + bank_w(scbanks[0])
                                        + bank_w(scbanks[1]))
                            diag = bp[0][2] == "diag"
                            k.emit("tensor", lambda t, hip=hip, h=h, o=o, pc=pc, kb=kb, scb=scb, half=half, diag=diag: t.matmul(
                                psc[scb][:, half * CH:(half + 1) * CH],
                                khh[:, hip * S + pc * CH + kb * 128:
                                    hip * S + pc * CH + kb * 128 + 128],
                                qT[:, h * OWN + o * CH: h * OWN + (o + 1) * CH],
                                start=True, stop=not diag),
                                waits=wts, inc=None if diag else tmm)
                        if bp[0][2] == "diag":
                            kb0 = bp[0][1]
                            for half in range(2):
                                k.emit("tensor", lambda t, scb=scb, kb0=kb0, half=half: t.matmul(
                                    psc[scb][:, half * CH:(half + 1) * CH],
                                    idb[:],
                                    trib[:, (kb0 + half) * CH:
                                         (kb0 + half + 1) * CH],
                                    start=False, stop=True),
                                    waits=[(sld, tri_ld)] if half == 0 else (),
                                    inc=tmm if half == 1 else None)
                        smm = tmm.n
                        slot = j % 2
                        kind = bp[0][2]
                        if kind == "bias":
                            bi = bp[0][3]
                            k.emit("scalar", lambda s, scb=scb, slot=slot, bi=bi: s.activation(
                                expm[:, slot * 1024:(slot + 1) * 1024],
                                psc[scb][:, 0:1024], AF.Exp,
                                bias=biasb[:, bi:bi + 1], scale=ATT_SCALE),
                                waits=[(tmm, max(smm, expm_rd.get(slot, 0)))],
                                inc=sse)
                            rdy = (sse, sse.n)
                            sc_guard[scb] = [(sse, sse.n)]
                            bank_guard[scbanks[0]] = (sse, sse.n)
                            bank_guard[scbanks[1]] = (sse, sse.n)
                        else:
                            k.emit("scalar", lambda s, scb=scb, slot=slot: s.activation(
                                expm[:, slot * 1024:(slot + 1) * 1024],
                                psc[scb][:, 0:1024], AF.Exp, scale=ATT_SCALE),
                                waits=[(tmm, max(smm, expm_rd.get(slot, 0)))],
                                inc=sse)
                            rdy = (sse, sse.n)
                            sc_guard[scb] = [(sse, sse.n)]
                            bank_guard[scbanks[0]] = (sse, sse.n)
                            bank_guard[scbanks[1]] = (sse, sse.n)
                        if pending is not None:
                            emit_denpv(pending)
                        pending = (j, bp, rdy)
                    emit_denpv(pending)
                    att_mm = tmm.n
                    finq.append((h, o, par, pvb, dnb, pv_bank, dn_bank,
                                 att_mm))
                    if len(finq) > 1:
                        emit_group_fin(finq.pop(0))
                    att_t = tmm.n
                    gidx += 1
                    for _ in range(4):
                        if swq:
                            emit_swjob2(swq.pop(0))
        while finq:
            emit_group_fin(finq.pop(0))
        while swq:
            emit_swjob2(swq.pop(0))
        a2_t = tmm.n
        a2_v = vve.n
        endphase()

        sigT = ov.tensor("sigT", [128, NCK * OWN], BF16, 3, 4)
        scrB1 = ov.tensor("scrB1", [128, 2048], F32, 3, 4)
        scrB2 = ov.tensor("scrB2", [128, 2048], F32, 3, 4)

        ch = {"e": {}, "v": {}}
        for oc in range(NCK):
            off = (oc % 2) * 1024
            k.emit("scalar", lambda s, oc=oc, off=off: s.activation(
                scrB1[:, off:off + 1024], qT[:, oc * OWN:(oc + 1) * OWN],
                AF.Relu, scale=-1.0),
                waits=[(vve, ch["v"].get(oc - 2, a2_v))], inc=sse)
            k.emit("scalar", lambda s, off=off: s.activation(
                scrB2[:, off:off + 1024], scrB1[:, off:off + 1024],
                AF.Exp, scale=-1.0), waits=[(sse, sse.n)], inc=sse)
            ch["e"][oc] = sse.n
            k.emit("vector", lambda v, oc=oc, off=off: v.scalar_tensor_tensor(
                sigT[:, oc * OWN:(oc + 1) * OWN],
                qT[:, oc * OWN:(oc + 1) * OWN], 0.0, scrB2[:, off:off + 1024],
                ALU.max, ALU.add), waits=[(sse, ch["e"][oc]), (vve, vve.n)],
                inc=vve)
            k.emit("vector", lambda v, oc=oc: v.reduce_sum(
                sbarO[:, oc:oc + 1], sigT[:, oc * OWN:(oc + 1) * OWN], AX.X),
                waits=[(vve, vve.n)], inc=vve)
            ch["v"][oc] = vve.n
        sigT_v = vve.n
        k.emit("vector", lambda v: v.tensor_add(sbarF[:], sbarO[:],
                                                sbarP[:, 0:NCK]),
               waits=[(vve, vve.n)], inc=vve)
        k.emit("vector", lambda v: v.tensor_add(sbarF[:], sbarF[:],
                                                sbarP[:, NCK:2 * NCK]),
               waits=[(vve, vve.n)], inc=vve)
        k.drain("vector", inc=vve)
        sbar_v = vve.n

        wts = [(vve, sbar_v), (sld, consts_ld)] + bank_w(0)
        for oc in range(NCK):
            k.emit("tensor", lambda t, oc=oc: t.matmul(
                ps[0][0:1, 0:M4], sbarF[:, oc:oc + 1],
                mntf[:, oc * M4:(oc + 1) * M4],
                start=(oc == 0), stop=(oc == NCK - 1)),
                waits=wts if oc == 0 else (),
                inc=tmm if oc == NCK - 1 else None)
        rel_mm = tmm.n
        k.emit("vector", lambda v: v.tensor_single_scalar(
            rr[0:1, 0:M4], ps[0][0:1, 0:M4], 1.0 / S, ALU.mult),
            waits=[(tmm, rel_mm)], inc=vve)
        bank_guard[0] = (vve, vve.n)
        k.drain("vector", inc=vve)
        k.emit("vector", lambda v: v.reduce_max(rr[0:1, 4:5], rr[0:1, 0:M4],
                                                AX.X), inc=vve)
        k.drain("vector", inc=vve)
        k.emit("vector", lambda v: v.tensor_single_scalar(
            rr[0:1, 5:9], rr[0:1, 0:M4], rr[0:1, 4:5], ALU.subtract),
            inc=vve)
        k.drain("vector", inc=vve)
        vv0 = vve.n
        k.emit("scalar", lambda s: s.activation(rr[0:1, 9:13], rr[0:1, 5:9],
                                                AF.Exp), waits=[(vve, vv0)],
               inc=sse)
        k.emit("vector", lambda v: v.reduce_sum(rr[0:1, 13:14], rr[0:1, 9:13],
                                                AX.X), waits=[(sse, sse.n)],
               inc=vve)
        k.drain("vector", inc=vve)
        k.emit("vector", lambda v: v.reciprocal(rr[0:1, 14:15],
                                                rr[0:1, 13:14]), inc=vve)
        k.drain("vector", inc=vve)
        k.emit("vector", lambda v: v.tensor_single_scalar(
            rr[0:1, 9:13], rr[0:1, 9:13], rr[0:1, 14:15], ALU.mult), inc=vve)
        k.drain("vector", inc=vve)
        wtil_v = vve.n
        wts = [(vve, wtil_v)] + bank_w(1)
        k.emit("tensor", lambda t: t.matmul(ps[1][:, 0:M4], onerf[:, :],
                                            rr[0:1, 9:13], start=True,
                                            stop=True), waits=wts, inc=tmm)
        k.emit("vector", lambda v: v.tensor_copy(wtilb[:], ps[1][:, 0:M4]),
               waits=[(tmm, tmm.n)], inc=vve)
        bank_guard[1] = (vve, vve.n)
        k.drain("vector", inc=vve)
        wtilb_v = vve.n

        for ss in range(8):
            bank = 2 + ss % 2
            wts = [(vve, sigT_v), (gld, gld.n)] + bank_w(bank)
            for oc in range(NCK):
                k.emit("tensor", lambda t, ss=ss, oc=oc, bank=bank: t.matmul(
                    ps[bank][:, 0:M4],
                    sigT[:, oc * OWN + ss * 128: oc * OWN + (ss + 1) * 128],
                    mntb[:, oc * M4:(oc + 1) * M4],
                    start=(oc == 0), stop=(oc == NCK - 1)),
                    waits=wts if oc == 0 else (),
                    inc=tmm if oc == NCK - 1 else None)
            rmm = tmm.n
            k.emit("vector", lambda v, bank=bank: v.tensor_single_scalar(
                scrB1[:, 0:M4], ps[bank][:, 0:M4], ATT_EPS, ALU.max),
                waits=[(tmm, rmm), (vve, wtilb_v)], inc=vve)
            bank_guard[bank] = (vve, vve.n)
            k.drain("vector", inc=vve)
            k.emit("vector", lambda v: v.reciprocal(scrB2[:, 0:M4],
                                                    scrB1[:, 0:M4]), inc=vve)
            k.drain("vector", inc=vve)
            k.emit("vector", lambda v, ss=ss: v.tensor_mul(
                coefb[:, ss * M4:(ss + 1) * M4], scrB2[:, 0:M4], wtilb[:]),
                inc=vve)
            k.drain("vector", inc=vve)
        coef_v = vve.n

        if dbg:
            for nm, tile, w in (("d_qT", qT, (vve, a1_v)),
                                ("d_sbar", sbarF, (vve, sbar_v)),
                                ("d_rel", rr, (vve, wtil_v)),
                                ("d_coef", coefb, (vve, coef_v)),
                                ("d_lsT", lsT, (vve, a2_v)),
                                ("d_sigT", sigT, (vve, sigT_v))):
                k.emit("sync", lambda sy, nm=nm, tile=tile: sy.dma_start(
                    dbgt[nm].ap(), tile[:]), waits=[w] + k.ch(sdbg),
                    inc=sdbg, inc_k=16)
                k.chend(sdbg)
        dbg_gate0 = sdbg.n
        endphase()

        membuf = [ov.tensor(f"membuf{i}", [128, M4 * NCK * 256], BF16, 4, 4)
                  for i in range(2)]
        memtok = ov.tensor("memtok", [128, 8 * D], BF16, 4, 5)

        mem_t = [a2_t, a2_t]
        ev_v = 0
        ev_s = 0
        for et in range(8):
            buf = et % 2
            for m in range(M4):
                c0 = m * NCK * 256
                k.emit("gpsimd", lambda g, et=et, m=m, buf=buf, c0=c0: g.dma_start(
                    membuf[buf][:, c0:c0 + NCK * 256].rearrange(
                        "p (c w) -> p c w", w=256),
                    mem.ap()[m * D:(m + 1) * D,
                             et * 256:(et + 1) * 256].rearrange(
                        "(c p) w -> p c w", p=128)),
                    waits=([(tmm, mem_t[buf]), (sdbg, dbg_gate0)]
                           + k.ch(s_me[buf]) if m == 0 else ()),
                    inc=s_me[buf], inc_k=16)
            mld = s_me[buf].n
            k.chend(s_me[buf])
            for ss in range(8):
                grp = 4 * (ss % 2)
                wts0 = [(s_me[buf], mld), (vve, coef_v)]
                for m in range(M4):
                    bank = grp + m
                    wts = wts0 + bank_w(bank)
                    for dc in range(NCK):
                        k.emit("tensor", lambda t, ss=ss, m=m, dc=dc, bank=bank, buf=buf: t.matmul(
                            ps[bank][:, 0:256],
                            sigT[:, dc * OWN + ss * 128: dc * OWN + (ss + 1) * 128],
                            membuf[buf][:, (m * NCK + dc) * 256:
                                        (m * NCK + dc + 1) * 256],
                            start=(dc == 0), stop=(dc == NCK - 1)),
                            waits=wts if dc == 0 else (),
                            inc=tmm if dc == NCK - 1 else None)
                amm = tmm.n
                par = (et * 8 + ss) % 2
                off = 512 * par
                k.emit("scalar", lambda s, grp=grp, ss=ss, off=off: s.activation(
                    scrB1[:, off:off + 256], ps[grp + 0][:, 0:256], AF.Copy,
                    scale=coefb[:, ss * M4:ss * M4 + 1]),
                    waits=[(tmm, amm), (vve, ev_v)], inc=sse)
                k.emit("scalar", lambda s, grp=grp, ss=ss, off=off: s.activation(
                    scrB1[:, off + 256:off + 512], ps[grp + 2][:, 0:256],
                    AF.Copy, scale=coefb[:, ss * M4 + 2:ss * M4 + 3]),
                    inc=sse)
                sev = sse.n
                k.emit("vector", lambda v, grp=grp, ss=ss, off=off: v.scalar_tensor_tensor(
                    scrB2[:, off:off + 256], ps[grp + 1][:, 0:256],
                    coefb[:, ss * M4 + 1:ss * M4 + 2], scrB1[:, off:off + 256],
                    ALU.mult, ALU.add), waits=[(sse, sev)], inc=vve)
                k.emit("vector", lambda v, grp=grp, ss=ss, off=off: v.scalar_tensor_tensor(
                    scrB2[:, off + 256:off + 512], ps[grp + 3][:, 0:256],
                    coefb[:, ss * M4 + 3:ss * M4 + 4],
                    scrB1[:, off + 256:off + 512], ALU.mult, ALU.add),
                    inc=vve)
                k.drain("vector", inc=vve)
                k.emit("vector", lambda v, ss=ss, et=et, off=off: v.tensor_add(
                    memtok[:, ss * D + et * 256: ss * D + (et + 1) * 256],
                    scrB2[:, off:off + 256], scrB2[:, off + 256:off + 512]),
                    inc=vve)
                for m in range(M4):
                    bank_guard[grp + m] = (vve, vve.n)
                ev_v = vve.n
                ev_s = sev
            mem_t[buf] = tmm.n
        k.drain("vector", inc=vve)
        k.drain("scalar", inc=sse)
        memtok_v = vve.n
        swp_v, swp_s = vve.n, sse.n

        k.emit("vector", lambda v: v.reduce_sum(scrB1[:, 0:1],
                                                swpart2[:, 0:64], AX.X),
               waits=[(vve, swp_v), (sse, swp_s)], inc=vve)
        k.drain("vector", inc=vve)
        k.emit("vector", lambda v: v.reduce_sum(scrB1[:, 1:2],
                                                swpart2[:, 64:128], AX.X),
               inc=vve)
        k.drain("vector", inc=vve)
        swred_v = vve.n
        wts = [(vve, swred_v)] + bank_w(0)
        k.emit("tensor", lambda t: t.matmul(ps[0][0:1, 0:1], scrB1[:, 0:1],
                                            onecf[:, 0:1], start=True,
                                            stop=True), waits=wts, inc=tmm)
        k.emit("tensor", lambda t: t.matmul(ps[0][0:1, 1:2], scrB1[:, 1:2],
                                            onecf[:, 0:1], start=True,
                                            stop=True), inc=tmm)
        swmm = tmm.n
        k.emit("vector", lambda v: v.tensor_copy(
            swr[0:1, 0:2], ps[0][0:1, 0:2]), waits=[(tmm, swmm)], inc=vve)
        bank_guard[0] = (vve, vve.n)
        k.drain("vector", inc=vve)
        k.emit("vector", lambda v: v.tensor_scalar(
            swr[0:1, 2:4], swr[0:1, 0:2], 1.0 / (D * DI), Q_EPS,
            ALU.mult, ALU.max), inc=vve)
        k.drain("vector", inc=vve)
        k.emit("vector", lambda v: v.tensor_single_scalar(
            swr[0:1, 4:6], swr[0:1, 2:4], 0.5, ALU.mult), inc=vve)
        k.drain("vector", inc=vve)
        swsc_v = vve.n
        wts = [(vve, swsc_v)] + bank_w(1)
        k.emit("tensor", lambda t: t.matmul(ps[1][:, 0:4], onerf[:, :],
                                            swr[0:1, 2:6], start=True,
                                            stop=True), waits=wts, inc=tmm)
        swbc = tmm.n
        for (dst, j) in ((swupb, 0), (swdnb, 1), (thrup, 2), (thrdn, 3)):
            k.emit("vector", lambda v, dst=dst, j=j: v.tensor_copy(
                dst[:], ps[1][:, j:j + 1]), waits=[(tmm, swbc)], inc=vve)
        bank_guard[1] = (vve, vve.n)
        k.drain("vector", inc=vve)
        sw_v = vve.n

        if dbg:
            k.emit("sync", lambda sy: sy.dma_start(
                dbgt["d_memtok"].ap(), memtok[:]),
                waits=[(vve, memtok_v)] + k.ch(sdbg), inc=sdbg, inc_k=16)
            k.chend(sdbg)
            k.emit("sync", lambda sy: sy.dma_start(
                dbgt["d_sw"].ap()[0:1, 0:6], swr[0:1, 0:6]),
                waits=[(vve, sw_v)] + k.ch(sdbg), inc=sdbg, inc_k=16)
            k.chend(sdbg)
        dbg_gate1 = sdbg.n
        endphase()

        attn_inT = lsT
        for ss in range(8):
            for oc in range(NCK):
                bank = (ss * NCK + oc) % 2
                wts = [(vve, memtok_v), (sld, consts_ld),
                       (sdbg, dbg_gate1)] + bank_w(bank)
                k.emit("tensor", lambda t, ss=ss, oc=oc, bank=bank: t.transpose(
                    psb[bank][:, 0:128],
                    memtok[:, ss * D + oc * 128: ss * D + (oc + 1) * 128],
                    idb[:]), waits=wts, inc=tmm)
                k.emit("vector", lambda v, ss=ss, oc=oc, bank=bank: v.scalar_tensor_tensor(
                    attn_inT[:, oc * OWN + ss * 128: oc * OWN + (ss + 1) * 128],
                    psb[bank][:, 0:128], gb[:, 0:1],
                    lsT[:, oc * OWN + ss * 128: oc * OWN + (ss + 1) * 128],
                    ALU.mult, ALU.add), waits=[(tmm, tmm.n), (vve, a2_v)],
                    inc=vve)
                bank_guard[bank] = (vve, vve.n)
        ai_v = vve.n
        endphase()

        x2 = ov.tensor("x2", [128, 8 * D], BF16, 6, 11, at=139264)
        wos = [ov.tensor("wos0", [128, NCK * CH], BF16, 6, 6, at=0),
               ov.tensor("wos1", [128, NCK * CH], BF16, 6, 6, at=16384)]
        xos = [ov.tensor(f"xos{i}", [128, CH], F32, 6, 6) for i in range(2)]
        sqscr = ov.tensor("sqscr", [128, CH], F32, 6, 11)
        pg1 = ov.tensor("pg1", [128, 32], F32, 6, 8)
        qg1 = ov.tensor("qg1", [128, 32], F32, 6, 8)
        sc3 = ov.tensor("sc3", [128, 8], F32, 6, 11)
        ncol = ov.tensor("ncol", [128, 8], F32, 6, 11)
        sr8 = ov.tensor("sr8", [128, 48], F32, 0, 11)
        wfs = ov.tensor("wfs", [128, NCK * 256], F32, 6, 8, at=73728)
        wub = [ov.tensor("wub0", [128, NCK * 256], BF16, 6, 8, at=90112),
               ov.tensor("wub1", [128, NCK * 256], BF16, 6, 8, at=65536)]
        tnb = ov.tensor("tnb", [128, 2048], BF16, 6, 10, at=172032)
        xqT = ov.tensor("xqT", [128, 8 * D], BF16, 7, 8, at=106496)
        xqtile = ov.tensor("xqtile", [128, D], BF16, 7, 8)
        rtq = ov.tensor("rtq", [128, 9 * DI], BF16, 8, 10, at=0)
        wdb = [ov.tensor("wdb0", [128, NCI * 256], BF16, 9, 10, at=73728),
               ov.tensor("wdb1", [128, NCI * 256], BF16, 9, 10, at=106496)]
        wfd = [ov.tensor("wfd0", [128, 8 * 256], F32, 9, 10, at=122880),
               ov.tensor("wfd1", [128, 8 * 256], F32, 9, 10, at=131072)]
        sciT2 = ov.tensor("sciT2", [128, 2048], I32, 9, 10, at=90112)
        scq2 = ov.tensor("scq2", [128, 2048], F32, 9, 10, at=98304)
        adjb2 = ov.tensor("adjb2", [128, 2048], BF16, 9, 10,
                          at=ov.addr["xqtile"])
        obuf = [ov.tensor("obuf0", [128, D], F32, 10, 11, at=90112),
                ov.tensor("obuf1", [128, D], F32, 10, 11, at=98304)]
        pmax = ov.tensor("pmax", [128, 32], F32, 8, 9)
        pgrid = ov.tensor("pgrid", [128, 64], F32, 9, 11)
        qgrid = ov.tensor("qgrid", [128, 64], F32, 9, 11)
        rqtile = ov.tensor("rqtile", [128, 2048], BF16, 9, 10)

        xob = [ov.tensor(f"xob{i}", [128, CH], BF16, 6, 6) for i in range(2)]
        mng = ov.tensor("mng", [128, 32], F32, 6, 8)
        mxg = ov.tensor("mxg", [128, 32], F32, 6, 8)
        cq1 = ov.tensor("cq1", [128, 16], F32, 6, 8)

        wo_t = [0, 0]
        xo_v = [0, 0]
        xob_t = [0, 0]

        last_s = [sse.n]
        last_v = [vve.n]
        last_g = [gcc.n]

        def s_emit(fn, waits=()):
            k.emit("scalar", fn, waits=[(sse, last_s[0])] + list(waits),
                   inc=sse)
            last_s[0] = sse.n
            return sse.n

        def v_emit(fn, waits=()):
            k.emit("vector", fn, waits=[(vve, last_v[0])] + list(waits),
                   inc=vve)
            last_v[0] = vve.n
            return vve.n

        def g_emit(fn, waits=()):
            k.emit("gpsimd", fn, waits=[(gcc, last_g[0])] + list(waits),
                   inc=gcc)
            last_g[0] = gcc.n
            return gcc.n

        up_ld = {}
        tern_half = {}
        up_t = [0, 0]
        tern_v = {}

        def emit_wfs_dma(it):
            lds = []
            for dh in range(2):
                sem = s_up if dh == 0 else s_sw[0]
                qeng = "gpsimd" if dh == 0 else "sync"
                w0 = [tern_half[(it - 1, dh)]] if it >= 1 else []
                k.emit(qeng, lambda g, it=it, dh=dh: g.dma_start(
                    wfs[:, dh * 2048:(dh + 1) * 2048].rearrange(
                        "p (c w) -> p c w", w=256),
                    wupT.ap()[dh * 1024:(dh + 1) * 1024,
                              it * 256:(it + 1) * 256].rearrange(
                        "(c p) w -> p c w", p=128)),
                    waits=w0 + k.ch(sem), inc=sem, inc_k=16)
                lds.append((sem, sem.n))
                k.chend(sem)
            up_ld[it] = lds

        def emit_tern_up(it, eng="vector", extra=()):
            buf = it % 2
            em = v_emit if eng == "vector" else g_emit
            sem = vve if eng == "vector" else gcc
            tb = tnb if eng == "vector" else tnbg
            for dh in range(2):
                em(lambda v, dh=dh, tb=tb: v.tensor_scalar(
                    tb[:, 0:2048], wfs[:, dh * 2048:(dh + 1) * 2048], -1.0,
                    thrup[:, 0:1], ALU.mult, ALU.is_ge),
                    waits=[up_ld[it][dh], (tmm, up_t[buf]),
                           (vve, sw_v)] + list(extra))
                tv = em(lambda v, dh=dh, buf=buf, tb=tb: v.scalar_tensor_tensor(
                    wub[buf][:, dh * 2048:(dh + 1) * 2048],
                    wfs[:, dh * 2048:(dh + 1) * 2048], thrup[:, 0:1],
                    tb[:, 0:2048], ALU.is_ge, ALU.subtract))
                tern_half[(it, dh)] = (sem, tv)
            tern_v[it] = (sem, tv)



        for ht in range(4):
            buf = ht % 2
            for dq in range(4):
                k.emit("gpsimd", lambda g, ht=ht, dq=dq, buf=buf: g.dma_start(
                    wos[buf][:, dq * 4 * CH:(dq + 1) * 4 * CH].rearrange(
                        "p (c w) -> p c w", w=CH),
                    woT.ap()[dq * 512:(dq + 1) * 512,
                             ht * CH:(ht + 1) * CH].rearrange(
                        "(c p) w -> p c w", p=128)),
                    waits=([(tmm, wo_t[buf]), (sdbg, dbg_gate1)]
                           + k.ch(s_wo[buf]) if dq == 0 else ()),
                    inc=s_wo[buf], inc_k=16)
            wld = s_wo[buf].n
            k.chend(s_wo[buf])
            for ss in range(8):
                xb = ss % 2
                k.emit("sync", lambda sy, ss=ss, ht=ht, xb=xb: sy.dma_start(
                    xos[xb][:, 0:CH],
                    x_own.ap()[ss * 128:(ss + 1) * 128,
                               ht * CH:(ht + 1) * CH]),
                    waits=[(vve, xo_v[xb])] + k.ch(s_xo[xb]),
                    inc=s_xo[xb], inc_k=16)
                xld = s_xo[xb].n
                k.chend(s_xo[xb])
                xcv = v_emit(lambda v, xb=xb: v.tensor_copy(
                    xob[xb][:, 0:CH], xos[xb][:, 0:CH]),
                    waits=[(s_xo[xb], xld), (tmm, xob_t[xb])])
                xo_v[xb] = xcv
                bank = 2 + ss % 2
                wts = [(s_wo[buf], wld), (vve, ai_v)] + bank_w(bank)
                for dc in range(NCK):
                    k.emit("tensor", lambda t, ss=ss, dc=dc, bank=bank, buf=buf: t.matmul(
                        ps[bank][:, 0:CH],
                        attn_inT[:, dc * OWN + ss * 128: dc * OWN + (ss + 1) * 128],
                        wos[buf][:, dc * CH:(dc + 1) * CH],
                        start=(dc == 0), stop=False),
                        waits=wts if dc == 0 else ())
                k.emit("tensor", lambda t, bank=bank, xb=xb: t.matmul(
                    ps[bank][:, 0:CH], idb[:], xob[xb][:, 0:CH],
                    start=False, stop=True),
                    waits=[(vve, xcv)], inc=tmm)
                womm = tmm.n
                xob_t[xb] = womm
                ev = s_emit(lambda s, ss=ss, ht=ht, bank=bank: s.activation(
                    x2[:, ss * D + ht * CH: ss * D + (ht + 1) * CH],
                    ps[bank][:, 0:CH], AF.Copy,
                    accum_out=pg1[:, ss * 4 + ht: ss * 4 + ht + 1]),
                    waits=[(tmm, womm)])
                bank_guard[bank] = (sse, ev)
                s_emit(lambda s, ss=ss, ht=ht: s.activation(
                    sqscr[:, 0:CH],
                    x2[:, ss * D + ht * CH: ss * D + (ht + 1) * CH], AF.Square,
                    accum_out=qg1[:, ss * 4 + ht: ss * 4 + ht + 1]))
                v_emit(lambda v, ss=ss, ht=ht: v.tensor_reduce(
                    mng[:, ss * 4 + ht: ss * 4 + ht + 1],
                    x2[:, ss * D + ht * CH: ss * D + (ht + 1) * CH], AX.X,
                    ALU.min), waits=[(sse, ev)])
                v_emit(lambda v, ss=ss, ht=ht: v.tensor_reduce(
                    mxg[:, ss * 4 + ht: ss * 4 + ht + 1],
                    x2[:, ss * D + ht * CH: ss * D + (ht + 1) * CH], AX.X,
                    ALU.max))
                if ht == 3:
                    v_emit(lambda v, ss=ss: v.reduce_sum(
                        sr8[:, ss:ss + 1], pg1[:, ss * 4:(ss + 1) * 4],
                        AX.X), waits=[(sse, last_s[0])])
                    v_emit(lambda v, ss=ss: v.reduce_sum(
                        sr8[:, 24 + ss:25 + ss], qg1[:, ss * 4:(ss + 1) * 4],
                        AX.X))
                    v_emit(lambda v, ss=ss: v.tensor_reduce(
                        ncol[:, ss:ss + 1], mng[:, ss * 4:(ss + 1) * 4],
                        AX.X, ALU.min))
                    v_emit(lambda v, ss=ss: v.tensor_reduce(
                        sc3[:, ss:ss + 1], mxg[:, ss * 4:(ss + 1) * 4],
                        AX.X, ALU.max))
            wo_t[buf] = tmm.n
        b3_last = tmm.n
        b3_sse = last_s[0]
        emit_wfs_dma(0)
        emit_tern_up(0)
        emit_wfs_dma(1)
        emit_tern_up(1)
        emit_wfs_dma(2)

        v_emit(lambda v: v.tensor_single_scalar(
            sr8[:, 8:16], sr8[:, 0:8], 1.0 / D, ALU.mult))
        v_emit(lambda v: v.tensor_mul(sr8[:, 16:24], sr8[:, 8:16],
                                      sr8[:, 8:16]))
        v_emit(lambda v: v.tensor_single_scalar(
            sr8[:, 16:24], sr8[:, 16:24], LN_EPS, ALU.subtract))
        v_emit(lambda v: v.scalar_tensor_tensor(
            sr8[:, 32:40], sr8[:, 24:32], 1.0 / D, sr8[:, 16:24],
            ALU.mult, ALU.subtract))
        sqv = s_emit(lambda s: s.activation(
            sr8[:, 40:48], sr8[:, 32:40], AF.Sqrt),
            waits=[(vve, last_v[0])])
        v_emit(lambda v: v.reciprocal(sr8[:, 40:48], sr8[:, 40:48]),
               waits=[(sse, sqv)])
        v_emit(lambda v: v.tensor_sub(sc3[:, 0:8], sc3[:, 0:8],
                                      sr8[:, 8:16]))
        v_emit(lambda v: v.tensor_sub(ncol[:, 0:8], sr8[:, 8:16],
                                      ncol[:, 0:8]))
        v_emit(lambda v: v.tensor_tensor(sc3[:, 0:8], sc3[:, 0:8],
                                         ncol[:, 0:8], ALU.max))
        v_emit(lambda v: v.tensor_mul(sc3[:, 0:8], sc3[:, 0:8],
                                      sr8[:, 40:48]))
        v_emit(lambda v: v.tensor_single_scalar(
            sc3[:, 0:8], sc3[:, 0:8], Q_EPS, ALU.max))
        v_emit(lambda v: v.tensor_scalar(
            f1col[:, 0:8], sc3[:, 0:8], 1.0 / 127.0, swupb[:, 0:1],
            ALU.mult, ALU.mult), waits=[(vve, sw_v)])
        v_emit(lambda v: v.reciprocal(ncol[:, 0:8], sc3[:, 0:8]))
        v_emit(lambda v: v.tensor_single_scalar(
            ncol[:, 0:8], ncol[:, 0:8], 127.0, ALU.mult))
        v_emit(lambda v: v.tensor_mul(cq1[:, 0:8], ncol[:, 0:8],
                                      sr8[:, 40:48]))
        v_emit(lambda v: v.tensor_mul(cq1[:, 8:16], sr8[:, 8:16],
                                      cq1[:, 0:8]))
        fin_v = v_emit(lambda v: v.tensor_scalar(
            cq1[:, 8:16], cq1[:, 8:16], -1.0, 128.0, ALU.mult, ALU.add))

        xq_evac = [0] * 8
        scq_s = [0] * 8
        c2mm = {}

        def emit_c2_mm(it, ss):
            bank = 2 + ss % 2
            buf = it % 2
            wts = [tern_v[it], (vve, xq_evac[ss])] + bank_w(bank)
            for dc in range(NCK):
                k.emit("tensor", lambda t, ss=ss, dc=dc, it=it, bank=bank, buf=buf: t.matmul(
                    ps[bank][:, 0:256],
                    xqT[:, ss * D + dc * 128: ss * D + dc * 128 + 128],
                    wub[buf][:, dc * 256:(dc + 1) * 256],
                    start=(dc == 0), stop=(dc == NCK - 1)),
                    waits=wts if dc == 0 else (),
                    inc=tmm if dc == NCK - 1 else None)
            umm = tmm.n
            ev = s_emit(lambda s, ss=ss, it=it, bank=bank: s.activation(
                rtq[:, ss * DI + it * 256: ss * DI + (it + 1) * 256],
                ps[bank][:, 0:256], AF.Relu, scale=f1col[:, ss:ss + 1]),
                waits=[(tmm, umm), (vve, fin_v)])
            bank_guard[bank] = (sse, ev)
            return ev

        evac_mark = {}
        tp_hist = {}
        xqv_hist = {}
        for ss in range(8):
            for h2 in range(2):
                step = ss * 2 + h2
                off = h2 * 1024
                xoff = ss * D + h2 * 1024
                scv = None
                s_emit(lambda s, off=off, xoff=xoff, ss=ss: s.activation(
                    sciT[:, off:off + 1024], x2[:, xoff:xoff + 1024],
                    AF.Relu, bias=cq1[:, 8 + ss:9 + ss],
                    scale=cq1[:, ss:ss + 1]),
                    waits=[(vve, fin_v), (vve, xqv_hist.get(step - 2, 0))])
                scv = s_emit(lambda s, off=off, xoff=xoff, ss=ss: s.activation(
                    scq[:, off:off + 1024], x2[:, xoff:xoff + 1024],
                    AF.Relu, bias=cq1[:, 8 + ss:9 + ss],
                    scale=cq1[:, ss:ss + 1]))
                if h2:
                    scq_s[ss] = scv
                v_emit(lambda v, off=off: v.scalar_tensor_tensor(
                    adjb[:, off:off + 1024], scq[:, off:off + 1024], -0.5,
                    sciT[:, off:off + 1024], ALU.add, ALU.is_gt),
                    waits=[(sse, scv)])
                xqv = v_emit(lambda v, off=off: v.scalar_tensor_tensor(
                    xqtile[:, off:off + 1024], sciT[:, off:off + 1024],
                    -128.0, adjb[:, off:off + 1024], ALU.add, ALU.add),
                    waits=[(tmm, tp_hist.get(step - 2, b3_last))])
                xqv_hist[step] = xqv
                bank = step % 2
                wts = [(vve, xqv)] + bank_w(bank)
                for j in range(8):
                    dc = h2 * 8 + j
                    k.emit("tensor", lambda t, dc=dc, j=j, bank=bank: t.transpose(
                        psb[bank][:, j * 128:(j + 1) * 128],
                        xqtile[:, dc * 128:(dc + 1) * 128], idb[:]),
                        waits=wts if j == 0 else (), inc=tmm if j == 7 else None)
                tpv = tmm.n
                tp_hist[step] = tpv
                ev = v_emit(lambda v, ss=ss, h2=h2, bank=bank: v.tensor_copy(
                    xqT[:, ss * D + h2 * 1024: ss * D + (h2 + 1) * 1024],
                    psb[bank][:, 0:1024]), waits=[(tmm, tpv)])
                bank_guard[bank] = (vve, ev)
            xq_evac[ss] = last_v[0]
            emit_c2_mm(0, ss)
            emit_c2_mm(1, ss)

        up_t[0] = tmm.n
        up_t[1] = tmm.n
        nrm_v = [0] * 8
        for it in range(2, NCK):
            emit_tern_up(it)
            if it + 1 < NCK:
                emit_wfs_dma(it + 1)
            if it - 2 < 8:
                ss = it - 2
                v_emit(lambda v, ss=ss: v.tensor_single_scalar(
                    x2[:, ss * D:(ss + 1) * D], x2[:, ss * D:(ss + 1) * D],
                    sr8[:, 8 + ss:9 + ss], ALU.subtract),
                    waits=[(sse, scq_s[ss])])
                nrm_v[ss] = v_emit(lambda v, ss=ss: v.tensor_single_scalar(
                    x2[:, ss * D:(ss + 1) * D], x2[:, ss * D:(ss + 1) * D],
                    sr8[:, 40 + ss:41 + ss], ALU.mult))
            if it % 4 == 0 and it >= 4:
                g = it // 4 - 1
                for ss in range(8):
                    v_emit(lambda v, ss=ss, g=g: v.reduce_max(
                        pmax[:, ss * 4 + g: ss * 4 + g + 1],
                        rtq[:, ss * DI + g * 1024: ss * DI + (g + 1) * 1024],
                        AX.X, apply_absolute_value=True),
                        waits=[(sse, evac_mark[(4 * g + 3, ss)])])
            for ss in range(8):
                evac_mark[(it, ss)] = emit_c2_mm(it, ss)
            up_t[it % 2] = tmm.n
        for ss in range(8):
            v_emit(lambda v, ss=ss: v.reduce_max(
                pmax[:, ss * 4 + 3: ss * 4 + 4],
                rtq[:, ss * DI + 3 * 1024: ss * DI + 4 * 1024],
                AX.X, apply_absolute_value=True),
                waits=[(sse, evac_mark[(15, ss)])])
        c2_last = tmm.n
        x2_v = nrm_v[7]

        if dbg:
            k.emit("sync", lambda sy: sy.dma_start(
                dbgt["d_x2"].ap(), x2[:]), waits=[(vve, x2_v)] + k.ch(sdbg),
                inc=sdbg, inc_k=16)
            k.chend(sdbg)
        dbg_gate2 = sdbg.n


        if dbg:
            k.emit("sync", lambda sy: sy.dma_start(
                dbgt["d_rt"].ap(), rtq[:, 0:8 * DI]),
                waits=[(vve, last_v[0])] + k.ch(sdbg), inc=sdbg, inc_k=16)
            k.chend(sdbg)
        dbg_gate3 = sdbg.n

        for ss in range(8):
            v_emit(lambda v, ss=ss: v.reduce_max(
                sr8[:, ss:ss + 1], pmax[:, ss * 4:(ss + 1) * 4], AX.X))
        v_emit(lambda v: v.tensor_mul(sr8[:, 8:16], sr8[:, 0:8], sr8[:, 0:8]))
        v_emit(lambda v: v.tensor_single_scalar(
            sr8[:, 8:16], sr8[:, 8:16], Q_EPS, ALU.max))
        v_emit(lambda v: v.reciprocal(sr8[:, 16:24], sr8[:, 8:16]))
        v_emit(lambda v: v.tensor_single_scalar(
            sr8[:, 16:24], sr8[:, 16:24], 127.0, ALU.mult))
        fv2 = v_emit(lambda v: v.tensor_scalar(
            f2col[:, 0:8], sr8[:, 8:16], 1.0 / 127.0, swdnb[:, 0:1],
            ALU.mult, ALU.mult))
        sqv = s_emit(lambda s: s.activation(sc3[:, 0:8], sr8[:, 16:24],
                                            AF.Sqrt),
                     waits=[(vve, fv2)])

        dn_ld = {}
        dn_t = [0, 0]
        terndn_v = {}
        tern_prev = {}

        def emit_wfd_dma(idx):
            sb_ = idx % 2
            qeng = "gpsimd" if sb_ == 0 else "sync"
            w0 = [(vve, tern_prev[idx - 2])] if idx >= 2 else [(tmm, c2_last)]
            ht, qb = idx // 4, idx % 4
            k.emit(qeng, lambda g, ht=ht, qb=qb, sb_=sb_: g.dma_start(
                wfd[sb_][:].rearrange("p (c w) -> p c w", w=256),
                wdnT.ap()[qb * 1024:(qb + 1) * 1024,
                          ht * 256:(ht + 1) * 256].rearrange(
                    "(c p) w -> p c w", p=128)),
                waits=w0 + [(sdbg, dbg_gate3)] + k.ch(s_dn[sb_]),
                inc=s_dn[sb_], inc_k=16)
            dn_ld[idx] = s_dn[sb_].n
            k.chend(s_dn[sb_])

        def emit_tern_dn_idx(idx):
            ht, qb = divmod(idx, 4)
            buf = ht % 2
            sb_ = idx % 2
            extra = [(tmm, c2_last)] if ht < 2 else []
            v_emit(lambda v, sb_=sb_: v.tensor_scalar(
                tnb[:, 0:2048], wfd[sb_][:, 0:2048], -1.0,
                thrdn[:, 0:1], ALU.mult, ALU.is_ge),
                waits=[(s_dn[sb_], dn_ld[idx]), (tmm, dn_t[buf])] + extra)
            tv = v_emit(lambda v, qb=qb, sb_=sb_, buf=buf: v.scalar_tensor_tensor(
                wdb[buf][:, qb * 2048:(qb + 1) * 2048],
                wfd[sb_][:, 0:2048], thrdn[:, 0:1],
                tnb[:, 0:2048], ALU.is_ge, ALU.subtract))
            tern_prev[idx] = tv
            if qb == 3:
                terndn_v[ht] = tv

        nd_dn = [0]
        nt_dn = [0]

        def adv_dn(tern_upto):
            while nt_dn[0] < tern_upto:
                while nd_dn[0] < min(nt_dn[0] + 2, 32):
                    emit_wfd_dma(nd_dn[0])
                    nd_dn[0] += 1
                emit_tern_dn_idx(nt_dn[0])
                nt_dn[0] += 1

        rq_done = [0] * 8
        tp3_done = {}

        def qbase(ss):
            return ((ss + 8) % 9) * DI

        qset = [(sciT, scq, adjb), (sciT2, scq2, adjb2)]
        rq_hist = {}
        pend_cells = []
        cell_k = [0]

        def emit_c4_mm(ht, ss):
            kk = cell_k[0]
            cell_k[0] += 1
            bank = 4 + kk % 4
            buf = ht % 2
            wts = [(vve, terndn_v[ht]), (vve, rq_done[ss])] + bank_w(bank)
            for ic in range(NCI):
                k.emit("tensor", lambda t, ss=ss, ic=ic, bank=bank, buf=buf: t.matmul(
                    ps[bank][:, 0:256],
                    rtq[:, qbase(ss) + ic * 128: qbase(ss) + ic * 128 + 128],
                    wdb[buf][:, ic * 256:(ic + 1) * 256],
                    start=(ic == 0), stop=(ic == NCI - 1)),
                    waits=wts if ic == 0 else (),
                    inc=tmm if ic == NCI - 1 else None)
            pend_cells.append((ht, ss, bank, tmm.n))

        def flush_cells(keep=0):
            while len(pend_cells) > keep:
                ht, ss, bank, dmm = pend_cells.pop(0)
                yv = v_emit(lambda v, ss=ss, ht=ht, bank=bank: v.scalar_tensor_tensor(
                    x2[:, ss * D + ht * 256: ss * D + (ht + 1) * 256],
                    ps[bank][:, 0:256], f2col[:, ss:ss + 1],
                    x2[:, ss * D + ht * 256: ss * D + (ht + 1) * 256],
                    ALU.mult, ALU.add), waits=[(tmm, dmm)])
                bank_guard[bank] = (vve, yv)
                s_emit(lambda s, ss=ss, ht=ht: s.activation(
                    sqscr[:, 0:256],
                    x2[:, ss * D + ht * 256: ss * D + (ht + 1) * 256],
                    AF.Copy,
                    accum_out=pgrid[:, ss * 8 + ht: ss * 8 + ht + 1]),
                    waits=[(vve, yv)])
                s_emit(lambda s, ss=ss, ht=ht: s.activation(
                    sqscr[:, 0:256],
                    x2[:, ss * D + ht * 256: ss * D + (ht + 1) * 256],
                    AF.Square,
                    accum_out=qgrid[:, ss * 8 + ht: ss * 8 + ht + 1]))

        for ss in range(8):
            for qc in range(4):
                step = ss * 4 + qc
                sci_, scq_, adj_ = qset[(step % 4) // 2]
                off = (step % 2) * 1024
                o0 = ss * DI + qc * 1024
                wts_ci = [(sdbg, dbg_gate3),
                          (vve, rq_hist[step - 4] if step >= 4
                           else last_v[0])]
                s_emit(lambda s, o0=o0, off=off, ss=ss, sci_=sci_: s.activation(
                    sci_[:, off:off + 1024], rtq[:, o0:o0 + 1024], AF.Square,
                    scale=sc3[:, ss:ss + 1]), waits=wts_ci)
                sqv2 = s_emit(lambda s, o0=o0, off=off, ss=ss, scq_=scq_: s.activation(
                    scq_[:, off:off + 1024], rtq[:, o0:o0 + 1024], AF.Square,
                    scale=sc3[:, ss:ss + 1]))
                v_emit(lambda v, off=off, sci_=sci_, scq_=scq_, adj_=adj_: v.scalar_tensor_tensor(
                    adj_[:, off:off + 1024], scq_[:, off:off + 1024], -0.5,
                    sci_[:, off:off + 1024], ALU.add, ALU.is_gt),
                    waits=[(sse, sqv2)])
                rqv = v_emit(lambda v, off=off, sci_=sci_, adj_=adj_: v.scalar_tensor_tensor(
                    rqtile[:, off:off + 1024], sci_[:, off:off + 1024],
                    0.0, adj_[:, off:off + 1024], ALU.add, ALU.add),
                    waits=[(tmm, tp3_done.get(step - 2, b3_last))])
                rq_hist[step] = rqv
                bank = step % 2
                wts = [(vve, rqv)] + bank_w(bank)
                for j in range(8):
                    k.emit("tensor", lambda t, off=off, j=j, bank=bank: t.transpose(
                        psb[bank][:, j * 128:(j + 1) * 128],
                        rqtile[:, off + j * 128: off + (j + 1) * 128],
                        idb[:]),
                        waits=wts if j == 0 else (),
                        inc=tmm if j == 7 else None)
                tp3_done[step] = tmm.n
                ev = v_emit(lambda v, ss=ss, qc=qc, bank=bank: v.tensor_copy(
                    rtq[:, qbase(ss) + qc * 1024: qbase(ss) + (qc + 1) * 1024],
                    psb[bank][:, 0:1024]), waits=[(tmm, tp3_done[step])])
                bank_guard[bank] = (vve, ev)
            rq_done[ss] = last_v[0]
            flush_cells()
            if ss < 2:
                adv_dn((ss + 1) * 4)
            emit_c4_mm(0, ss)
            if ss >= 1:
                emit_c4_mm(1, ss - 1)
        emit_c4_mm(1, 7)
        flush_cells()
        dn_t[0] = tmm.n
        dn_t[1] = tmm.n

        adv_dn(12)
        for ht in range(2, 8):
            for ss in range(8):
                emit_c4_mm(ht, ss)
                flush_cells(keep=1)
                if ss == 2 and ht < 7:
                    adv_dn((ht + 2) * 4)
            dn_t[ht % 2] = tmm.n
        flush_cells()
        y2_v = last_v[0]
        y2_s = last_s[0]

        if dbg:
            k.emit("sync", lambda sy: sy.dma_start(
                dbgt["d_y2"].ap(), x2[:]), waits=[(vve, y2_v), (sse, y2_s)]
                + k.ch(sdbg), inc=sdbg, inc_k=16)
            k.chend(sdbg)

        ob_st = [0, 0]
        for ss in range(8):
            obp = ss % 2
            v_emit(lambda v, ss=ss: v.reduce_sum(
                sr8[:, ss:ss + 1], pgrid[:, ss * 8:(ss + 1) * 8], AX.X),
                waits=[(sse, y2_s)])
            v_emit(lambda v, ss=ss: v.reduce_sum(
                sr8[:, 24 + ss:25 + ss], qgrid[:, ss * 8:(ss + 1) * 8], AX.X))
            v_emit(lambda v, ss=ss: v.tensor_single_scalar(
                sr8[:, 8 + ss:9 + ss], sr8[:, ss:ss + 1], 1.0 / D, ALU.mult))
            v_emit(lambda v, ss=ss: v.tensor_mul(
                sr8[:, 16 + ss:17 + ss], sr8[:, 8 + ss:9 + ss],
                sr8[:, 8 + ss:9 + ss]))
            v_emit(lambda v, ss=ss: v.tensor_single_scalar(
                sr8[:, 32 + ss:33 + ss], sr8[:, 24 + ss:25 + ss], 1.0 / D,
                ALU.mult))
            v_emit(lambda v, ss=ss: v.scalar_tensor_tensor(
                sr8[:, 32 + ss:33 + ss], sr8[:, 32 + ss:33 + ss], LN_EPS,
                sr8[:, 16 + ss:17 + ss], ALU.add, ALU.subtract))
            sv = s_emit(lambda s, ss=ss: s.activation(
                sr8[:, 40 + ss:41 + ss], sr8[:, 32 + ss:33 + ss], AF.Sqrt),
                waits=[(vve, last_v[0])])
            v_emit(lambda v, ss=ss: v.reciprocal(
                sr8[:, 40 + ss:41 + ss], sr8[:, 40 + ss:41 + ss]),
                waits=[(sse, sv)])
            v_emit(lambda v, ss=ss, obp=obp: v.tensor_single_scalar(
                obuf[obp][:], x2[:, ss * D:(ss + 1) * D],
                sr8[:, 8 + ss:9 + ss], ALU.subtract),
                waits=[(sst, ob_st[obp])])
            outv = v_emit(lambda v, ss=ss, obp=obp: v.tensor_single_scalar(
                obuf[obp][:], obuf[obp][:], sr8[:, 40 + ss:41 + ss],
                ALU.mult))
            k.emit("scalar", lambda s, ss=ss, obp=obp: s.dma_start(
                out.ap()[ss * 128:(ss + 1) * 128, :], obuf[obp][:]),
                waits=[(vve, outv)] + k.ch(sst), inc=sst, inc_k=16)
            ob_st[obp] = sst.n
            k.chend(sst)

        fin = sst.n
        k.emit("sync", lambda sy: sy.nop(),
               waits=[(sst, fin), (sdbg, sdbg.n)])

        with nc.Block() as blk:
            for eng in K.ENGINES:
                stream = k.ops[eng]
                if not stream:
                    continue

                def mk(stream=stream):
                    def _(e):
                        for f in stream:
                            f(e)
                    return _
                getattr(blk, eng)(mk())
    return nc


def _core_inputs(x_b, core):
    h = core % 2
    order = LOCAL_ORDER[h]
    perm = np.concatenate([np.arange(c * CH, (c + 1) * CH) for c in order])
    xTl = np.ascontiguousarray(x_b.T[:, perm])
    own = order[:2]
    x_own_ = np.ascontiguousarray(
        np.concatenate([x_b[c * CH:(c + 1) * CH] for c in own], axis=0))
    bias = np.zeros((128, 4), np.float32)
    for (po, pc, kind, bi) in PAIRS:
        if kind != "bias":
            continue
        q_g, k_g = order[po], order[pc]
        bias[:, bi] = 0.0 if k_g < q_g else -1e9
    return xTl, x_own_, bias, own


_CONSTS = None


def _consts():
    global _CONSTS
    if _CONSTS is None:
        tri = np.zeros((128, 4 * CH), BF)
        p = np.arange(128)[:, None]
        q = np.arange(CH)[None, :]
        for kb in range(4):
            tri[:, kb * CH:(kb + 1) * CH] = np.where(
                (kb * 128 + p) <= q, 0.0, -1e9).astype(BF)
        _CONSTS = {
            "tri": tri,
            "identb": np.eye(128, dtype=BF),
            "onescb": np.ones((128, 1), BF),
            "onesrf": np.ones((1, 128), np.float32),
            "onescf": np.ones((128, 1), np.float32),
        }
    return _CONSTS


def kernel(**inputs):
    cst = _consts()
    f32c = lambda a: np.ascontiguousarray(np.asarray(a, dtype=np.float32))
    wqTn = f32c(np.asarray(inputs["L0_attn_w_q"]).T)
    wkTn = f32c(np.asarray(inputs["L0_attn_w_k"]).T)
    wvTn = f32c(np.asarray(inputs["L0_attn_w_v"]).T)
    woTn = f32c(np.asarray(inputs["L0_attn_w_o"]).T)
    memn = f32c(np.asarray(inputs["L0_attn_memories"]).reshape(M4 * D, D))
    mnTn = f32c(np.asarray(inputs["L0_attn_memory_norms"]).T)
    wupTn = f32c(np.asarray(inputs["L0_ffn_w_up"]).T)
    wdnTn = f32c(np.asarray(inputs["L0_ffn_w_down"]).T)
    gaten = f32c(np.asarray(inputs["L0_attn_gate"]).reshape(1, 1))
    xall = np.asarray(inputs["L0_attn_x"], dtype=np.float32)

    nc = build_program()
    in_maps = []
    owns = []
    for core in range(8):
        xTl, x_own_, bias, own = _core_inputs(xall[core // 2], core)
        owns.append(own)
        in_maps.append({
            "xT": xTl, "x_own": x_own_, "wqT": wqTn, "wkT": wkTn,
            "wvT": wvTn, "woT": woTn, "mem": memn, "mnT": mnTn,
            "wupT": wupTn, "wdnT": wdnTn, "gate": gaten, "tri": cst["tri"],
            "biasc": bias, "identb": cst["identb"], "onescb": cst["onescb"],
            "onesrf": cst["onesrf"], "onescf": cst["onescf"],
        })
    res = run_bass_kernel_spmd(nc, in_maps, core_ids=list(range(8)))
    outp = np.empty((4, S, D), np.float32)
    for core in range(8):
        b = core // 2
        o = np.asarray(res.results[core]["out"])
        for j, c in enumerate(owns[core]):
            outp[b, c * CH:(c + 1) * CH] = o[j * CH:(j + 1) * CH]
    return outp


# version 19
# speedup vs baseline: 1.0379x; 1.0379x over previous
import numpy as np
import ml_dtypes
from contextlib import ExitStack

import concourse.bass as bass
import concourse.mybir as mybir
from concourse.bass_utils import run_bass_kernel_spmd

F32 = mybir.dt.float32
BF16 = mybir.dt.bfloat16
I32 = mybir.dt.int32
ALU = mybir.AluOpType
AX = mybir.AxisListType
AF = mybir.ActivationFunctionType
BF = ml_dtypes.bfloat16

D = 2048
S = 2048
M4 = 4
DI = 4096
NCK = 16
NCI = 32
OWN = 1024
CH = 512
ATT_SCALE = float(128) ** -0.5
LN_EPS = 1e-5
ATT_EPS = 1e-6
Q_EPS = 1e-5

LOCAL_ORDER = {0: [0, 3, 1, 2], 1: [1, 2, 0, 3]}
PAIRS = [(0, 0, "diag", None), (0, 2, "bias", 0), (1, 0, "bias", 1),
         (1, 1, "diag", None), (1, 2, "bias", 2), (1, 3, "bias", 3)]


class Sem:
    def __init__(self, nc, es, name):
        self.h = es.enter_context(nc.semaphore(name))
        self.n = 0


class K:
    ENGINES = ("sync", "gpsimd", "scalar", "vector", "tensor")

    def __init__(self):
        self.ops = {e: [] for e in self.ENGINES}
        self.waited = {e: {} for e in self.ENGINES}
        self.fence = []
        self.chain = {}

    def emit(self, eng, fn, waits=(), inc=None, inc_k=1):
        wl = []
        for wv in list(self.fence) + list(waits):
            if wv is None:
                continue
            s, v = wv
            if s is None or v is None or v <= 0:
                continue
            if v > self.waited[eng].get(id(s), 0):
                wl.append((s.h, v))
                self.waited[eng][id(s)] = v
        if inc is not None:
            inc.n += inc_k

        def g(e, fn=fn, wl=wl, inc=inc, inc_k=inc_k):
            for (h, v) in wl:
                e.wait_ge(h, v)
            r = fn(e)
            if inc is not None:
                r.then_inc(inc.h, inc_k)
        self.ops[eng].append(g)

    def drain(self, eng, inc=None):
        self.emit(eng, lambda e: e.drain(), inc=inc)

    def ch(self, sem):
        return [(sem, self.chain.get(id(sem), 0))]

    def chend(self, sem):
        self.chain[id(sem)] = sem.n


PH = {"A0": 0, "A1": 1, "A2": 2, "B0": 3, "B1": 4, "B2": 5, "B3": 6,
      "C1": 7, "C2": 8, "C3": 9, "C4": 10, "C5": 11}
class Overlay:

    def __init__(self, nc):
        self.nc = nc
        self.base = (int(nc.sbuf_base) + 31) // 32 * 32
        self.total = int(nc.sbuf_top) - self.base
        self.live = []
        self.peak = 0

    def tensor(self, name, shape, dtype, p0, p1, at=None):
        per = 1
        for s_ in shape[1:]:
            per *= s_
        per = (per * mybir.dt.size(dtype) + 31) // 32 * 32
        if at is not None:
            self.live.append((at, per, p1))
            if not hasattr(self, "addr"):
                self.addr = {}
            self.addr[name] = at
            return self.nc.alloc_sbuf_tensor_at(
                name, list(shape), dtype, offset=self.base + at)
        self.live = [t for t in self.live if t[2] >= p0]
        self.live.sort()
        gaps = []
        prev = 0
        for (st, sz, _e) in self.live:
            if st - prev >= per:
                gaps.append((st - prev, prev))
            prev = max(prev, st + sz)
        if self.total - prev >= per:
            gaps.append((self.total - prev, prev))
        assert gaps, (
            f"overlay overflow: {name} needs {per}B, live "
            f"{[(s, z, e) for (s, z, e) in self.live]}")
        if per < 16384:
            gsz, gst = max(gaps, key=lambda g: g[1])
            addr = gst + gsz - per
        else:
            addr = min(gaps)[1]
        self.live.append((addr, per, p1))
        self.peak = max(self.peak, addr + per)
        if not hasattr(self, "addr"):
            self.addr = {}
        self.addr[name] = addr
        return self.nc.alloc_sbuf_tensor_at(
            name, list(shape), dtype, offset=self.base + addr)


def build_program(dbg=False):
    nc = bass.Bass("TRN2", target_bir_lowering=False)
    dt = nc.dram_tensor
    xT = dt("xT", [D, S], F32, kind="ExternalInput")
    x_own = dt("x_own", [OWN, D], F32, kind="ExternalInput")
    wqT = dt("wqT", [D, D], F32, kind="ExternalInput")
    wkT = dt("wkT", [D, D], F32, kind="ExternalInput")
    wvT = dt("wvT", [D, D], F32, kind="ExternalInput")
    woT = dt("woT", [D, D], F32, kind="ExternalInput")
    mem = dt("mem", [M4 * D, D], F32, kind="ExternalInput")
    mnT = dt("mnT", [D, M4], F32, kind="ExternalInput")
    wupT = dt("wupT", [D, DI], F32, kind="ExternalInput")
    wdnT = dt("wdnT", [DI, D], F32, kind="ExternalInput")
    gate = dt("gate", [1, 1], F32, kind="ExternalInput")
    tri = dt("tri", [128, 4 * CH], BF16, kind="ExternalInput")
    biasc = dt("biasc", [128, 4], F32, kind="ExternalInput")
    identb = dt("identb", [128, 128], BF16, kind="ExternalInput")
    onescb = dt("onescb", [128, 1], BF16, kind="ExternalInput")
    onesrf = dt("onesrf", [1, 128], F32, kind="ExternalInput")
    onescf = dt("onescf", [128, 1], F32, kind="ExternalInput")
    out = dt("out", [OWN, D], F32, kind="ExternalOutput")
    dbgt = {}
    if dbg:
        for nm, shp, dty in (("d_qT", [128, NCK * OWN], BF16),
                             ("d_sigT", [128, NCK * OWN], BF16),
                             ("d_sbar", [128, NCK], F32),
                             ("d_rel", [1, 16], F32),
                             ("d_coef", [128, 8 * M4], F32),
                             ("d_lsT", [128, NCK * OWN], BF16),
                             ("d_memtok", [128, 8 * D], BF16),
                             ("d_x2", [128, 8 * D], BF16),
                             ("d_rt", [128, 8 * DI], BF16),
                             ("d_y2", [128, 8 * D], BF16),
                             ("d_sw", [1, 8], F32)):
            dbgt[nm] = dt(nm, shp, dty, kind="ExternalOutput")

    root = ExitStack()
    with root:
        k = K()
        sems = root.enter_context(ExitStack())

        def mksem(name):
            return Sem(nc, sems, name)

        gld = mksem("gld")
        sld = mksem("sld")
        sst = mksem("sst")
        tmm = mksem("tmm")
        vve = mksem("vve")
        sse = mksem("sse")
        s_kv = mksem("s_kv")
        s_me = [mksem(f"s_me{i}") for i in range(2)]
        s_m2 = [mksem(f"s_m2{i}") for i in range(2)]
        s_wo = [mksem(f"s_wo{i}") for i in range(2)]
        s_up = mksem("s_up")
        s_dn = [mksem(f"s_dn{i}") for i in range(2)]
        s_xo = [mksem(f"s_xo{i}") for i in range(2)]
        s_sw = [mksem(f"s_sw{i}") for i in range(2)]
        gcc = mksem("gcc")
        sdbg = mksem("sdbg")

        ov = Overlay(nc)

        def endphase():
            k.fence = [(tmm, tmm.n), (vve, vve.n), (sse, sse.n),
                       (sst, sst.n), (sdbg, sdbg.n)]

        P = lambda name, p, c, d_: ov.tensor(name, [p, c], d_, 0, 11)
        biasb = P("biasb", 128, 4, F32)
        idb = P("idb", 128, 128, BF16)
        onecb = P("onecb", 128, 1, BF16)
        onerf = P("onerf", 1, 128, F32)
        onecf = P("onecf", 128, 1, F32)
        sbarP = P("sbarP", 128, 2 * NCK, F32)
        sbarO = P("sbarO", 128, NCK, F32)
        sbarF = P("sbarF", 128, NCK, F32)
        gb = P("gb", 128, 2, F32)
        wtilb = P("wtilb", 128, M4, F32)
        coefb = P("coefb", 128, 8 * M4, F32)
        rr = P("rr", 1, 16, F32)
        swupb = P("swupb", 128, 1, F32)
        swdnb = P("swdnb", 128, 1, F32)
        thrup = P("thrup", 128, 1, F32)
        thrdn = P("thrdn", 128, 1, F32)
        swpart2 = P("swpart2", 128, 128, F32)
        srow = [P(f"srow{i}", 128, 1, F32) for i in range(8)]
        f1col = P("f1col", 128, 8, F32)
        f2col = P("f2col", 128, 8, F32)
        swr = P("swr", 1, 8, F32)
        scq = ov.tensor("scq", [128, 2048], F32, 0, 11)
        sciT = ov.tensor("sciT", [128, 2048], I32, 0, 11)
        adjb = ov.tensor("adjb", [128, 2048], BF16, 0, 11)

        ps_all = root.enter_context(nc.psum_tensor("ps_all", [128, 4096], F32))
        ps = [ps_all[:, i * 512:(i + 1) * 512] for i in range(8)]
        psb_all = ps_all.bitcast(BF16)
        psb = [psb_all[:, i * 1024:(i + 1) * 1024] for i in range(8)]
        psc = [ps_all[:, 0:1024], ps_all[:, 3072:4096]]
        bank_guard = [None] * 8

        def bank_w(b):
            return [bank_guard[b]] if bank_guard[b] is not None else []

        qT = ov.tensor("qT", [128, NCK * OWN], BF16, 0, 3)
        mntf = ov.tensor("mntf", [128, NCK * M4], F32, 0, 3)
        mntb = ov.tensor("mntb", [128, NCK * M4], BF16, 0, 3)
        xTb = ov.tensor("xTb", [128, NCK * S], BF16, 0, 2)
        wqb = ov.tensor("wqb", [128, NCK * D], BF16, 0, 1)
        swst = [ov.tensor("swst0", [128, 1024], F32, 0, 2, at=152736),
                ov.tensor("swst1", [128, 1024], F32, 0, 2, at=131072)]
        scrA1 = ov.tensor("scrA1", [128, 1024], F32, 0, 1)
        scrA2 = ov.tensor("scrA2", [128, 1024], F32, 0, 1)

        xchunk_ld = []
        for lc in range(4):
            for dc in range(NCK):
                k.emit("gpsimd", lambda g, lc=lc, dc=dc: g.dma_start(
                    xTb[:, dc * S + lc * CH: dc * S + (lc + 1) * CH],
                    xT.ap()[dc * 128:(dc + 1) * 128, lc * CH:(lc + 1) * CH]),
                    waits=k.ch(gld) if dc == 0 else (), inc=gld, inc_k=16)
            xchunk_ld.append(gld.n)
            k.chend(gld)
        for dst, src in ((biasb, biasc), (idb, identb), (onecb, onescb),
                         (onerf, onesrf), (onecf, onescf)):
            k.emit("sync", lambda sy, dst=dst, src=src: sy.dma_start(
                dst[:], src.ap()), waits=k.ch(sld), inc=sld, inc_k=16)
            k.chend(sld)
        k.emit("sync", lambda sy: sy.dma_start(
            mntf[:].rearrange("p (c m) -> p c m", m=M4),
            mnT.ap().rearrange("(c p) m -> p c m", p=128)),
            waits=k.ch(sld), inc=sld, inc_k=16)
        k.chend(sld)
        k.emit("gpsimd", lambda g: g.dma_start(
            mntb[:].rearrange("p (c m) -> p c m", m=M4),
            mnT.ap().rearrange("(c p) m -> p c m", p=128)),
            waits=k.ch(gld), inc=gld, inc_k=16)
        k.chend(gld)
        k.emit("sync", lambda sy: sy.dma_start(gb[0:1, 0:1], gate.ap()),
               waits=k.ch(sld), inc=sld, inc_k=16)
        k.chend(sld)
        consts_ld = sld.n
        k.emit("gpsimd", lambda g: g.dma_start(
            wqb[:].rearrange("p (c w) -> p c w", w=D),
            wqT.ap().rearrange("(c p) w -> p c w", p=128)),
            waits=k.ch(gld), inc=gld, inc_k=16)
        wq_ld = gld.n
        k.chend(gld)

        swjobs = []
        for wi, w_dram in enumerate((wupT, wdnT)):
            nrb = (D if wi == 0 else DI) // 128
            ncol = (DI if wi == 0 else D)
            for rb in range(nrb):
                for chb in range(ncol // 1024):
                    swjobs.append((w_dram, rb, chb, len(swjobs)))
        sw_guard = [None, None]
        sw_abs_prev = [0]

        def emit_swjob2(j):
            w_dram, rb, chb, idx = j
            buf = idx % 2
            k.emit("sync", lambda sy, w_dram=w_dram, rb=rb, chb=chb, buf=buf: sy.dma_start(
                swst[buf][:],
                w_dram.ap()[rb * 128:(rb + 1) * 128,
                            chb * 1024:(chb + 1) * 1024]),
                waits=[sw_guard[buf]] + k.ch(s_sw[buf]),
                inc=s_sw[buf], inc_k=16)
            k.chend(s_sw[buf])
            ldv = s_sw[buf].n
            k.emit("vector", lambda v, buf=buf, idx=idx: v.reduce_sum(
                swpart2[:, idx:idx + 1], swst[buf][:], AX.X,
                apply_absolute_value=True),
                waits=[(s_sw[buf], ldv)], inc=vve)
            sw_guard[buf] = (vve, vve.n)

        qPscr = ov.tensor("qPscr", [128, 1024], F32, 0, 1)
        sigch = {"exp": 0, "v": 0}
        swq = list(swjobs)
        for oc in range(NCK):
            for lc in range(4):
                bank = lc
                wts = [(gld, max(wq_ld, xchunk_ld[lc]))] + bank_w(bank)
                for dc in range(NCK):
                    k.emit("tensor", lambda t, oc=oc, lc=lc, dc=dc, bank=bank: t.matmul(
                        ps[bank][:, 0:CH],
                        wqb[:, dc * D + oc * 128: dc * D + (oc + 1) * 128],
                        xTb[:, dc * S + lc * CH: dc * S + (lc + 1) * CH],
                        start=(dc == 0), stop=(dc == NCK - 1)),
                        waits=wts if dc == 0 else (),
                        inc=tmm if dc == NCK - 1 else None)
                mmv = tmm.n
                if lc < 2:
                    k.emit("vector", lambda v, oc=oc, lc=lc, bank=bank: v.tensor_copy(
                        qT[:, oc * OWN + lc * CH: oc * OWN + (lc + 1) * CH],
                        ps[bank][:, 0:CH]),
                        waits=[(tmm, mmv), (vve, vve.n)], inc=vve)
                    bank_guard[bank] = (vve, vve.n)
                else:
                    par = lc - 2
                    k.emit("vector", lambda v, bank=bank, par=par: v.tensor_copy(
                        qPscr[:, par * CH:(par + 1) * CH], ps[bank][:, 0:CH]),
                        waits=[(tmm, mmv), (vve, sigch["v"])], inc=vve)
                    qcp = vve.n
                    bank_guard[bank] = (vve, qcp)
                    k.emit("scalar", lambda s, par=par: s.activation(
                        scrA1[:, par * CH:(par + 1) * CH],
                        qPscr[:, par * CH:(par + 1) * CH], AF.Relu,
                        scale=-1.0),
                        waits=[(vve, qcp), (sse, sigch["exp"])], inc=sse)
                    k.emit("scalar", lambda s, par=par: s.activation(
                        scrA2[:, par * CH:(par + 1) * CH],
                        scrA1[:, par * CH:(par + 1) * CH], AF.Exp,
                        scale=-1.0), waits=[(sse, sse.n)], inc=sse)
                    sigch["exp"] = sse.n
                    k.emit("vector", lambda v, par=par: v.scalar_tensor_tensor(
                        scrA1[:, par * CH:(par + 1) * CH],
                        qPscr[:, par * CH:(par + 1) * CH],
                        0.0, scrA2[:, par * CH:(par + 1) * CH],
                        ALU.max, ALU.add),
                        waits=[(sse, sigch["exp"]), (vve, vve.n)], inc=vve)
                    k.emit("vector", lambda v, oc=oc, par=par: v.reduce_sum(
                        sbarP[:, par * NCK + oc: par * NCK + oc + 1],
                        scrA1[:, par * CH:(par + 1) * CH], AX.X),
                        waits=[(vve, vve.n)], inc=vve)
                    sigch["v"] = vve.n
        a1_v = vve.n
        a1_t = tmm.n
        endphase()

        lsT = ov.tensor("lsT", [128, NCK * OWN], BF16, 2, 6)
        khh = ov.tensor("khh", [128, 2 * S], BF16, 2, 2)
        vpp = ov.tensor("vpp", [128, 16 * 256], BF16, 2, 2)
        kw = ov.tensor("kw", [128, NCK * 256], BF16, 2, 2)
        vw = ov.tensor("vw", [128, NCK * 256], BF16, 2, 2)
        expm = ov.tensor("expm", [128, 2 * 1024], BF16, 2, 2)
        trib = ov.tensor("trib", [128, 4 * CH], BF16, 2, 2)
        denr = ov.tensor("denr", [1, CH], F32, 2, 2)
        denb = ov.tensor("denb", [128, CH], F32, 2, 2)

        k.emit("sync", lambda sy: sy.dma_start(trib[:], tri.ap()),
               waits=k.ch(sld), inc=sld, inc_k=16)
        tri_ld = sld.n
        k.chend(sld)

        k.emit("scalar", lambda s: s.activation(rr[0:1, 15:16], gb[0:1, 0:1],
                                                AF.Sigmoid),
               waits=[(sld, consts_ld)], inc=sse)
        gsig = sse.n
        wts = [(sse, gsig)] + bank_w(2)
        k.emit("tensor", lambda t: t.matmul(ps[2][:, 0:1], onerf[:, :],
                                            rr[0:1, 15:16], start=True,
                                            stop=True), waits=wts, inc=tmm)
        k.emit("vector", lambda v: v.tensor_copy(gb[:, 0:1], ps[2][:, 0:1]),
               waits=[(tmm, tmm.n)], inc=vve)
        bank_guard[2] = (vve, vve.n)
        k.drain("vector", inc=vve)
        k.emit("vector", lambda v: v.tensor_scalar(gb[:, 1:2], gb[:, 0:1],
                                                   -1.0, 1.0, ALU.mult,
                                                   ALU.add), inc=vve)
        k.drain("vector", inc=vve)
        gb_v = vve.n

        pairs_by_o = {0: [p for p in PAIRS if p[0] == 0],
                      1: [p for p in PAIRS if p[0] == 1]}
        kw_t = 0
        vw_t = 0
        att_t = a1_t
        sc_guard = [[], []]
        expm_rd = {}
        expp_rd = {}
        grp_guard = [[], []]
        den_bc_t = [0]
        finq = []

        def emit_group_fin(info):
            h, o, par, pvb, dnb, pv_bank, dn_bank, att_mm = info
            k.emit("vector", lambda v, dnb=dnb: v.reciprocal(
                denr[0:1, 0:CH], dnb[0:1, 0:CH]),
                waits=[(tmm, att_mm), (vve, gb_v),
                       (tmm, den_bc_t[0])], inc=vve)
            dn_guard_v = vve.n
            k.drain("vector", inc=vve)
            k.emit("vector", lambda v: v.tensor_single_scalar(
                denr[0:1, 0:CH], denr[0:1, 0:CH], gb[0:1, 1:2],
                ALU.mult), inc=vve)
            k.drain("vector", inc=vve)
            drv = vve.n
            wts = [(vve, drv)] + bank_w(6)
            k.emit("tensor", lambda t: t.matmul(
                ps[6][:, 0:CH], onerf[:, :], denr[0:1, 0:CH],
                start=True, stop=True), waits=wts, inc=tmm)
            den_bc_t[0] = tmm.n
            k.emit("vector", lambda v: v.tensor_copy(
                denb[:], ps[6][:, 0:CH]), waits=[(tmm, den_bc_t[0])],
                inc=vve)
            bank_guard[6] = (vve, vve.n)
            k.drain("vector", inc=vve)
            k.emit("vector", lambda v, h=h, o=o, pvb=pvb: v.tensor_mul(
                lsT[:, h * OWN + o * CH: h * OWN + (o + 1) * CH],
                pvb[:, 0:CH], denb[:]), inc=vve)
            grp_guard[par] = [(vve, vve.n)]
            bank_guard[pv_bank] = (vve, vve.n)
            bank_guard[dn_bank] = (vve, dn_guard_v)

        gidx = 0
        for hp in range(8):
            k.emit("gpsimd", lambda g, hp=hp: g.dma_start(
                kw[:].rearrange("p (c w) -> p c w", w=256),
                wkT.ap()[:, hp * 256:(hp + 1) * 256].rearrange(
                    "(c p) w -> p c w", p=128)),
                waits=[(tmm, kw_t)] + k.ch(s_kv), inc=s_kv, inc_k=16)
            kld = s_kv.n
            k.chend(s_kv)
            k.emit("gpsimd", lambda g, hp=hp: g.dma_start(
                vw[:].rearrange("p (c w) -> p c w", w=256),
                wvT.ap()[:, hp * 256:(hp + 1) * 256].rearrange(
                    "(c p) w -> p c w", p=128)),
                waits=[(tmm, vw_t)] + k.ch(s_kv), inc=s_kv, inc_k=16)
            vld = s_kv.n
            k.chend(s_kv)
            for hip in range(2):
                for st in range(4):
                    bank = (hip * 4 + st) % 2
                    wts = [(s_kv, kld), (gld, wq_ld)] + bank_w(bank)
                    for dc in range(NCK):
                        k.emit("tensor", lambda t, hip=hip, st=st, dc=dc, bank=bank: t.matmul(
                            ps[bank][:, 0:CH],
                            kw[:, dc * 256 + hip * 128: dc * 256 + hip * 128 + 128],
                            xTb[:, dc * S + st * CH: dc * S + (st + 1) * CH],
                            start=(dc == 0), stop=(dc == NCK - 1)),
                            waits=wts if dc == 0 else (),
                            inc=tmm if dc == NCK - 1 else None)
                    k.emit("scalar", lambda s, hip=hip, st=st, bank=bank: s.activation(
                        khh[:, hip * S + st * CH: hip * S + (st + 1) * CH],
                        ps[bank][:, 0:CH], AF.Copy),
                        waits=[(tmm, max(tmm.n, att_t)), (sse, sse.n)],
                        inc=sse)
                    bank_guard[bank] = (sse, sse.n)
            kw_t = tmm.n
            while finq:
                emit_group_fin(finq.pop(0))
            for ss in range(16):
                bank = 2 + ss % 2
                wts = [(s_kv, vld)] + bank_w(bank)
                for dc in range(NCK):
                    k.emit("tensor", lambda t, ss=ss, dc=dc, bank=bank: t.matmul(
                        ps[bank][:, 0:256],
                        xTb[:, dc * S + ss * 128: dc * S + (ss + 1) * 128],
                        vw[:, dc * 256:(dc + 1) * 256],
                        start=(dc == 0), stop=(dc == NCK - 1)),
                        waits=wts if dc == 0 else (),
                        inc=tmm if dc == NCK - 1 else None)
                k.emit("scalar", lambda s, ss=ss, bank=bank: s.activation(
                    vpp[:, ss * 256:(ss + 1) * 256], ps[bank][:, 0:256],
                    AF.Copy),
                    waits=[(tmm, max(tmm.n, att_t)), (sse, sse.n)], inc=sse)
                bank_guard[bank] = (sse, sse.n)
            kvgen_v = (sse, sse.n)
            vw_t = tmm.n

            for hip in range(2):
                h = hp * 2 + hip
                for o in range(2):
                    plist = pairs_by_o[o]
                    blocks = [(pc, kb, kind, bi)
                              for (po, pc, kind, bi) in plist
                              for kb in range(4)]
                    bpairs = [tuple(blocks[2 * j:2 * j + 2])
                              for j in range(len(blocks) // 2)]
                    nblk = len(blocks)
                    par = gidx % 2
                    pvb = ps[4] if par == 0 else ps[2]
                    dnb = ps[5] if par == 0 else ps[3]
                    pv_bank = 4 if par == 0 else 2
                    dn_bank = 5 if par == 0 else 3

                    pending = None

                    def emit_denpv(pend):
                        (j, bp, rdy) = pend
                        slot = j % 2
                        for half, (pc, kb, kind, bi) in enumerate(bp):
                            bi0 = 2 * j + half
                            first = (bi0 == 0)
                            last = (bi0 == nblk - 1)
                            dwts = [rdy]
                            if first:
                                dwts += (grp_guard[par] + bank_w(pv_bank)
                                         + bank_w(dn_bank))
                            k.emit("tensor", lambda t, slot=slot, half=half, first=first, last=last, dnb=dnb: t.matmul(
                                dnb[0:1, 0:CH], onecb[:, 0:1],
                                expm[:, slot * 1024 + half * CH:
                                     slot * 1024 + (half + 1) * CH],
                                start=first, stop=last), waits=dwts, inc=tmm)
                            k.emit("tensor", lambda t, hip=hip, pc=pc, kb=kb, slot=slot, half=half, first=first, last=last, pvb=pvb: t.matmul(
                                pvb[:, 0:CH],
                                vpp[:, (pc * 4 + kb) * 256 + hip * 128:
                                    (pc * 4 + kb) * 256 + hip * 128 + 128],
                                expm[:, slot * 1024 + half * CH:
                                     slot * 1024 + (half + 1) * CH],
                                start=first, stop=last), inc=tmm)
                        expm_rd[slot] = tmm.n

                    for j, bp in enumerate(bpairs):
                        scb = j % 2
                        scbanks = (0, 1) if scb == 0 else (6, 7)
                        for half, (pc, kb, kind, bi) in enumerate(bp):
                            wts = [kvgen_v, (sld, tri_ld)]
                            if half == 0:
                                wts += (sc_guard[scb] + bank_w(scbanks[0])
                                        + bank_w(scbanks[1]))
                            diag = bp[0][2] == "diag"
                            k.emit("tensor", lambda t, hip=hip, h=h, o=o, pc=pc, kb=kb, scb=scb, half=half, diag=diag: t.matmul(
                                psc[scb][:, half * CH:(half + 1) * CH],
                                khh[:, hip * S + pc * CH + kb * 128:
                                    hip * S + pc * CH + kb * 128 + 128],
                                qT[:, h * OWN + o * CH: h * OWN + (o + 1) * CH],
                                start=True, stop=not diag),
                                waits=wts, inc=None if diag else tmm)
                        if bp[0][2] == "diag":
                            kb0 = bp[0][1]
                            for half in range(2):
                                k.emit("tensor", lambda t, scb=scb, kb0=kb0, half=half: t.matmul(
                                    psc[scb][:, half * CH:(half + 1) * CH],
                                    idb[:],
                                    trib[:, (kb0 + half) * CH:
                                         (kb0 + half + 1) * CH],
                                    start=False, stop=True),
                                    waits=[(sld, tri_ld)] if half == 0 else (),
                                    inc=tmm if half == 1 else None)
                        smm = tmm.n
                        slot = j % 2
                        kind = bp[0][2]
                        if kind == "bias":
                            bi = bp[0][3]
                            k.emit("scalar", lambda s, scb=scb, slot=slot, bi=bi: s.activation(
                                expm[:, slot * 1024:(slot + 1) * 1024],
                                psc[scb][:, 0:1024], AF.Exp,
                                bias=biasb[:, bi:bi + 1], scale=ATT_SCALE),
                                waits=[(tmm, max(smm, expm_rd.get(slot, 0)))],
                                inc=sse)
                            rdy = (sse, sse.n)
                            sc_guard[scb] = [(sse, sse.n)]
                            bank_guard[scbanks[0]] = (sse, sse.n)
                            bank_guard[scbanks[1]] = (sse, sse.n)
                        else:
                            k.emit("scalar", lambda s, scb=scb, slot=slot: s.activation(
                                expm[:, slot * 1024:(slot + 1) * 1024],
                                psc[scb][:, 0:1024], AF.Exp, scale=ATT_SCALE),
                                waits=[(tmm, max(smm, expm_rd.get(slot, 0)))],
                                inc=sse)
                            rdy = (sse, sse.n)
                            sc_guard[scb] = [(sse, sse.n)]
                            bank_guard[scbanks[0]] = (sse, sse.n)
                            bank_guard[scbanks[1]] = (sse, sse.n)
                        if pending is not None:
                            emit_denpv(pending)
                        pending = (j, bp, rdy)
                    emit_denpv(pending)
                    att_mm = tmm.n
                    finq.append((h, o, par, pvb, dnb, pv_bank, dn_bank,
                                 att_mm))
                    if len(finq) > 1:
                        emit_group_fin(finq.pop(0))
                    att_t = tmm.n
                    gidx += 1
                    for _ in range(4):
                        if swq:
                            emit_swjob2(swq.pop(0))
        while finq:
            emit_group_fin(finq.pop(0))
        while swq:
            emit_swjob2(swq.pop(0))
        a2_t = tmm.n
        a2_v = vve.n
        endphase()

        sigT = ov.tensor("sigT", [128, NCK * OWN], BF16, 3, 4)
        scrB1 = ov.tensor("scrB1", [128, 2048], F32, 3, 4)
        scrB2 = ov.tensor("scrB2", [128, 2048], F32, 3, 4)

        ch = {"e": {}, "v": {}}
        for oc in range(NCK):
            off = (oc % 2) * 1024
            k.emit("scalar", lambda s, oc=oc, off=off: s.activation(
                scrB1[:, off:off + 1024], qT[:, oc * OWN:(oc + 1) * OWN],
                AF.Relu, scale=-1.0),
                waits=[(vve, ch["v"].get(oc - 2, a2_v))], inc=sse)
            k.emit("scalar", lambda s, off=off: s.activation(
                scrB2[:, off:off + 1024], scrB1[:, off:off + 1024],
                AF.Exp, scale=-1.0), waits=[(sse, sse.n)], inc=sse)
            ch["e"][oc] = sse.n
            k.emit("vector", lambda v, oc=oc, off=off: v.scalar_tensor_tensor(
                sigT[:, oc * OWN:(oc + 1) * OWN],
                qT[:, oc * OWN:(oc + 1) * OWN], 0.0, scrB2[:, off:off + 1024],
                ALU.max, ALU.add), waits=[(sse, ch["e"][oc]), (vve, vve.n)],
                inc=vve)
            k.emit("vector", lambda v, oc=oc: v.reduce_sum(
                sbarO[:, oc:oc + 1], sigT[:, oc * OWN:(oc + 1) * OWN], AX.X),
                waits=[(vve, vve.n)], inc=vve)
            ch["v"][oc] = vve.n
        sigT_v = vve.n
        k.emit("vector", lambda v: v.tensor_add(sbarF[:], sbarO[:],
                                                sbarP[:, 0:NCK]),
               waits=[(vve, vve.n)], inc=vve)
        k.emit("vector", lambda v: v.tensor_add(sbarF[:], sbarF[:],
                                                sbarP[:, NCK:2 * NCK]),
               waits=[(vve, vve.n)], inc=vve)
        k.drain("vector", inc=vve)
        sbar_v = vve.n

        wts = [(vve, sbar_v), (sld, consts_ld)] + bank_w(0)
        for oc in range(NCK):
            k.emit("tensor", lambda t, oc=oc: t.matmul(
                ps[0][0:1, 0:M4], sbarF[:, oc:oc + 1],
                mntf[:, oc * M4:(oc + 1) * M4],
                start=(oc == 0), stop=(oc == NCK - 1)),
                waits=wts if oc == 0 else (),
                inc=tmm if oc == NCK - 1 else None)
        rel_mm = tmm.n
        k.emit("vector", lambda v: v.tensor_single_scalar(
            rr[0:1, 0:M4], ps[0][0:1, 0:M4], 1.0 / S, ALU.mult),
            waits=[(tmm, rel_mm)], inc=vve)
        bank_guard[0] = (vve, vve.n)
        k.drain("vector", inc=vve)
        k.emit("vector", lambda v: v.reduce_max(rr[0:1, 4:5], rr[0:1, 0:M4],
                                                AX.X), inc=vve)
        k.drain("vector", inc=vve)
        k.emit("vector", lambda v: v.tensor_single_scalar(
            rr[0:1, 5:9], rr[0:1, 0:M4], rr[0:1, 4:5], ALU.subtract),
            inc=vve)
        k.drain("vector", inc=vve)
        vv0 = vve.n
        k.emit("scalar", lambda s: s.activation(rr[0:1, 9:13], rr[0:1, 5:9],
                                                AF.Exp), waits=[(vve, vv0)],
               inc=sse)
        k.emit("vector", lambda v: v.reduce_sum(rr[0:1, 13:14], rr[0:1, 9:13],
                                                AX.X), waits=[(sse, sse.n)],
               inc=vve)
        k.drain("vector", inc=vve)
        k.emit("vector", lambda v: v.reciprocal(rr[0:1, 14:15],
                                                rr[0:1, 13:14]), inc=vve)
        k.drain("vector", inc=vve)
        k.emit("vector", lambda v: v.tensor_single_scalar(
            rr[0:1, 9:13], rr[0:1, 9:13], rr[0:1, 14:15], ALU.mult), inc=vve)
        k.drain("vector", inc=vve)
        wtil_v = vve.n
        wts = [(vve, wtil_v)] + bank_w(1)
        k.emit("tensor", lambda t: t.matmul(ps[1][:, 0:M4], onerf[:, :],
                                            rr[0:1, 9:13], start=True,
                                            stop=True), waits=wts, inc=tmm)
        k.emit("vector", lambda v: v.tensor_copy(wtilb[:], ps[1][:, 0:M4]),
               waits=[(tmm, tmm.n)], inc=vve)
        bank_guard[1] = (vve, vve.n)
        k.drain("vector", inc=vve)
        wtilb_v = vve.n

        for ss in range(8):
            bank = 2 + ss % 2
            wts = [(vve, sigT_v), (gld, gld.n)] + bank_w(bank)
            for oc in range(NCK):
                k.emit("tensor", lambda t, ss=ss, oc=oc, bank=bank: t.matmul(
                    ps[bank][:, 0:M4],
                    sigT[:, oc * OWN + ss * 128: oc * OWN + (ss + 1) * 128],
                    mntb[:, oc * M4:(oc + 1) * M4],
                    start=(oc == 0), stop=(oc == NCK - 1)),
                    waits=wts if oc == 0 else (),
                    inc=tmm if oc == NCK - 1 else None)
            rmm = tmm.n
            k.emit("vector", lambda v, bank=bank: v.tensor_single_scalar(
                scrB1[:, 0:M4], ps[bank][:, 0:M4], ATT_EPS, ALU.max),
                waits=[(tmm, rmm), (vve, wtilb_v)], inc=vve)
            bank_guard[bank] = (vve, vve.n)
            k.drain("vector", inc=vve)
            k.emit("vector", lambda v: v.reciprocal(scrB2[:, 0:M4],
                                                    scrB1[:, 0:M4]), inc=vve)
            k.drain("vector", inc=vve)
            k.emit("vector", lambda v, ss=ss: v.tensor_mul(
                coefb[:, ss * M4:(ss + 1) * M4], scrB2[:, 0:M4], wtilb[:]),
                inc=vve)
            k.drain("vector", inc=vve)
        coef_v = vve.n

        if dbg:
            for nm, tile, w in (("d_qT", qT, (vve, a1_v)),
                                ("d_sbar", sbarF, (vve, sbar_v)),
                                ("d_rel", rr, (vve, wtil_v)),
                                ("d_coef", coefb, (vve, coef_v)),
                                ("d_lsT", lsT, (vve, a2_v)),
                                ("d_sigT", sigT, (vve, sigT_v))):
                k.emit("sync", lambda sy, nm=nm, tile=tile: sy.dma_start(
                    dbgt[nm].ap(), tile[:]), waits=[w] + k.ch(sdbg),
                    inc=sdbg, inc_k=16)
                k.chend(sdbg)
        dbg_gate0 = sdbg.n
        endphase()

        membuf = [ov.tensor(f"membuf{i}", [128, M4 * NCK * 256], BF16, 4, 4)
                  for i in range(2)]
        memtok = ov.tensor("memtok", [128, 8 * D], BF16, 4, 5)

        mem_t = [a2_t, a2_t]
        ev_v = 0
        ev_s = 0
        for et in range(8):
            buf = et % 2
            for m in range(M4):
                c0 = m * NCK * 256
                k.emit("gpsimd", lambda g, et=et, m=m, buf=buf, c0=c0: g.dma_start(
                    membuf[buf][:, c0:c0 + NCK * 256].rearrange(
                        "p (c w) -> p c w", w=256),
                    mem.ap()[m * D:(m + 1) * D,
                             et * 256:(et + 1) * 256].rearrange(
                        "(c p) w -> p c w", p=128)),
                    waits=([(tmm, mem_t[buf]), (sdbg, dbg_gate0)]
                           + k.ch(s_me[buf]) if m == 0 else ()),
                    inc=s_me[buf], inc_k=16)
            mld = s_me[buf].n
            k.chend(s_me[buf])
            for ss in range(8):
                grp = 4 * (ss % 2)
                wts0 = [(s_me[buf], mld), (vve, coef_v)]
                for m in range(M4):
                    bank = grp + m
                    wts = wts0 + bank_w(bank)
                    for dc in range(NCK):
                        k.emit("tensor", lambda t, ss=ss, m=m, dc=dc, bank=bank, buf=buf: t.matmul(
                            ps[bank][:, 0:256],
                            sigT[:, dc * OWN + ss * 128: dc * OWN + (ss + 1) * 128],
                            membuf[buf][:, (m * NCK + dc) * 256:
                                        (m * NCK + dc + 1) * 256],
                            start=(dc == 0), stop=(dc == NCK - 1)),
                            waits=wts if dc == 0 else (),
                            inc=tmm if dc == NCK - 1 else None)
                amm = tmm.n
                par = (et * 8 + ss) % 2
                off = 512 * par
                k.emit("scalar", lambda s, grp=grp, ss=ss, off=off: s.activation(
                    scrB1[:, off:off + 256], ps[grp + 0][:, 0:256], AF.Copy,
                    scale=coefb[:, ss * M4:ss * M4 + 1]),
                    waits=[(tmm, amm), (vve, ev_v)], inc=sse)
                k.emit("scalar", lambda s, grp=grp, ss=ss, off=off: s.activation(
                    scrB1[:, off + 256:off + 512], ps[grp + 2][:, 0:256],
                    AF.Copy, scale=coefb[:, ss * M4 + 2:ss * M4 + 3]),
                    inc=sse)
                sev = sse.n
                k.emit("vector", lambda v, grp=grp, ss=ss, off=off: v.scalar_tensor_tensor(
                    scrB2[:, off:off + 256], ps[grp + 1][:, 0:256],
                    coefb[:, ss * M4 + 1:ss * M4 + 2], scrB1[:, off:off + 256],
                    ALU.mult, ALU.add), waits=[(sse, sev)], inc=vve)
                k.emit("vector", lambda v, grp=grp, ss=ss, off=off: v.scalar_tensor_tensor(
                    scrB2[:, off + 256:off + 512], ps[grp + 3][:, 0:256],
                    coefb[:, ss * M4 + 3:ss * M4 + 4],
                    scrB1[:, off + 256:off + 512], ALU.mult, ALU.add),
                    inc=vve)
                k.drain("vector", inc=vve)
                k.emit("vector", lambda v, ss=ss, et=et, off=off: v.tensor_add(
                    memtok[:, ss * D + et * 256: ss * D + (et + 1) * 256],
                    scrB2[:, off:off + 256], scrB2[:, off + 256:off + 512]),
                    inc=vve)
                for m in range(M4):
                    bank_guard[grp + m] = (vve, vve.n)
                ev_v = vve.n
                ev_s = sev
            mem_t[buf] = tmm.n
        k.drain("vector", inc=vve)
        k.drain("scalar", inc=sse)
        memtok_v = vve.n
        swp_v, swp_s = vve.n, sse.n

        k.emit("vector", lambda v: v.reduce_sum(scrB1[:, 0:1],
                                                swpart2[:, 0:64], AX.X),
               waits=[(vve, swp_v), (sse, swp_s)], inc=vve)
        k.drain("vector", inc=vve)
        k.emit("vector", lambda v: v.reduce_sum(scrB1[:, 1:2],
                                                swpart2[:, 64:128], AX.X),
               inc=vve)
        k.drain("vector", inc=vve)
        swred_v = vve.n
        wts = [(vve, swred_v)] + bank_w(0)
        k.emit("tensor", lambda t: t.matmul(ps[0][0:1, 0:1], scrB1[:, 0:1],
                                            onecf[:, 0:1], start=True,
                                            stop=True), waits=wts, inc=tmm)
        k.emit("tensor", lambda t: t.matmul(ps[0][0:1, 1:2], scrB1[:, 1:2],
                                            onecf[:, 0:1], start=True,
                                            stop=True), inc=tmm)
        swmm = tmm.n
        k.emit("vector", lambda v: v.tensor_copy(
            swr[0:1, 0:2], ps[0][0:1, 0:2]), waits=[(tmm, swmm)], inc=vve)
        bank_guard[0] = (vve, vve.n)
        k.drain("vector", inc=vve)
        k.emit("vector", lambda v: v.tensor_scalar(
            swr[0:1, 2:4], swr[0:1, 0:2], 1.0 / (D * DI), Q_EPS,
            ALU.mult, ALU.max), inc=vve)
        k.drain("vector", inc=vve)
        k.emit("vector", lambda v: v.tensor_single_scalar(
            swr[0:1, 4:6], swr[0:1, 2:4], 0.5, ALU.mult), inc=vve)
        k.drain("vector", inc=vve)
        swsc_v = vve.n
        wts = [(vve, swsc_v)] + bank_w(1)
        k.emit("tensor", lambda t: t.matmul(ps[1][:, 0:4], onerf[:, :],
                                            swr[0:1, 2:6], start=True,
                                            stop=True), waits=wts, inc=tmm)
        swbc = tmm.n
        for (dst, j) in ((swupb, 0), (swdnb, 1), (thrup, 2), (thrdn, 3)):
            k.emit("vector", lambda v, dst=dst, j=j: v.tensor_copy(
                dst[:], ps[1][:, j:j + 1]), waits=[(tmm, swbc)], inc=vve)
        bank_guard[1] = (vve, vve.n)
        k.drain("vector", inc=vve)
        sw_v = vve.n

        if dbg:
            k.emit("sync", lambda sy: sy.dma_start(
                dbgt["d_memtok"].ap(), memtok[:]),
                waits=[(vve, memtok_v)] + k.ch(sdbg), inc=sdbg, inc_k=16)
            k.chend(sdbg)
            k.emit("sync", lambda sy: sy.dma_start(
                dbgt["d_sw"].ap()[0:1, 0:6], swr[0:1, 0:6]),
                waits=[(vve, sw_v)] + k.ch(sdbg), inc=sdbg, inc_k=16)
            k.chend(sdbg)
        dbg_gate1 = sdbg.n
        endphase()

        attn_inT = lsT
        for oc in range(NCK):
            bank = oc % 2
            wts = [(vve, memtok_v), (sld, consts_ld),
                   (sdbg, dbg_gate1)] + bank_w(bank)
            for ss in range(8):
                k.emit("tensor", lambda t, ss=ss, oc=oc, bank=bank: t.transpose(
                    psb[bank][:, ss * 128:(ss + 1) * 128],
                    memtok[:, ss * D + oc * 128: ss * D + (oc + 1) * 128],
                    idb[:]), waits=wts if ss == 0 else (),
                    inc=tmm if ss == 7 else None)
            tpv = tmm.n
            k.emit("vector", lambda v, oc=oc, bank=bank: v.scalar_tensor_tensor(
                attn_inT[:, oc * OWN:(oc + 1) * OWN],
                psb[bank][:, 0:1024], gb[:, 0:1],
                lsT[:, oc * OWN:(oc + 1) * OWN],
                ALU.mult, ALU.add), waits=[(tmm, tpv), (vve, a2_v)],
                inc=vve)
            bank_guard[bank] = (vve, vve.n)
        ai_v = vve.n
        endphase()

        x2 = ov.tensor("x2", [128, 8 * D], BF16, 6, 11, at=139264)
        wos = [ov.tensor("wos0", [128, NCK * CH], BF16, 6, 6, at=0),
               ov.tensor("wos1", [128, NCK * CH], BF16, 6, 6, at=16384)]
        xos = [ov.tensor(f"xos{i}", [128, CH], F32, 6, 6) for i in range(2)]
        sqscr = ov.tensor("sqscr", [128, CH], F32, 6, 11)
        pg1 = ov.tensor("pg1", [128, 32], F32, 6, 8)
        qg1 = ov.tensor("qg1", [128, 32], F32, 6, 8)
        sc3 = ov.tensor("sc3", [128, 8], F32, 6, 11)
        ncol = ov.tensor("ncol", [128, 8], F32, 6, 11)
        sr8 = ov.tensor("sr8", [128, 48], F32, 0, 11)
        wfs = ov.tensor("wfs", [128, NCK * 256], F32, 6, 8, at=73728)
        wub = [ov.tensor("wub0", [128, NCK * 256], BF16, 6, 8, at=90112),
               ov.tensor("wub1", [128, NCK * 256], BF16, 6, 8, at=98304)]
        tnb = ov.tensor("tnb", [128, 2048], BF16, 6, 10, at=172032)
        xqT = ov.tensor("xqT", [128, 8 * D], BF16, 7, 8, at=106496)
        xqtile = ov.tensor("xqtile", [128, D], BF16, 7, 8)
        rtq = ov.tensor("rtq", [128, 9 * DI], BF16, 8, 10, at=0)
        wdb = [ov.tensor("wdb0", [128, NCI * 256], BF16, 9, 10, at=73728),
               ov.tensor("wdb1", [128, NCI * 256], BF16, 9, 10, at=106496)]
        wfd = [ov.tensor("wfd0", [128, 8 * 256], F32, 9, 10, at=122880),
               ov.tensor("wfd1", [128, 8 * 256], F32, 9, 10, at=131072)]
        sciT2 = ov.tensor("sciT2", [128, 2048], I32, 9, 10, at=90112)
        scq2 = ov.tensor("scq2", [128, 2048], F32, 9, 10, at=98304)
        adjb2 = ov.tensor("adjb2", [128, 2048], BF16, 9, 10,
                          at=ov.addr["xqtile"])
        obuf = [ov.tensor("obuf0", [128, D], F32, 10, 11, at=90112),
                ov.tensor("obuf1", [128, D], F32, 10, 11, at=98304)]
        pmax = ov.tensor("pmax", [128, 32], F32, 8, 9)
        pgrid = ov.tensor("pgrid", [128, 64], F32, 9, 11)
        qgrid = ov.tensor("qgrid", [128, 64], F32, 9, 11)
        rqtile = ov.tensor("rqtile", [128, 2048], BF16, 9, 10)

        xob = [ov.tensor(f"xob{i}", [128, CH], BF16, 6, 6) for i in range(2)]
        tnbg = ov.tensor("tnbg", [128, 2048], BF16, 6, 6, at=65536)
        mng = ov.tensor("mng", [128, 32], F32, 6, 8)
        mxg = ov.tensor("mxg", [128, 32], F32, 6, 8)
        cq1 = ov.tensor("cq1", [128, 16], F32, 6, 8)

        wo_t = [0, 0]
        xo_v = [0, 0]
        xob_t = [0, 0]

        last_s = [sse.n]
        last_v = [vve.n]
        last_g = [gcc.n]

        def s_emit(fn, waits=()):
            k.emit("scalar", fn, waits=[(sse, last_s[0])] + list(waits),
                   inc=sse)
            last_s[0] = sse.n
            return sse.n

        def v_emit(fn, waits=()):
            k.emit("vector", fn, waits=[(vve, last_v[0])] + list(waits),
                   inc=vve)
            last_v[0] = vve.n
            return vve.n

        def g_emit(fn, waits=()):
            k.emit("gpsimd", fn, waits=[(gcc, last_g[0])] + list(waits),
                   inc=gcc)
            last_g[0] = gcc.n
            return gcc.n

        for ht in range(4):
            buf = ht % 2
            for dq in range(4):
                k.emit("gpsimd", lambda g, ht=ht, dq=dq, buf=buf: g.dma_start(
                    wos[buf][:, dq * 4 * CH:(dq + 1) * 4 * CH].rearrange(
                        "p (c w) -> p c w", w=CH),
                    woT.ap()[dq * 512:(dq + 1) * 512,
                             ht * CH:(ht + 1) * CH].rearrange(
                        "(c p) w -> p c w", p=128)),
                    waits=([(tmm, wo_t[buf]), (sdbg, dbg_gate1)]
                           + k.ch(s_wo[buf]) if dq == 0 else ()),
                    inc=s_wo[buf], inc_k=16)
            wld = s_wo[buf].n
            k.chend(s_wo[buf])
            for ss in range(8):
                xb = ss % 2
                k.emit("sync", lambda sy, ss=ss, ht=ht, xb=xb: sy.dma_start(
                    xos[xb][:, 0:CH],
                    x_own.ap()[ss * 128:(ss + 1) * 128,
                               ht * CH:(ht + 1) * CH]),
                    waits=[(vve, xo_v[xb])] + k.ch(s_xo[xb]),
                    inc=s_xo[xb], inc_k=16)
                xld = s_xo[xb].n
                k.chend(s_xo[xb])
                xcv = v_emit(lambda v, xb=xb: v.tensor_copy(
                    xob[xb][:, 0:CH], xos[xb][:, 0:CH]),
                    waits=[(s_xo[xb], xld), (tmm, xob_t[xb])])
                xo_v[xb] = xcv
                bank = 2 + ss % 2
                wts = [(s_wo[buf], wld), (vve, ai_v)] + bank_w(bank)
                for dc in range(NCK):
                    k.emit("tensor", lambda t, ss=ss, dc=dc, bank=bank, buf=buf: t.matmul(
                        ps[bank][:, 0:CH],
                        attn_inT[:, dc * OWN + ss * 128: dc * OWN + (ss + 1) * 128],
                        wos[buf][:, dc * CH:(dc + 1) * CH],
                        start=(dc == 0), stop=False),
                        waits=wts if dc == 0 else ())
                k.emit("tensor", lambda t, bank=bank, xb=xb: t.matmul(
                    ps[bank][:, 0:CH], idb[:], xob[xb][:, 0:CH],
                    start=False, stop=True),
                    waits=[(vve, xcv)], inc=tmm)
                womm = tmm.n
                xob_t[xb] = womm
                ev = s_emit(lambda s, ss=ss, ht=ht, bank=bank: s.activation(
                    x2[:, ss * D + ht * CH: ss * D + (ht + 1) * CH],
                    ps[bank][:, 0:CH], AF.Copy,
                    accum_out=pg1[:, ss * 4 + ht: ss * 4 + ht + 1]),
                    waits=[(tmm, womm)])
                bank_guard[bank] = (sse, ev)
                s_emit(lambda s, ss=ss, ht=ht: s.activation(
                    sqscr[:, 0:CH],
                    x2[:, ss * D + ht * CH: ss * D + (ht + 1) * CH], AF.Square,
                    accum_out=qg1[:, ss * 4 + ht: ss * 4 + ht + 1]))
                v_emit(lambda v, ss=ss, ht=ht: v.tensor_reduce(
                    mng[:, ss * 4 + ht: ss * 4 + ht + 1],
                    x2[:, ss * D + ht * CH: ss * D + (ht + 1) * CH], AX.X,
                    ALU.min), waits=[(sse, ev)])
                v_emit(lambda v, ss=ss, ht=ht: v.tensor_reduce(
                    mxg[:, ss * 4 + ht: ss * 4 + ht + 1],
                    x2[:, ss * D + ht * CH: ss * D + (ht + 1) * CH], AX.X,
                    ALU.max))
                if ht == 3:
                    v_emit(lambda v, ss=ss: v.reduce_sum(
                        sr8[:, ss:ss + 1], pg1[:, ss * 4:(ss + 1) * 4],
                        AX.X), waits=[(sse, last_s[0])])
                    v_emit(lambda v, ss=ss: v.reduce_sum(
                        sr8[:, 24 + ss:25 + ss], qg1[:, ss * 4:(ss + 1) * 4],
                        AX.X))
                    v_emit(lambda v, ss=ss: v.tensor_reduce(
                        ncol[:, ss:ss + 1], mng[:, ss * 4:(ss + 1) * 4],
                        AX.X, ALU.min))
                    v_emit(lambda v, ss=ss: v.tensor_reduce(
                        sc3[:, ss:ss + 1], mxg[:, ss * 4:(ss + 1) * 4],
                        AX.X, ALU.max))
            wo_t[buf] = tmm.n
        b3_last = tmm.n
        b3_sse = last_s[0]

        up_ld = {}
        tern_half = {}
        up_t = [0, 0]
        tern_v = {}

        def emit_wfs_dma(it):
            lds = []
            for dh in range(2):
                sem = s_up if dh == 0 else s_sw[0]
                qeng = "gpsimd" if dh == 0 else "sync"
                w0 = [tern_half[(it - 1, dh)]] if it >= 1 else []
                k.emit(qeng, lambda g, it=it, dh=dh: g.dma_start(
                    wfs[:, dh * 2048:(dh + 1) * 2048].rearrange(
                        "p (c w) -> p c w", w=256),
                    wupT.ap()[dh * 1024:(dh + 1) * 1024,
                              it * 256:(it + 1) * 256].rearrange(
                        "(c p) w -> p c w", p=128)),
                    waits=w0 + k.ch(sem), inc=sem, inc_k=16)
                lds.append((sem, sem.n))
                k.chend(sem)
            up_ld[it] = lds

        def emit_tern_up(it, eng="vector", extra=()):
            buf = it % 2
            em = v_emit if eng == "vector" else g_emit
            sem = vve if eng == "vector" else gcc
            tb = tnb if eng == "vector" else tnbg
            for dh in range(2):
                em(lambda v, dh=dh, tb=tb: v.tensor_scalar(
                    tb[:, 0:2048], wfs[:, dh * 2048:(dh + 1) * 2048], -1.0,
                    thrup[:, 0:1], ALU.mult, ALU.is_ge),
                    waits=[up_ld[it][dh], (tmm, up_t[buf]),
                           (vve, sw_v)] + list(extra))
                tv = em(lambda v, dh=dh, buf=buf, tb=tb: v.scalar_tensor_tensor(
                    wub[buf][:, dh * 2048:(dh + 1) * 2048],
                    wfs[:, dh * 2048:(dh + 1) * 2048], thrup[:, 0:1],
                    tb[:, 0:2048], ALU.is_ge, ALU.subtract))
                tern_half[(it, dh)] = (sem, tv)
            tern_v[it] = (sem, tv)

        emit_wfs_dma(0)
        emit_tern_up(0)
        emit_wfs_dma(1)
        emit_tern_up(1, extra=[(tmm, b3_last)])
        emit_wfs_dma(2)

        v_emit(lambda v: v.tensor_single_scalar(
            sr8[:, 8:16], sr8[:, 0:8], 1.0 / D, ALU.mult))
        v_emit(lambda v: v.tensor_mul(sr8[:, 16:24], sr8[:, 8:16],
                                      sr8[:, 8:16]))
        v_emit(lambda v: v.tensor_single_scalar(
            sr8[:, 16:24], sr8[:, 16:24], LN_EPS, ALU.subtract))
        v_emit(lambda v: v.scalar_tensor_tensor(
            sr8[:, 32:40], sr8[:, 24:32], 1.0 / D, sr8[:, 16:24],
            ALU.mult, ALU.subtract))
        sqv = s_emit(lambda s: s.activation(
            sr8[:, 40:48], sr8[:, 32:40], AF.Sqrt),
            waits=[(vve, last_v[0])])
        v_emit(lambda v: v.reciprocal(sr8[:, 40:48], sr8[:, 40:48]),
               waits=[(sse, sqv)])
        v_emit(lambda v: v.tensor_sub(sc3[:, 0:8], sc3[:, 0:8],
                                      sr8[:, 8:16]))
        v_emit(lambda v: v.tensor_sub(ncol[:, 0:8], sr8[:, 8:16],
                                      ncol[:, 0:8]))
        v_emit(lambda v: v.tensor_tensor(sc3[:, 0:8], sc3[:, 0:8],
                                         ncol[:, 0:8], ALU.max))
        v_emit(lambda v: v.tensor_mul(sc3[:, 0:8], sc3[:, 0:8],
                                      sr8[:, 40:48]))
        v_emit(lambda v: v.tensor_single_scalar(
            sc3[:, 0:8], sc3[:, 0:8], Q_EPS, ALU.max))
        v_emit(lambda v: v.tensor_scalar(
            f1col[:, 0:8], sc3[:, 0:8], 1.0 / 127.0, swupb[:, 0:1],
            ALU.mult, ALU.mult), waits=[(vve, sw_v)])
        v_emit(lambda v: v.reciprocal(ncol[:, 0:8], sc3[:, 0:8]))
        v_emit(lambda v: v.tensor_single_scalar(
            ncol[:, 0:8], ncol[:, 0:8], 127.0, ALU.mult))
        v_emit(lambda v: v.tensor_mul(cq1[:, 0:8], ncol[:, 0:8],
                                      sr8[:, 40:48]))
        v_emit(lambda v: v.tensor_mul(cq1[:, 8:16], sr8[:, 8:16],
                                      cq1[:, 0:8]))
        fin_v = v_emit(lambda v: v.tensor_scalar(
            cq1[:, 8:16], cq1[:, 8:16], -1.0, 128.0, ALU.mult, ALU.add))

        xq_evac = [0] * 8
        scq_s = [0] * 8
        c2mm = {}

        def emit_c2_mm(it, ss):
            bank = 2 + ss % 2
            buf = it % 2
            wts = [tern_v[it], (vve, xq_evac[ss])] + bank_w(bank)
            for dc in range(NCK):
                k.emit("tensor", lambda t, ss=ss, dc=dc, it=it, bank=bank, buf=buf: t.matmul(
                    ps[bank][:, 0:256],
                    xqT[:, ss * D + dc * 128: ss * D + dc * 128 + 128],
                    wub[buf][:, dc * 256:(dc + 1) * 256],
                    start=(dc == 0), stop=(dc == NCK - 1)),
                    waits=wts if dc == 0 else (),
                    inc=tmm if dc == NCK - 1 else None)
            umm = tmm.n
            ev = s_emit(lambda s, ss=ss, it=it, bank=bank: s.activation(
                rtq[:, ss * DI + it * 256: ss * DI + (it + 1) * 256],
                ps[bank][:, 0:256], AF.Relu, scale=f1col[:, ss:ss + 1]),
                waits=[(tmm, umm), (vve, fin_v)])
            bank_guard[bank] = (sse, ev)
            return ev

        evac_mark = {}
        tp_hist = {}
        xqv_hist = {}
        for ss in range(8):
            for h2 in range(2):
                step = ss * 2 + h2
                off = h2 * 1024
                xoff = ss * D + h2 * 1024
                scv = None
                s_emit(lambda s, off=off, xoff=xoff, ss=ss: s.activation(
                    sciT[:, off:off + 1024], x2[:, xoff:xoff + 1024],
                    AF.Relu, bias=cq1[:, 8 + ss:9 + ss],
                    scale=cq1[:, ss:ss + 1]),
                    waits=[(vve, fin_v), (vve, xqv_hist.get(step - 2, 0))])
                scv = s_emit(lambda s, off=off, xoff=xoff, ss=ss: s.activation(
                    scq[:, off:off + 1024], x2[:, xoff:xoff + 1024],
                    AF.Relu, bias=cq1[:, 8 + ss:9 + ss],
                    scale=cq1[:, ss:ss + 1]))
                if h2:
                    scq_s[ss] = scv
                v_emit(lambda v, off=off: v.scalar_tensor_tensor(
                    adjb[:, off:off + 1024], scq[:, off:off + 1024], -0.5,
                    sciT[:, off:off + 1024], ALU.add, ALU.is_gt),
                    waits=[(sse, scv)])
                xqv = v_emit(lambda v, off=off: v.scalar_tensor_tensor(
                    xqtile[:, off:off + 1024], sciT[:, off:off + 1024],
                    -128.0, adjb[:, off:off + 1024], ALU.add, ALU.add),
                    waits=[(tmm, tp_hist.get(step - 2, b3_last))])
                xqv_hist[step] = xqv
                bank = step % 2
                wts = [(vve, xqv)] + bank_w(bank)
                for j in range(8):
                    dc = h2 * 8 + j
                    k.emit("tensor", lambda t, dc=dc, j=j, bank=bank: t.transpose(
                        psb[bank][:, j * 128:(j + 1) * 128],
                        xqtile[:, dc * 128:(dc + 1) * 128], idb[:]),
                        waits=wts if j == 0 else (), inc=tmm if j == 7 else None)
                tpv = tmm.n
                tp_hist[step] = tpv
                ev = v_emit(lambda v, ss=ss, h2=h2, bank=bank: v.tensor_copy(
                    xqT[:, ss * D + h2 * 1024: ss * D + (h2 + 1) * 1024],
                    psb[bank][:, 0:1024]), waits=[(tmm, tpv)])
                bank_guard[bank] = (vve, ev)
            xq_evac[ss] = last_v[0]
            emit_c2_mm(0, ss)
            emit_c2_mm(1, ss)

        up_t[0] = tmm.n
        up_t[1] = tmm.n
        nrm_v = [0] * 8
        for it in range(2, NCK):
            emit_tern_up(it)
            if it + 1 < NCK:
                emit_wfs_dma(it + 1)
            if it - 2 < 8:
                ss = it - 2
                v_emit(lambda v, ss=ss: v.tensor_single_scalar(
                    x2[:, ss * D:(ss + 1) * D], x2[:, ss * D:(ss + 1) * D],
                    sr8[:, 8 + ss:9 + ss], ALU.subtract),
                    waits=[(sse, scq_s[ss])])
                nrm_v[ss] = v_emit(lambda v, ss=ss: v.tensor_single_scalar(
                    x2[:, ss * D:(ss + 1) * D], x2[:, ss * D:(ss + 1) * D],
                    sr8[:, 40 + ss:41 + ss], ALU.mult))
            if it % 4 == 0 and it >= 4:
                g = it // 4 - 1
                for ss in range(8):
                    v_emit(lambda v, ss=ss, g=g: v.reduce_max(
                        pmax[:, ss * 4 + g: ss * 4 + g + 1],
                        rtq[:, ss * DI + g * 1024: ss * DI + (g + 1) * 1024],
                        AX.X, apply_absolute_value=True),
                        waits=[(sse, evac_mark[(4 * g + 3, ss)])])
            for ss in range(8):
                evac_mark[(it, ss)] = emit_c2_mm(it, ss)
            up_t[it % 2] = tmm.n
        for ss in range(8):
            v_emit(lambda v, ss=ss: v.reduce_max(
                pmax[:, ss * 4 + 3: ss * 4 + 4],
                rtq[:, ss * DI + 3 * 1024: ss * DI + 4 * 1024],
                AX.X, apply_absolute_value=True),
                waits=[(sse, evac_mark[(15, ss)])])
        c2_last = tmm.n
        x2_v = nrm_v[7]

        if dbg:
            k.emit("sync", lambda sy: sy.dma_start(
                dbgt["d_x2"].ap(), x2[:]), waits=[(vve, x2_v)] + k.ch(sdbg),
                inc=sdbg, inc_k=16)
            k.chend(sdbg)
        dbg_gate2 = sdbg.n


        if dbg:
            k.emit("sync", lambda sy: sy.dma_start(
                dbgt["d_rt"].ap(), rtq[:, 0:8 * DI]),
                waits=[(vve, last_v[0])] + k.ch(sdbg), inc=sdbg, inc_k=16)
            k.chend(sdbg)
        dbg_gate3 = sdbg.n

        for ss in range(8):
            v_emit(lambda v, ss=ss: v.reduce_max(
                sr8[:, ss:ss + 1], pmax[:, ss * 4:(ss + 1) * 4], AX.X))
        v_emit(lambda v: v.tensor_mul(sr8[:, 8:16], sr8[:, 0:8], sr8[:, 0:8]))
        v_emit(lambda v: v.tensor_single_scalar(
            sr8[:, 8:16], sr8[:, 8:16], Q_EPS, ALU.max))
        v_emit(lambda v: v.reciprocal(sr8[:, 16:24], sr8[:, 8:16]))
        v_emit(lambda v: v.tensor_single_scalar(
            sr8[:, 16:24], sr8[:, 16:24], 127.0, ALU.mult))
        fv2 = v_emit(lambda v: v.tensor_scalar(
            f2col[:, 0:8], sr8[:, 8:16], 1.0 / 127.0, swdnb[:, 0:1],
            ALU.mult, ALU.mult))
        sqv = s_emit(lambda s: s.activation(sc3[:, 0:8], sr8[:, 16:24],
                                            AF.Sqrt),
                     waits=[(vve, fv2)])

        dn_ld = {}
        dn_t = [0, 0]
        terndn_v = {}
        tern_prev = {}

        def emit_wfd_dma(idx):
            sb_ = idx % 2
            qeng = "gpsimd" if sb_ == 0 else "sync"
            w0 = [(vve, tern_prev[idx - 2])] if idx >= 2 else [(tmm, c2_last)]
            ht, qb = idx // 4, idx % 4
            k.emit(qeng, lambda g, ht=ht, qb=qb, sb_=sb_: g.dma_start(
                wfd[sb_][:].rearrange("p (c w) -> p c w", w=256),
                wdnT.ap()[qb * 1024:(qb + 1) * 1024,
                          ht * 256:(ht + 1) * 256].rearrange(
                    "(c p) w -> p c w", p=128)),
                waits=w0 + [(sdbg, dbg_gate3)] + k.ch(s_dn[sb_]),
                inc=s_dn[sb_], inc_k=16)
            dn_ld[idx] = s_dn[sb_].n
            k.chend(s_dn[sb_])

        def emit_tern_dn_idx(idx):
            ht, qb = divmod(idx, 4)
            buf = ht % 2
            sb_ = idx % 2
            extra = [(tmm, c2_last)] if ht < 2 else []
            v_emit(lambda v, sb_=sb_: v.tensor_scalar(
                tnb[:, 0:2048], wfd[sb_][:, 0:2048], -1.0,
                thrdn[:, 0:1], ALU.mult, ALU.is_ge),
                waits=[(s_dn[sb_], dn_ld[idx]), (tmm, dn_t[buf])] + extra)
            tv = v_emit(lambda v, qb=qb, sb_=sb_, buf=buf: v.scalar_tensor_tensor(
                wdb[buf][:, qb * 2048:(qb + 1) * 2048],
                wfd[sb_][:, 0:2048], thrdn[:, 0:1],
                tnb[:, 0:2048], ALU.is_ge, ALU.subtract))
            tern_prev[idx] = tv
            if qb == 3:
                terndn_v[ht] = tv

        nd_dn = [0]
        nt_dn = [0]

        def adv_dn(tern_upto):
            while nt_dn[0] < tern_upto:
                while nd_dn[0] < min(nt_dn[0] + 2, 32):
                    emit_wfd_dma(nd_dn[0])
                    nd_dn[0] += 1
                emit_tern_dn_idx(nt_dn[0])
                nt_dn[0] += 1

        rq_done = [0] * 8
        tp3_done = {}

        def qbase(ss):
            return ((ss + 8) % 9) * DI

        qset = [(sciT, scq, adjb), (sciT2, scq2, adjb2)]
        rq_hist = {}
        pend_cells = []
        cell_k = [0]

        def emit_c4_mm(ht, ss):
            kk = cell_k[0]
            cell_k[0] += 1
            bank = 4 + kk % 4
            buf = ht % 2
            wts = [(vve, terndn_v[ht]), (vve, rq_done[ss])] + bank_w(bank)
            for ic in range(NCI):
                k.emit("tensor", lambda t, ss=ss, ic=ic, bank=bank, buf=buf: t.matmul(
                    ps[bank][:, 0:256],
                    rtq[:, qbase(ss) + ic * 128: qbase(ss) + ic * 128 + 128],
                    wdb[buf][:, ic * 256:(ic + 1) * 256],
                    start=(ic == 0), stop=(ic == NCI - 1)),
                    waits=wts if ic == 0 else (),
                    inc=tmm if ic == NCI - 1 else None)
            pend_cells.append((ht, ss, bank, tmm.n))

        def flush_cells(keep=0):
            while len(pend_cells) > keep:
                ht, ss, bank, dmm = pend_cells.pop(0)
                yv = v_emit(lambda v, ss=ss, ht=ht, bank=bank: v.scalar_tensor_tensor(
                    x2[:, ss * D + ht * 256: ss * D + (ht + 1) * 256],
                    ps[bank][:, 0:256], f2col[:, ss:ss + 1],
                    x2[:, ss * D + ht * 256: ss * D + (ht + 1) * 256],
                    ALU.mult, ALU.add), waits=[(tmm, dmm)])
                bank_guard[bank] = (vve, yv)
                s_emit(lambda s, ss=ss, ht=ht: s.activation(
                    sqscr[:, 0:256],
                    x2[:, ss * D + ht * 256: ss * D + (ht + 1) * 256],
                    AF.Copy,
                    accum_out=pgrid[:, ss * 8 + ht: ss * 8 + ht + 1]),
                    waits=[(vve, yv)])
                s_emit(lambda s, ss=ss, ht=ht: s.activation(
                    sqscr[:, 0:256],
                    x2[:, ss * D + ht * 256: ss * D + (ht + 1) * 256],
                    AF.Square,
                    accum_out=qgrid[:, ss * 8 + ht: ss * 8 + ht + 1]))

        for ss in range(8):
            for qc in range(4):
                step = ss * 4 + qc
                sci_, scq_, adj_ = qset[(step % 4) // 2]
                off = (step % 2) * 1024
                o0 = ss * DI + qc * 1024
                wts_ci = [(sdbg, dbg_gate3),
                          (vve, rq_hist[step - 4] if step >= 4
                           else last_v[0])]
                s_emit(lambda s, o0=o0, off=off, ss=ss, sci_=sci_: s.activation(
                    sci_[:, off:off + 1024], rtq[:, o0:o0 + 1024], AF.Square,
                    scale=sc3[:, ss:ss + 1]), waits=wts_ci)
                sqv2 = s_emit(lambda s, o0=o0, off=off, ss=ss, scq_=scq_: s.activation(
                    scq_[:, off:off + 1024], rtq[:, o0:o0 + 1024], AF.Square,
                    scale=sc3[:, ss:ss + 1]))
                v_emit(lambda v, off=off, sci_=sci_, scq_=scq_, adj_=adj_: v.scalar_tensor_tensor(
                    adj_[:, off:off + 1024], scq_[:, off:off + 1024], -0.5,
                    sci_[:, off:off + 1024], ALU.add, ALU.is_gt),
                    waits=[(sse, sqv2)])
                rqv = v_emit(lambda v, off=off, sci_=sci_, adj_=adj_: v.scalar_tensor_tensor(
                    rqtile[:, off:off + 1024], sci_[:, off:off + 1024],
                    0.0, adj_[:, off:off + 1024], ALU.add, ALU.add),
                    waits=[(tmm, tp3_done.get(step - 2, b3_last))])
                rq_hist[step] = rqv
                bank = step % 2
                wts = [(vve, rqv)] + bank_w(bank)
                for j in range(8):
                    k.emit("tensor", lambda t, off=off, j=j, bank=bank: t.transpose(
                        psb[bank][:, j * 128:(j + 1) * 128],
                        rqtile[:, off + j * 128: off + (j + 1) * 128],
                        idb[:]),
                        waits=wts if j == 0 else (),
                        inc=tmm if j == 7 else None)
                tp3_done[step] = tmm.n
                ev = v_emit(lambda v, ss=ss, qc=qc, bank=bank: v.tensor_copy(
                    rtq[:, qbase(ss) + qc * 1024: qbase(ss) + (qc + 1) * 1024],
                    psb[bank][:, 0:1024]), waits=[(tmm, tp3_done[step])])
                bank_guard[bank] = (vve, ev)
            rq_done[ss] = last_v[0]
            flush_cells()
            if ss < 2:
                adv_dn((ss + 1) * 4)
            emit_c4_mm(0, ss)
            if ss >= 1:
                emit_c4_mm(1, ss - 1)
        emit_c4_mm(1, 7)
        flush_cells()
        dn_t[0] = tmm.n
        dn_t[1] = tmm.n

        adv_dn(12)
        for ht in range(2, 8):
            for ss in range(8):
                emit_c4_mm(ht, ss)
                flush_cells(keep=1)
                if ss == 2 and ht < 7:
                    adv_dn((ht + 2) * 4)
            dn_t[ht % 2] = tmm.n
        flush_cells()
        y2_v = last_v[0]
        y2_s = last_s[0]

        if dbg:
            k.emit("sync", lambda sy: sy.dma_start(
                dbgt["d_y2"].ap(), x2[:]), waits=[(vve, y2_v), (sse, y2_s)]
                + k.ch(sdbg), inc=sdbg, inc_k=16)
            k.chend(sdbg)

        ob_st = [0, 0]
        for ss in range(8):
            obp = ss % 2
            v_emit(lambda v, ss=ss: v.reduce_sum(
                sr8[:, ss:ss + 1], pgrid[:, ss * 8:(ss + 1) * 8], AX.X),
                waits=[(sse, y2_s)])
            v_emit(lambda v, ss=ss: v.reduce_sum(
                sr8[:, 24 + ss:25 + ss], qgrid[:, ss * 8:(ss + 1) * 8], AX.X))
            v_emit(lambda v, ss=ss: v.tensor_single_scalar(
                sr8[:, 8 + ss:9 + ss], sr8[:, ss:ss + 1], 1.0 / D, ALU.mult))
            v_emit(lambda v, ss=ss: v.tensor_mul(
                sr8[:, 16 + ss:17 + ss], sr8[:, 8 + ss:9 + ss],
                sr8[:, 8 + ss:9 + ss]))
            v_emit(lambda v, ss=ss: v.tensor_single_scalar(
                sr8[:, 32 + ss:33 + ss], sr8[:, 24 + ss:25 + ss], 1.0 / D,
                ALU.mult))
            v_emit(lambda v, ss=ss: v.scalar_tensor_tensor(
                sr8[:, 32 + ss:33 + ss], sr8[:, 32 + ss:33 + ss], LN_EPS,
                sr8[:, 16 + ss:17 + ss], ALU.add, ALU.subtract))
            sv = s_emit(lambda s, ss=ss: s.activation(
                sr8[:, 40 + ss:41 + ss], sr8[:, 32 + ss:33 + ss], AF.Sqrt),
                waits=[(vve, last_v[0])])
            v_emit(lambda v, ss=ss: v.reciprocal(
                sr8[:, 40 + ss:41 + ss], sr8[:, 40 + ss:41 + ss]),
                waits=[(sse, sv)])
            v_emit(lambda v, ss=ss, obp=obp: v.tensor_single_scalar(
                obuf[obp][:], x2[:, ss * D:(ss + 1) * D],
                sr8[:, 8 + ss:9 + ss], ALU.subtract),
                waits=[(sst, ob_st[obp])])
            outv = v_emit(lambda v, ss=ss, obp=obp: v.tensor_single_scalar(
                obuf[obp][:], obuf[obp][:], sr8[:, 40 + ss:41 + ss],
                ALU.mult))
            k.emit("scalar", lambda s, ss=ss, obp=obp: s.dma_start(
                out.ap()[ss * 128:(ss + 1) * 128, :], obuf[obp][:]),
                waits=[(vve, outv)] + k.ch(sst), inc=sst, inc_k=16)
            ob_st[obp] = sst.n
            k.chend(sst)

        fin = sst.n
        k.emit("sync", lambda sy: sy.nop(),
               waits=[(sst, fin), (sdbg, sdbg.n)])

        with nc.Block() as blk:
            for eng in K.ENGINES:
                stream = k.ops[eng]
                if not stream:
                    continue

                def mk(stream=stream):
                    def _(e):
                        for f in stream:
                            f(e)
                    return _
                getattr(blk, eng)(mk())
    return nc


def _core_inputs(x_b, core):
    h = core % 2
    order = LOCAL_ORDER[h]
    perm = np.concatenate([np.arange(c * CH, (c + 1) * CH) for c in order])
    xTl = np.ascontiguousarray(x_b.T[:, perm])
    own = order[:2]
    x_own_ = np.ascontiguousarray(
        np.concatenate([x_b[c * CH:(c + 1) * CH] for c in own], axis=0))
    bias = np.zeros((128, 4), np.float32)
    for (po, pc, kind, bi) in PAIRS:
        if kind != "bias":
            continue
        q_g, k_g = order[po], order[pc]
        bias[:, bi] = 0.0 if k_g < q_g else -1e9
    return xTl, x_own_, bias, own


_CONSTS = None


def _consts():
    global _CONSTS
    if _CONSTS is None:
        tri = np.zeros((128, 4 * CH), BF)
        p = np.arange(128)[:, None]
        q = np.arange(CH)[None, :]
        for kb in range(4):
            tri[:, kb * CH:(kb + 1) * CH] = np.where(
                (kb * 128 + p) <= q, 0.0, -1e9).astype(BF)
        _CONSTS = {
            "tri": tri,
            "identb": np.eye(128, dtype=BF),
            "onescb": np.ones((128, 1), BF),
            "onesrf": np.ones((1, 128), np.float32),
            "onescf": np.ones((128, 1), np.float32),
        }
    return _CONSTS


def kernel(**inputs):
    cst = _consts()
    f32c = lambda a: np.ascontiguousarray(np.asarray(a, dtype=np.float32))
    wqTn = f32c(np.asarray(inputs["L0_attn_w_q"]).T)
    wkTn = f32c(np.asarray(inputs["L0_attn_w_k"]).T)
    wvTn = f32c(np.asarray(inputs["L0_attn_w_v"]).T)
    woTn = f32c(np.asarray(inputs["L0_attn_w_o"]).T)
    memn = f32c(np.asarray(inputs["L0_attn_memories"]).reshape(M4 * D, D))
    mnTn = f32c(np.asarray(inputs["L0_attn_memory_norms"]).T)
    wupTn = f32c(np.asarray(inputs["L0_ffn_w_up"]).T)
    wdnTn = f32c(np.asarray(inputs["L0_ffn_w_down"]).T)
    gaten = f32c(np.asarray(inputs["L0_attn_gate"]).reshape(1, 1))
    xall = np.asarray(inputs["L0_attn_x"], dtype=np.float32)

    nc = build_program()
    in_maps = []
    owns = []
    for core in range(8):
        xTl, x_own_, bias, own = _core_inputs(xall[core // 2], core)
        owns.append(own)
        in_maps.append({
            "xT": xTl, "x_own": x_own_, "wqT": wqTn, "wkT": wkTn,
            "wvT": wvTn, "woT": woTn, "mem": memn, "mnT": mnTn,
            "wupT": wupTn, "wdnT": wdnTn, "gate": gaten, "tri": cst["tri"],
            "biasc": bias, "identb": cst["identb"], "onescb": cst["onescb"],
            "onesrf": cst["onesrf"], "onescf": cst["onescf"],
        })
    res = run_bass_kernel_spmd(nc, in_maps, core_ids=list(range(8)))
    outp = np.empty((4, S, D), np.float32)
    for core in range(8):
        b = core // 2
        o = np.asarray(res.results[core]["out"])
        for j, c in enumerate(owns[core]):
            outp[b, c * CH:(c + 1) * CH] = o[j * CH:(j + 1) * CH]
    return outp


# version 20 (final)
# speedup vs baseline: 1.0389x; 1.0009x over previous
import numpy as np
import ml_dtypes
from contextlib import ExitStack

import concourse.bass as bass
import concourse.mybir as mybir
from concourse.bass_utils import run_bass_kernel_spmd

F32 = mybir.dt.float32
BF16 = mybir.dt.bfloat16
I32 = mybir.dt.int32
ALU = mybir.AluOpType
AX = mybir.AxisListType
AF = mybir.ActivationFunctionType
BF = ml_dtypes.bfloat16

D = 2048
S = 2048
M4 = 4
DI = 4096
NCK = 16
NCI = 32
OWN = 1024
CH = 512
ATT_SCALE = float(128) ** -0.5
LN_EPS = 1e-5
ATT_EPS = 1e-6
Q_EPS = 1e-5

LOCAL_ORDER = {0: [0, 3, 1, 2], 1: [1, 2, 0, 3]}
PAIRS = [(0, 0, "diag", None), (0, 2, "bias", 0), (1, 0, "bias", 1),
         (1, 1, "diag", None), (1, 2, "bias", 2), (1, 3, "bias", 3)]


class Sem:
    def __init__(self, nc, es, name):
        self.h = es.enter_context(nc.semaphore(name))
        self.n = 0


class K:
    ENGINES = ("sync", "gpsimd", "scalar", "vector", "tensor")

    def __init__(self):
        self.ops = {e: [] for e in self.ENGINES}
        self.waited = {e: {} for e in self.ENGINES}
        self.fence = []
        self.chain = {}

    def emit(self, eng, fn, waits=(), inc=None, inc_k=1):
        wl = []
        for wv in list(self.fence) + list(waits):
            if wv is None:
                continue
            s, v = wv
            if s is None or v is None or v <= 0:
                continue
            if v > self.waited[eng].get(id(s), 0):
                wl.append((s.h, v))
                self.waited[eng][id(s)] = v
        if inc is not None:
            inc.n += inc_k

        def g(e, fn=fn, wl=wl, inc=inc, inc_k=inc_k):
            for (h, v) in wl:
                e.wait_ge(h, v)
            r = fn(e)
            if inc is not None:
                r.then_inc(inc.h, inc_k)
        self.ops[eng].append(g)

    def drain(self, eng, inc=None):
        self.emit(eng, lambda e: e.drain(), inc=inc)

    def ch(self, sem):
        return [(sem, self.chain.get(id(sem), 0))]

    def chend(self, sem):
        self.chain[id(sem)] = sem.n


PH = {"A0": 0, "A1": 1, "A2": 2, "B0": 3, "B1": 4, "B2": 5, "B3": 6,
      "C1": 7, "C2": 8, "C3": 9, "C4": 10, "C5": 11}
class Overlay:

    def __init__(self, nc):
        self.nc = nc
        self.base = (int(nc.sbuf_base) + 31) // 32 * 32
        self.total = int(nc.sbuf_top) - self.base
        self.live = []
        self.peak = 0

    def tensor(self, name, shape, dtype, p0, p1, at=None):
        per = 1
        for s_ in shape[1:]:
            per *= s_
        per = (per * mybir.dt.size(dtype) + 31) // 32 * 32
        if at is not None:
            self.live.append((at, per, p1))
            if not hasattr(self, "addr"):
                self.addr = {}
            self.addr[name] = at
            return self.nc.alloc_sbuf_tensor_at(
                name, list(shape), dtype, offset=self.base + at)
        self.live = [t for t in self.live if t[2] >= p0]
        self.live.sort()
        gaps = []
        prev = 0
        for (st, sz, _e) in self.live:
            if st - prev >= per:
                gaps.append((st - prev, prev))
            prev = max(prev, st + sz)
        if self.total - prev >= per:
            gaps.append((self.total - prev, prev))
        assert gaps, (
            f"overlay overflow: {name} needs {per}B, live "
            f"{[(s, z, e) for (s, z, e) in self.live]}")
        if per < 16384:
            gsz, gst = max(gaps, key=lambda g: g[1])
            addr = gst + gsz - per
        else:
            addr = min(gaps)[1]
        self.live.append((addr, per, p1))
        self.peak = max(self.peak, addr + per)
        if not hasattr(self, "addr"):
            self.addr = {}
        self.addr[name] = addr
        return self.nc.alloc_sbuf_tensor_at(
            name, list(shape), dtype, offset=self.base + addr)


def build_program(dbg=False):
    nc = bass.Bass("TRN2", target_bir_lowering=False)
    dt = nc.dram_tensor
    xT = dt("xT", [D, S], F32, kind="ExternalInput")
    x_own = dt("x_own", [OWN, D], F32, kind="ExternalInput")
    wqT = dt("wqT", [D, D], F32, kind="ExternalInput")
    wkT = dt("wkT", [D, D], F32, kind="ExternalInput")
    wvT = dt("wvT", [D, D], F32, kind="ExternalInput")
    woT = dt("woT", [D, D], F32, kind="ExternalInput")
    mem = dt("mem", [M4 * D, D], F32, kind="ExternalInput")
    mnT = dt("mnT", [D, M4], F32, kind="ExternalInput")
    wupT = dt("wupT", [D, DI], F32, kind="ExternalInput")
    wdnT = dt("wdnT", [DI, D], F32, kind="ExternalInput")
    gate = dt("gate", [1, 1], F32, kind="ExternalInput")
    tri = dt("tri", [128, 4 * CH], BF16, kind="ExternalInput")
    biasc = dt("biasc", [128, 4], F32, kind="ExternalInput")
    identb = dt("identb", [128, 128], BF16, kind="ExternalInput")
    onescb = dt("onescb", [128, 1], BF16, kind="ExternalInput")
    onesrf = dt("onesrf", [1, 128], F32, kind="ExternalInput")
    onescf = dt("onescf", [128, 1], F32, kind="ExternalInput")
    out = dt("out", [OWN, D], F32, kind="ExternalOutput")
    dbgt = {}
    if dbg:
        for nm, shp, dty in (("d_qT", [128, NCK * OWN], BF16),
                             ("d_sigT", [128, NCK * OWN], BF16),
                             ("d_sbar", [128, NCK], F32),
                             ("d_rel", [1, 16], F32),
                             ("d_coef", [128, 8 * M4], F32),
                             ("d_lsT", [128, NCK * OWN], BF16),
                             ("d_memtok", [128, 8 * D], BF16),
                             ("d_x2", [128, 8 * D], BF16),
                             ("d_rt", [128, 8 * DI], BF16),
                             ("d_y2", [128, 8 * D], BF16),
                             ("d_sw", [1, 8], F32)):
            dbgt[nm] = dt(nm, shp, dty, kind="ExternalOutput")

    root = ExitStack()
    with root:
        k = K()
        sems = root.enter_context(ExitStack())

        def mksem(name):
            return Sem(nc, sems, name)

        gld = mksem("gld")
        sld = mksem("sld")
        sst = mksem("sst")
        tmm = mksem("tmm")
        vve = mksem("vve")
        sse = mksem("sse")
        s_kv = mksem("s_kv")
        s_me = [mksem(f"s_me{i}") for i in range(2)]
        s_m2 = [mksem(f"s_m2{i}") for i in range(2)]
        s_wo = [mksem(f"s_wo{i}") for i in range(2)]
        s_up = mksem("s_up")
        s_dn = [mksem(f"s_dn{i}") for i in range(2)]
        s_xo = [mksem(f"s_xo{i}") for i in range(2)]
        s_sw = [mksem(f"s_sw{i}") for i in range(2)]
        gcc = mksem("gcc")
        sdbg = mksem("sdbg")

        ov = Overlay(nc)

        def endphase():
            k.fence = [(tmm, tmm.n), (vve, vve.n), (sse, sse.n),
                       (sst, sst.n), (sdbg, sdbg.n)]

        P = lambda name, p, c, d_: ov.tensor(name, [p, c], d_, 0, 11)
        biasb = P("biasb", 128, 4, F32)
        idb = P("idb", 128, 128, BF16)
        onecb = P("onecb", 128, 1, BF16)
        onerf = P("onerf", 1, 128, F32)
        onecf = P("onecf", 128, 1, F32)
        sbarP = P("sbarP", 128, 2 * NCK, F32)
        sbarO = P("sbarO", 128, NCK, F32)
        sbarF = P("sbarF", 128, NCK, F32)
        gb = P("gb", 128, 2, F32)
        wtilb = P("wtilb", 128, M4, F32)
        coefb = P("coefb", 128, 8 * M4, F32)
        rr = P("rr", 1, 16, F32)
        swupb = P("swupb", 128, 1, F32)
        swdnb = P("swdnb", 128, 1, F32)
        thrup = P("thrup", 128, 1, F32)
        thrdn = P("thrdn", 128, 1, F32)
        swpart2 = P("swpart2", 128, 128, F32)
        srow = [P(f"srow{i}", 128, 1, F32) for i in range(8)]
        f1col = P("f1col", 128, 8, F32)
        f2col = P("f2col", 128, 8, F32)
        swr = P("swr", 1, 8, F32)
        scq = ov.tensor("scq", [128, 2048], F32, 0, 11)
        sciT = ov.tensor("sciT", [128, 2048], I32, 0, 11)
        adjb = ov.tensor("adjb", [128, 2048], BF16, 0, 11)

        ps_all = root.enter_context(nc.psum_tensor("ps_all", [128, 4096], F32))
        ps = [ps_all[:, i * 512:(i + 1) * 512] for i in range(8)]
        psb_all = ps_all.bitcast(BF16)
        psb = [psb_all[:, i * 1024:(i + 1) * 1024] for i in range(8)]
        psc = [ps_all[:, 0:1024], ps_all[:, 3072:4096]]
        bank_guard = [None] * 8

        def bank_w(b):
            return [bank_guard[b]] if bank_guard[b] is not None else []

        qT = ov.tensor("qT", [128, NCK * OWN], BF16, 0, 3)
        mntf = ov.tensor("mntf", [128, NCK * M4], F32, 0, 3)
        mntb = ov.tensor("mntb", [128, NCK * M4], BF16, 0, 3)
        xTb = ov.tensor("xTb", [128, NCK * S], BF16, 0, 2)
        wqb = ov.tensor("wqb", [128, NCK * D], BF16, 0, 1)
        swst = [ov.tensor("swst0", [128, 1024], F32, 0, 2, at=152736),
                ov.tensor("swst1", [128, 1024], F32, 0, 2, at=131072)]
        scrA1 = ov.tensor("scrA1", [128, 1024], F32, 0, 1)
        scrA2 = ov.tensor("scrA2", [128, 1024], F32, 0, 1)

        xchunk_ld = []
        for lc in range(4):
            for dc in range(NCK):
                k.emit("gpsimd", lambda g, lc=lc, dc=dc: g.dma_start(
                    xTb[:, dc * S + lc * CH: dc * S + (lc + 1) * CH],
                    xT.ap()[dc * 128:(dc + 1) * 128, lc * CH:(lc + 1) * CH]),
                    waits=k.ch(gld) if dc == 0 else (), inc=gld, inc_k=16)
            xchunk_ld.append(gld.n)
            k.chend(gld)
        for dst, src in ((biasb, biasc), (idb, identb), (onecb, onescb),
                         (onerf, onesrf), (onecf, onescf)):
            k.emit("sync", lambda sy, dst=dst, src=src: sy.dma_start(
                dst[:], src.ap()), waits=k.ch(sld), inc=sld, inc_k=16)
            k.chend(sld)
        k.emit("sync", lambda sy: sy.dma_start(
            mntf[:].rearrange("p (c m) -> p c m", m=M4),
            mnT.ap().rearrange("(c p) m -> p c m", p=128)),
            waits=k.ch(sld), inc=sld, inc_k=16)
        k.chend(sld)
        k.emit("gpsimd", lambda g: g.dma_start(
            mntb[:].rearrange("p (c m) -> p c m", m=M4),
            mnT.ap().rearrange("(c p) m -> p c m", p=128)),
            waits=k.ch(gld), inc=gld, inc_k=16)
        k.chend(gld)
        k.emit("sync", lambda sy: sy.dma_start(gb[0:1, 0:1], gate.ap()),
               waits=k.ch(sld), inc=sld, inc_k=16)
        k.chend(sld)
        consts_ld = sld.n
        k.emit("gpsimd", lambda g: g.dma_start(
            wqb[:].rearrange("p (c w) -> p c w", w=D),
            wqT.ap().rearrange("(c p) w -> p c w", p=128)),
            waits=k.ch(gld), inc=gld, inc_k=16)
        wq_ld = gld.n
        k.chend(gld)

        swjobs = []
        for wi, w_dram in enumerate((wupT, wdnT)):
            nrb = (D if wi == 0 else DI) // 128
            ncol = (DI if wi == 0 else D)
            for rb in range(nrb):
                for chb in range(ncol // 1024):
                    swjobs.append((w_dram, rb, chb, len(swjobs)))
        sw_guard = [None, None]
        sw_abs_prev = [0]

        def emit_swjob2(j):
            w_dram, rb, chb, idx = j
            buf = idx % 2
            k.emit("sync", lambda sy, w_dram=w_dram, rb=rb, chb=chb, buf=buf: sy.dma_start(
                swst[buf][:],
                w_dram.ap()[rb * 128:(rb + 1) * 128,
                            chb * 1024:(chb + 1) * 1024]),
                waits=[sw_guard[buf]] + k.ch(s_sw[buf]),
                inc=s_sw[buf], inc_k=16)
            k.chend(s_sw[buf])
            ldv = s_sw[buf].n
            k.emit("vector", lambda v, buf=buf, idx=idx: v.reduce_sum(
                swpart2[:, idx:idx + 1], swst[buf][:], AX.X,
                apply_absolute_value=True),
                waits=[(s_sw[buf], ldv)], inc=vve)
            sw_guard[buf] = (vve, vve.n)

        qPscr = ov.tensor("qPscr", [128, 1024], F32, 0, 1)
        sigch = {"exp": 0, "v": 0}
        swq = list(swjobs)
        for oc in range(NCK):
            for lc in range(4):
                bank = lc
                wts = [(gld, max(wq_ld, xchunk_ld[lc]))] + bank_w(bank)
                for dc in range(NCK):
                    k.emit("tensor", lambda t, oc=oc, lc=lc, dc=dc, bank=bank: t.matmul(
                        ps[bank][:, 0:CH],
                        wqb[:, dc * D + oc * 128: dc * D + (oc + 1) * 128],
                        xTb[:, dc * S + lc * CH: dc * S + (lc + 1) * CH],
                        start=(dc == 0), stop=(dc == NCK - 1)),
                        waits=wts if dc == 0 else (),
                        inc=tmm if dc == NCK - 1 else None)
                mmv = tmm.n
                if lc < 2:
                    k.emit("vector", lambda v, oc=oc, lc=lc, bank=bank: v.tensor_copy(
                        qT[:, oc * OWN + lc * CH: oc * OWN + (lc + 1) * CH],
                        ps[bank][:, 0:CH]),
                        waits=[(tmm, mmv), (vve, vve.n)], inc=vve)
                    bank_guard[bank] = (vve, vve.n)
                else:
                    par = lc - 2
                    k.emit("vector", lambda v, bank=bank, par=par: v.tensor_copy(
                        qPscr[:, par * CH:(par + 1) * CH], ps[bank][:, 0:CH]),
                        waits=[(tmm, mmv), (vve, sigch["v"])], inc=vve)
                    qcp = vve.n
                    bank_guard[bank] = (vve, qcp)
                    k.emit("scalar", lambda s, par=par: s.activation(
                        scrA1[:, par * CH:(par + 1) * CH],
                        qPscr[:, par * CH:(par + 1) * CH], AF.Relu,
                        scale=-1.0),
                        waits=[(vve, qcp), (sse, sigch["exp"])], inc=sse)
                    k.emit("scalar", lambda s, par=par: s.activation(
                        scrA2[:, par * CH:(par + 1) * CH],
                        scrA1[:, par * CH:(par + 1) * CH], AF.Exp,
                        scale=-1.0), waits=[(sse, sse.n)], inc=sse)
                    sigch["exp"] = sse.n
                    k.emit("vector", lambda v, par=par: v.scalar_tensor_tensor(
                        scrA1[:, par * CH:(par + 1) * CH],
                        qPscr[:, par * CH:(par + 1) * CH],
                        0.0, scrA2[:, par * CH:(par + 1) * CH],
                        ALU.max, ALU.add),
                        waits=[(sse, sigch["exp"]), (vve, vve.n)], inc=vve)
                    k.emit("vector", lambda v, oc=oc, par=par: v.reduce_sum(
                        sbarP[:, par * NCK + oc: par * NCK + oc + 1],
                        scrA1[:, par * CH:(par + 1) * CH], AX.X),
                        waits=[(vve, vve.n)], inc=vve)
                    sigch["v"] = vve.n
        a1_v = vve.n
        a1_t = tmm.n
        endphase()

        lsT = ov.tensor("lsT", [128, NCK * OWN], BF16, 2, 6)
        khh = ov.tensor("khh", [128, 2 * S], BF16, 2, 2)
        vpp = ov.tensor("vpp", [128, 16 * 256], BF16, 2, 2)
        kw = ov.tensor("kw", [128, NCK * 256], BF16, 2, 2)
        vw = ov.tensor("vw", [128, NCK * 256], BF16, 2, 2)
        expm = ov.tensor("expm", [128, 2 * 1024], BF16, 2, 2)
        trib = ov.tensor("trib", [128, 4 * CH], BF16, 2, 2)
        denr = ov.tensor("denr", [1, CH], F32, 2, 2)
        denb = ov.tensor("denb", [128, CH], F32, 2, 2)

        k.emit("sync", lambda sy: sy.dma_start(trib[:], tri.ap()),
               waits=k.ch(sld), inc=sld, inc_k=16)
        tri_ld = sld.n
        k.chend(sld)

        k.emit("scalar", lambda s: s.activation(rr[0:1, 15:16], gb[0:1, 0:1],
                                                AF.Sigmoid),
               waits=[(sld, consts_ld)], inc=sse)
        gsig = sse.n
        wts = [(sse, gsig)] + bank_w(2)
        k.emit("tensor", lambda t: t.matmul(ps[2][:, 0:1], onerf[:, :],
                                            rr[0:1, 15:16], start=True,
                                            stop=True), waits=wts, inc=tmm)
        k.emit("vector", lambda v: v.tensor_copy(gb[:, 0:1], ps[2][:, 0:1]),
               waits=[(tmm, tmm.n)], inc=vve)
        bank_guard[2] = (vve, vve.n)
        k.drain("vector", inc=vve)
        k.emit("vector", lambda v: v.tensor_scalar(gb[:, 1:2], gb[:, 0:1],
                                                   -1.0, 1.0, ALU.mult,
                                                   ALU.add), inc=vve)
        k.drain("vector", inc=vve)
        gb_v = vve.n

        pairs_by_o = {0: [p for p in PAIRS if p[0] == 0],
                      1: [p for p in PAIRS if p[0] == 1]}
        kw_t = 0
        vw_t = 0
        att_t = a1_t
        sc_guard = [[], []]
        expm_rd = {}
        expp_rd = {}
        grp_guard = [[], []]
        den_bc_t = [0]
        finq = []

        def emit_group_fin(info):
            h, o, par, pvb, dnb, pv_bank, dn_bank, att_mm = info
            k.emit("vector", lambda v, dnb=dnb: v.reciprocal(
                denr[0:1, 0:CH], dnb[0:1, 0:CH]),
                waits=[(tmm, att_mm), (vve, gb_v),
                       (tmm, den_bc_t[0])], inc=vve)
            dn_guard_v = vve.n
            k.drain("vector", inc=vve)
            k.emit("vector", lambda v: v.tensor_single_scalar(
                denr[0:1, 0:CH], denr[0:1, 0:CH], gb[0:1, 1:2],
                ALU.mult), inc=vve)
            k.drain("vector", inc=vve)
            drv = vve.n
            wts = [(vve, drv)] + bank_w(6)
            k.emit("tensor", lambda t: t.matmul(
                ps[6][:, 0:CH], onerf[:, :], denr[0:1, 0:CH],
                start=True, stop=True), waits=wts, inc=tmm)
            den_bc_t[0] = tmm.n
            k.emit("vector", lambda v: v.tensor_copy(
                denb[:], ps[6][:, 0:CH]), waits=[(tmm, den_bc_t[0])],
                inc=vve)
            bank_guard[6] = (vve, vve.n)
            k.drain("vector", inc=vve)
            k.emit("vector", lambda v, h=h, o=o, pvb=pvb: v.tensor_mul(
                lsT[:, h * OWN + o * CH: h * OWN + (o + 1) * CH],
                pvb[:, 0:CH], denb[:]), inc=vve)
            grp_guard[par] = [(vve, vve.n)]
            bank_guard[pv_bank] = (vve, vve.n)
            bank_guard[dn_bank] = (vve, dn_guard_v)

        gidx = 0
        for hp in range(8):
            k.emit("gpsimd", lambda g, hp=hp: g.dma_start(
                kw[:].rearrange("p (c w) -> p c w", w=256),
                wkT.ap()[:, hp * 256:(hp + 1) * 256].rearrange(
                    "(c p) w -> p c w", p=128)),
                waits=[(tmm, kw_t)] + k.ch(s_kv), inc=s_kv, inc_k=16)
            kld = s_kv.n
            k.chend(s_kv)
            k.emit("gpsimd", lambda g, hp=hp: g.dma_start(
                vw[:].rearrange("p (c w) -> p c w", w=256),
                wvT.ap()[:, hp * 256:(hp + 1) * 256].rearrange(
                    "(c p) w -> p c w", p=128)),
                waits=[(tmm, vw_t)] + k.ch(s_kv), inc=s_kv, inc_k=16)
            vld = s_kv.n
            k.chend(s_kv)
            for hip in range(2):
                for st in range(4):
                    bank = (hip * 4 + st) % 2
                    wts = [(s_kv, kld), (gld, wq_ld)] + bank_w(bank)
                    for dc in range(NCK):
                        k.emit("tensor", lambda t, hip=hip, st=st, dc=dc, bank=bank: t.matmul(
                            ps[bank][:, 0:CH],
                            kw[:, dc * 256 + hip * 128: dc * 256 + hip * 128 + 128],
                            xTb[:, dc * S + st * CH: dc * S + (st + 1) * CH],
                            start=(dc == 0), stop=(dc == NCK - 1)),
                            waits=wts if dc == 0 else (),
                            inc=tmm if dc == NCK - 1 else None)
                    k.emit("scalar", lambda s, hip=hip, st=st, bank=bank: s.activation(
                        khh[:, hip * S + st * CH: hip * S + (st + 1) * CH],
                        ps[bank][:, 0:CH], AF.Copy),
                        waits=[(tmm, max(tmm.n, att_t)), (sse, sse.n)],
                        inc=sse)
                    bank_guard[bank] = (sse, sse.n)
            kw_t = tmm.n
            while finq:
                emit_group_fin(finq.pop(0))
            for ss in range(16):
                bank = 2 + ss % 2
                wts = [(s_kv, vld)] + bank_w(bank)
                for dc in range(NCK):
                    k.emit("tensor", lambda t, ss=ss, dc=dc, bank=bank: t.matmul(
                        ps[bank][:, 0:256],
                        xTb[:, dc * S + ss * 128: dc * S + (ss + 1) * 128],
                        vw[:, dc * 256:(dc + 1) * 256],
                        start=(dc == 0), stop=(dc == NCK - 1)),
                        waits=wts if dc == 0 else (),
                        inc=tmm if dc == NCK - 1 else None)
                k.emit("scalar", lambda s, ss=ss, bank=bank: s.activation(
                    vpp[:, ss * 256:(ss + 1) * 256], ps[bank][:, 0:256],
                    AF.Copy),
                    waits=[(tmm, max(tmm.n, att_t)), (sse, sse.n)], inc=sse)
                bank_guard[bank] = (sse, sse.n)
            kvgen_v = (sse, sse.n)
            vw_t = tmm.n

            for hip in range(2):
                h = hp * 2 + hip
                for o in range(2):
                    plist = pairs_by_o[o]
                    blocks = [(pc, kb, kind, bi)
                              for (po, pc, kind, bi) in plist
                              for kb in range(4)]
                    bpairs = [tuple(blocks[2 * j:2 * j + 2])
                              for j in range(len(blocks) // 2)]
                    nblk = len(blocks)
                    par = gidx % 2
                    pvb = ps[4] if par == 0 else ps[2]
                    dnb = ps[5] if par == 0 else ps[3]
                    pv_bank = 4 if par == 0 else 2
                    dn_bank = 5 if par == 0 else 3

                    pending = None

                    def emit_denpv(pend):
                        (j, bp, rdy) = pend
                        slot = j % 2
                        for half, (pc, kb, kind, bi) in enumerate(bp):
                            bi0 = 2 * j + half
                            first = (bi0 == 0)
                            last = (bi0 == nblk - 1)
                            dwts = [rdy]
                            if first:
                                dwts += (grp_guard[par] + bank_w(pv_bank)
                                         + bank_w(dn_bank))
                            k.emit("tensor", lambda t, slot=slot, half=half, first=first, last=last, dnb=dnb: t.matmul(
                                dnb[0:1, 0:CH], onecb[:, 0:1],
                                expm[:, slot * 1024 + half * CH:
                                     slot * 1024 + (half + 1) * CH],
                                start=first, stop=last), waits=dwts, inc=tmm)
                            k.emit("tensor", lambda t, hip=hip, pc=pc, kb=kb, slot=slot, half=half, first=first, last=last, pvb=pvb: t.matmul(
                                pvb[:, 0:CH],
                                vpp[:, (pc * 4 + kb) * 256 + hip * 128:
                                    (pc * 4 + kb) * 256 + hip * 128 + 128],
                                expm[:, slot * 1024 + half * CH:
                                     slot * 1024 + (half + 1) * CH],
                                start=first, stop=last), inc=tmm)
                        expm_rd[slot] = tmm.n

                    for j, bp in enumerate(bpairs):
                        scb = j % 2
                        scbanks = (0, 1) if scb == 0 else (6, 7)
                        for half, (pc, kb, kind, bi) in enumerate(bp):
                            wts = [kvgen_v, (sld, tri_ld)]
                            if half == 0:
                                wts += (sc_guard[scb] + bank_w(scbanks[0])
                                        + bank_w(scbanks[1]))
                            diag = bp[0][2] == "diag"
                            k.emit("tensor", lambda t, hip=hip, h=h, o=o, pc=pc, kb=kb, scb=scb, half=half, diag=diag: t.matmul(
                                psc[scb][:, half * CH:(half + 1) * CH],
                                khh[:, hip * S + pc * CH + kb * 128:
                                    hip * S + pc * CH + kb * 128 + 128],
                                qT[:, h * OWN + o * CH: h * OWN + (o + 1) * CH],
                                start=True, stop=not diag),
                                waits=wts, inc=None if diag else tmm)
                        if bp[0][2] == "diag":
                            kb0 = bp[0][1]
                            for half in range(2):
                                k.emit("tensor", lambda t, scb=scb, kb0=kb0, half=half: t.matmul(
                                    psc[scb][:, half * CH:(half + 1) * CH],
                                    idb[:],
                                    trib[:, (kb0 + half) * CH:
                                         (kb0 + half + 1) * CH],
                                    start=False, stop=True),
                                    waits=[(sld, tri_ld)] if half == 0 else (),
                                    inc=tmm if half == 1 else None)
                        smm = tmm.n
                        slot = j % 2
                        kind = bp[0][2]
                        if kind == "bias":
                            bi = bp[0][3]
                            k.emit("scalar", lambda s, scb=scb, slot=slot, bi=bi: s.activation(
                                expm[:, slot * 1024:(slot + 1) * 1024],
                                psc[scb][:, 0:1024], AF.Exp,
                                bias=biasb[:, bi:bi + 1], scale=ATT_SCALE),
                                waits=[(tmm, max(smm, expm_rd.get(slot, 0)))],
                                inc=sse)
                            rdy = (sse, sse.n)
                            sc_guard[scb] = [(sse, sse.n)]
                            bank_guard[scbanks[0]] = (sse, sse.n)
                            bank_guard[scbanks[1]] = (sse, sse.n)
                        else:
                            k.emit("scalar", lambda s, scb=scb, slot=slot: s.activation(
                                expm[:, slot * 1024:(slot + 1) * 1024],
                                psc[scb][:, 0:1024], AF.Exp, scale=ATT_SCALE),
                                waits=[(tmm, max(smm, expm_rd.get(slot, 0)))],
                                inc=sse)
                            rdy = (sse, sse.n)
                            sc_guard[scb] = [(sse, sse.n)]
                            bank_guard[scbanks[0]] = (sse, sse.n)
                            bank_guard[scbanks[1]] = (sse, sse.n)
                        if pending is not None:
                            emit_denpv(pending)
                        pending = (j, bp, rdy)
                    emit_denpv(pending)
                    att_mm = tmm.n
                    finq.append((h, o, par, pvb, dnb, pv_bank, dn_bank,
                                 att_mm))
                    if len(finq) > 1:
                        emit_group_fin(finq.pop(0))
                    att_t = tmm.n
                    gidx += 1
                    for _ in range(4):
                        if swq:
                            emit_swjob2(swq.pop(0))
        while finq:
            emit_group_fin(finq.pop(0))
        while swq:
            emit_swjob2(swq.pop(0))
        a2_t = tmm.n
        a2_v = vve.n
        endphase()

        sigT = ov.tensor("sigT", [128, NCK * OWN], BF16, 3, 4)
        scrB1 = ov.tensor("scrB1", [128, 2048], F32, 3, 4)
        scrB2 = ov.tensor("scrB2", [128, 2048], F32, 3, 4)

        ch = {"e": {}, "v": {}}
        for oc in range(NCK):
            off = (oc % 2) * 1024
            k.emit("scalar", lambda s, oc=oc, off=off: s.activation(
                scrB1[:, off:off + 1024], qT[:, oc * OWN:(oc + 1) * OWN],
                AF.Relu, scale=-1.0),
                waits=[(vve, ch["v"].get(oc - 2, a2_v))], inc=sse)
            k.emit("scalar", lambda s, off=off: s.activation(
                scrB2[:, off:off + 1024], scrB1[:, off:off + 1024],
                AF.Exp, scale=-1.0), waits=[(sse, sse.n)], inc=sse)
            ch["e"][oc] = sse.n
            k.emit("vector", lambda v, oc=oc, off=off: v.scalar_tensor_tensor(
                sigT[:, oc * OWN:(oc + 1) * OWN],
                qT[:, oc * OWN:(oc + 1) * OWN], 0.0, scrB2[:, off:off + 1024],
                ALU.max, ALU.add), waits=[(sse, ch["e"][oc]), (vve, vve.n)],
                inc=vve)
            k.emit("vector", lambda v, oc=oc: v.reduce_sum(
                sbarO[:, oc:oc + 1], sigT[:, oc * OWN:(oc + 1) * OWN], AX.X),
                waits=[(vve, vve.n)], inc=vve)
            ch["v"][oc] = vve.n
        sigT_v = vve.n
        k.emit("vector", lambda v: v.tensor_add(sbarF[:], sbarO[:],
                                                sbarP[:, 0:NCK]),
               waits=[(vve, vve.n)], inc=vve)
        k.emit("vector", lambda v: v.tensor_add(sbarF[:], sbarF[:],
                                                sbarP[:, NCK:2 * NCK]),
               waits=[(vve, vve.n)], inc=vve)
        k.drain("vector", inc=vve)
        sbar_v = vve.n

        wts = [(vve, sbar_v), (sld, consts_ld)] + bank_w(0)
        for oc in range(NCK):
            k.emit("tensor", lambda t, oc=oc: t.matmul(
                ps[0][0:1, 0:M4], sbarF[:, oc:oc + 1],
                mntf[:, oc * M4:(oc + 1) * M4],
                start=(oc == 0), stop=(oc == NCK - 1)),
                waits=wts if oc == 0 else (),
                inc=tmm if oc == NCK - 1 else None)
        rel_mm = tmm.n
        k.emit("vector", lambda v: v.tensor_single_scalar(
            rr[0:1, 0:M4], ps[0][0:1, 0:M4], 1.0 / S, ALU.mult),
            waits=[(tmm, rel_mm)], inc=vve)
        bank_guard[0] = (vve, vve.n)
        k.drain("vector", inc=vve)
        k.emit("vector", lambda v: v.reduce_max(rr[0:1, 4:5], rr[0:1, 0:M4],
                                                AX.X), inc=vve)
        k.drain("vector", inc=vve)
        k.emit("vector", lambda v: v.tensor_single_scalar(
            rr[0:1, 5:9], rr[0:1, 0:M4], rr[0:1, 4:5], ALU.subtract),
            inc=vve)
        k.drain("vector", inc=vve)
        vv0 = vve.n
        k.emit("scalar", lambda s: s.activation(rr[0:1, 9:13], rr[0:1, 5:9],
                                                AF.Exp), waits=[(vve, vv0)],
               inc=sse)
        k.emit("vector", lambda v: v.reduce_sum(rr[0:1, 13:14], rr[0:1, 9:13],
                                                AX.X), waits=[(sse, sse.n)],
               inc=vve)
        k.drain("vector", inc=vve)
        k.emit("vector", lambda v: v.reciprocal(rr[0:1, 14:15],
                                                rr[0:1, 13:14]), inc=vve)
        k.drain("vector", inc=vve)
        k.emit("vector", lambda v: v.tensor_single_scalar(
            rr[0:1, 9:13], rr[0:1, 9:13], rr[0:1, 14:15], ALU.mult), inc=vve)
        k.drain("vector", inc=vve)
        wtil_v = vve.n
        wts = [(vve, wtil_v)] + bank_w(1)
        k.emit("tensor", lambda t: t.matmul(ps[1][:, 0:M4], onerf[:, :],
                                            rr[0:1, 9:13], start=True,
                                            stop=True), waits=wts, inc=tmm)
        k.emit("vector", lambda v: v.tensor_copy(wtilb[:], ps[1][:, 0:M4]),
               waits=[(tmm, tmm.n)], inc=vve)
        bank_guard[1] = (vve, vve.n)
        k.drain("vector", inc=vve)
        wtilb_v = vve.n

        for ss in range(8):
            bank = 2 + ss % 2
            wts = [(vve, sigT_v), (gld, gld.n)] + bank_w(bank)
            for oc in range(NCK):
                k.emit("tensor", lambda t, ss=ss, oc=oc, bank=bank: t.matmul(
                    ps[bank][:, 0:M4],
                    sigT[:, oc * OWN + ss * 128: oc * OWN + (ss + 1) * 128],
                    mntb[:, oc * M4:(oc + 1) * M4],
                    start=(oc == 0), stop=(oc == NCK - 1)),
                    waits=wts if oc == 0 else (),
                    inc=tmm if oc == NCK - 1 else None)
            rmm = tmm.n
            k.emit("vector", lambda v, bank=bank: v.tensor_single_scalar(
                scrB1[:, 0:M4], ps[bank][:, 0:M4], ATT_EPS, ALU.max),
                waits=[(tmm, rmm), (vve, wtilb_v)], inc=vve)
            bank_guard[bank] = (vve, vve.n)
            k.drain("vector", inc=vve)
            k.emit("vector", lambda v: v.reciprocal(scrB2[:, 0:M4],
                                                    scrB1[:, 0:M4]), inc=vve)
            k.drain("vector", inc=vve)
            k.emit("vector", lambda v, ss=ss: v.tensor_mul(
                coefb[:, ss * M4:(ss + 1) * M4], scrB2[:, 0:M4], wtilb[:]),
                inc=vve)
            k.drain("vector", inc=vve)
        coef_v = vve.n

        if dbg:
            for nm, tile, w in (("d_qT", qT, (vve, a1_v)),
                                ("d_sbar", sbarF, (vve, sbar_v)),
                                ("d_rel", rr, (vve, wtil_v)),
                                ("d_coef", coefb, (vve, coef_v)),
                                ("d_lsT", lsT, (vve, a2_v)),
                                ("d_sigT", sigT, (vve, sigT_v))):
                k.emit("sync", lambda sy, nm=nm, tile=tile: sy.dma_start(
                    dbgt[nm].ap(), tile[:]), waits=[w] + k.ch(sdbg),
                    inc=sdbg, inc_k=16)
                k.chend(sdbg)
        dbg_gate0 = sdbg.n
        endphase()

        membuf = [ov.tensor(f"membuf{i}", [128, M4 * NCK * 256], BF16, 4, 4)
                  for i in range(2)]
        memtok = ov.tensor("memtok", [128, 8 * D], BF16, 4, 5)

        mem_t = [a2_t, a2_t]
        ev_v = 0
        ev_s = 0
        for et in range(8):
            buf = et % 2
            for m in range(M4):
                c0 = m * NCK * 256
                k.emit("gpsimd", lambda g, et=et, m=m, buf=buf, c0=c0: g.dma_start(
                    membuf[buf][:, c0:c0 + NCK * 256].rearrange(
                        "p (c w) -> p c w", w=256),
                    mem.ap()[m * D:(m + 1) * D,
                             et * 256:(et + 1) * 256].rearrange(
                        "(c p) w -> p c w", p=128)),
                    waits=([(tmm, mem_t[buf]), (sdbg, dbg_gate0)]
                           + k.ch(s_me[buf]) if m == 0 else ()),
                    inc=s_me[buf], inc_k=16)
            mld = s_me[buf].n
            k.chend(s_me[buf])
            for ss in range(8):
                grp = 4 * (ss % 2)
                wts0 = [(s_me[buf], mld), (vve, coef_v)]
                for m in range(M4):
                    bank = grp + m
                    wts = wts0 + bank_w(bank)
                    for dc in range(NCK):
                        k.emit("tensor", lambda t, ss=ss, m=m, dc=dc, bank=bank, buf=buf: t.matmul(
                            ps[bank][:, 0:256],
                            sigT[:, dc * OWN + ss * 128: dc * OWN + (ss + 1) * 128],
                            membuf[buf][:, (m * NCK + dc) * 256:
                                        (m * NCK + dc + 1) * 256],
                            start=(dc == 0), stop=(dc == NCK - 1)),
                            waits=wts if dc == 0 else (),
                            inc=tmm if dc == NCK - 1 else None)
                amm = tmm.n
                par = (et * 8 + ss) % 2
                off = 512 * par
                k.emit("scalar", lambda s, grp=grp, ss=ss, off=off: s.activation(
                    scrB1[:, off:off + 256], ps[grp + 0][:, 0:256], AF.Copy,
                    scale=coefb[:, ss * M4:ss * M4 + 1]),
                    waits=[(tmm, amm), (vve, ev_v)], inc=sse)
                k.emit("scalar", lambda s, grp=grp, ss=ss, off=off: s.activation(
                    scrB1[:, off + 256:off + 512], ps[grp + 2][:, 0:256],
                    AF.Copy, scale=coefb[:, ss * M4 + 2:ss * M4 + 3]),
                    inc=sse)
                sev = sse.n
                k.emit("vector", lambda v, grp=grp, ss=ss, off=off: v.scalar_tensor_tensor(
                    scrB2[:, off:off + 256], ps[grp + 1][:, 0:256],
                    coefb[:, ss * M4 + 1:ss * M4 + 2], scrB1[:, off:off + 256],
                    ALU.mult, ALU.add), waits=[(sse, sev)], inc=vve)
                k.emit("vector", lambda v, grp=grp, ss=ss, off=off: v.scalar_tensor_tensor(
                    scrB2[:, off + 256:off + 512], ps[grp + 3][:, 0:256],
                    coefb[:, ss * M4 + 3:ss * M4 + 4],
                    scrB1[:, off + 256:off + 512], ALU.mult, ALU.add),
                    inc=vve)
                k.drain("vector", inc=vve)
                k.emit("vector", lambda v, ss=ss, et=et, off=off: v.tensor_add(
                    memtok[:, ss * D + et * 256: ss * D + (et + 1) * 256],
                    scrB2[:, off:off + 256], scrB2[:, off + 256:off + 512]),
                    inc=vve)
                for m in range(M4):
                    bank_guard[grp + m] = (vve, vve.n)
                ev_v = vve.n
                ev_s = sev
            mem_t[buf] = tmm.n
        k.drain("vector", inc=vve)
        k.drain("scalar", inc=sse)
        memtok_v = vve.n
        swp_v, swp_s = vve.n, sse.n

        k.emit("vector", lambda v: v.reduce_sum(scrB1[:, 0:1],
                                                swpart2[:, 0:64], AX.X),
               waits=[(vve, swp_v), (sse, swp_s)], inc=vve)
        k.drain("vector", inc=vve)
        k.emit("vector", lambda v: v.reduce_sum(scrB1[:, 1:2],
                                                swpart2[:, 64:128], AX.X),
               inc=vve)
        k.drain("vector", inc=vve)
        swred_v = vve.n
        wts = [(vve, swred_v)] + bank_w(0)
        k.emit("tensor", lambda t: t.matmul(ps[0][0:1, 0:1], scrB1[:, 0:1],
                                            onecf[:, 0:1], start=True,
                                            stop=True), waits=wts, inc=tmm)
        k.emit("tensor", lambda t: t.matmul(ps[0][0:1, 1:2], scrB1[:, 1:2],
                                            onecf[:, 0:1], start=True,
                                            stop=True), inc=tmm)
        swmm = tmm.n
        k.emit("vector", lambda v: v.tensor_copy(
            swr[0:1, 0:2], ps[0][0:1, 0:2]), waits=[(tmm, swmm)], inc=vve)
        bank_guard[0] = (vve, vve.n)
        k.drain("vector", inc=vve)
        k.emit("vector", lambda v: v.tensor_scalar(
            swr[0:1, 2:4], swr[0:1, 0:2], 1.0 / (D * DI), Q_EPS,
            ALU.mult, ALU.max), inc=vve)
        k.drain("vector", inc=vve)
        k.emit("vector", lambda v: v.tensor_single_scalar(
            swr[0:1, 4:6], swr[0:1, 2:4], 0.5, ALU.mult), inc=vve)
        k.drain("vector", inc=vve)
        swsc_v = vve.n
        wts = [(vve, swsc_v)] + bank_w(1)
        k.emit("tensor", lambda t: t.matmul(ps[1][:, 0:4], onerf[:, :],
                                            swr[0:1, 2:6], start=True,
                                            stop=True), waits=wts, inc=tmm)
        swbc = tmm.n
        for (dst, j) in ((swupb, 0), (swdnb, 1), (thrup, 2), (thrdn, 3)):
            k.emit("vector", lambda v, dst=dst, j=j: v.tensor_copy(
                dst[:], ps[1][:, j:j + 1]), waits=[(tmm, swbc)], inc=vve)
        bank_guard[1] = (vve, vve.n)
        k.drain("vector", inc=vve)
        sw_v = vve.n

        if dbg:
            k.emit("sync", lambda sy: sy.dma_start(
                dbgt["d_memtok"].ap(), memtok[:]),
                waits=[(vve, memtok_v)] + k.ch(sdbg), inc=sdbg, inc_k=16)
            k.chend(sdbg)
            k.emit("sync", lambda sy: sy.dma_start(
                dbgt["d_sw"].ap()[0:1, 0:6], swr[0:1, 0:6]),
                waits=[(vve, sw_v)] + k.ch(sdbg), inc=sdbg, inc_k=16)
            k.chend(sdbg)
        dbg_gate1 = sdbg.n
        endphase()

        attn_inT = lsT
        for oc in range(NCK):
            bank = oc % 2
            wts = [(vve, memtok_v), (sld, consts_ld),
                   (sdbg, dbg_gate1)] + bank_w(bank)
            for ss in range(8):
                k.emit("tensor", lambda t, ss=ss, oc=oc, bank=bank: t.transpose(
                    psb[bank][:, ss * 128:(ss + 1) * 128],
                    memtok[:, ss * D + oc * 128: ss * D + (oc + 1) * 128],
                    idb[:]), waits=wts if ss == 0 else (),
                    inc=tmm if ss == 7 else None)
            tpv = tmm.n
            k.emit("vector", lambda v, oc=oc, bank=bank: v.scalar_tensor_tensor(
                attn_inT[:, oc * OWN:(oc + 1) * OWN],
                psb[bank][:, 0:1024], gb[:, 0:1],
                lsT[:, oc * OWN:(oc + 1) * OWN],
                ALU.mult, ALU.add), waits=[(tmm, tpv), (vve, a2_v)],
                inc=vve)
            bank_guard[bank] = (vve, vve.n)
        ai_v = vve.n
        endphase()

        x2 = ov.tensor("x2", [128, 8 * D], BF16, 6, 11, at=139264)
        wos = [ov.tensor("wos0", [128, NCK * CH], BF16, 6, 6, at=0),
               ov.tensor("wos1", [128, NCK * CH], BF16, 6, 6, at=16384)]
        xos = [ov.tensor(f"xos{i}", [128, CH], F32, 6, 6) for i in range(2)]
        sqscr = ov.tensor("sqscr", [128, CH], F32, 6, 11)
        pg1 = ov.tensor("pg1", [128, 32], F32, 6, 8)
        qg1 = ov.tensor("qg1", [128, 32], F32, 6, 8)
        sc3 = ov.tensor("sc3", [128, 8], F32, 6, 11)
        ncol = ov.tensor("ncol", [128, 8], F32, 6, 11)
        sr8 = ov.tensor("sr8", [128, 48], F32, 0, 11)
        wfs = ov.tensor("wfs", [128, NCK * 256], F32, 6, 8, at=73728)
        wub = [ov.tensor("wub0", [128, NCK * 256], BF16, 6, 8, at=90112),
               ov.tensor("wub1", [128, NCK * 256], BF16, 6, 8, at=98304)]
        tnb = ov.tensor("tnb", [128, 2048], BF16, 6, 10, at=172032)
        xqT = ov.tensor("xqT", [128, 8 * D], BF16, 7, 8, at=106496)
        xqtile = ov.tensor("xqtile", [128, D], BF16, 7, 8)
        rtq = ov.tensor("rtq", [128, 9 * DI], BF16, 8, 10, at=0)
        wdb = [ov.tensor("wdb0", [128, NCI * 256], BF16, 9, 10, at=73728),
               ov.tensor("wdb1", [128, NCI * 256], BF16, 9, 10, at=106496)]
        wfd = [ov.tensor("wfd0", [128, 8 * 256], F32, 9, 10, at=122880),
               ov.tensor("wfd1", [128, 8 * 256], F32, 9, 10, at=131072)]
        sciT2 = ov.tensor("sciT2", [128, 2048], I32, 9, 10, at=90112)
        scq2 = ov.tensor("scq2", [128, 2048], F32, 9, 10, at=98304)
        adjb2 = ov.tensor("adjb2", [128, 2048], BF16, 9, 10,
                          at=ov.addr["xqtile"])
        obuf = [ov.tensor("obuf0", [128, D], F32, 10, 11, at=90112),
                ov.tensor("obuf1", [128, D], F32, 10, 11, at=98304)]
        pmax = ov.tensor("pmax", [128, 32], F32, 8, 9)
        pgrid = ov.tensor("pgrid", [128, 64], F32, 9, 11)
        qgrid = ov.tensor("qgrid", [128, 64], F32, 9, 11)
        rqtile = ov.tensor("rqtile", [128, 2048], BF16, 9, 10)

        xob = [ov.tensor(f"xob{i}", [128, CH], BF16, 6, 6) for i in range(2)]
        tnbg = ov.tensor("tnbg", [128, 2048], BF16, 6, 6, at=65536)
        mng = ov.tensor("mng", [128, 32], F32, 6, 8)
        mxg = ov.tensor("mxg", [128, 32], F32, 6, 8)
        cq1 = ov.tensor("cq1", [128, 16], F32, 6, 8)

        wo_t = [0, 0]
        xo_v = [0, 0]
        xob_t = [0, 0]

        last_s = [sse.n]
        last_v = [vve.n]
        last_g = [gcc.n]

        def s_emit(fn, waits=()):
            k.emit("scalar", fn, waits=[(sse, last_s[0])] + list(waits),
                   inc=sse)
            last_s[0] = sse.n
            return sse.n

        def v_emit(fn, waits=()):
            k.emit("vector", fn, waits=[(vve, last_v[0])] + list(waits),
                   inc=vve)
            last_v[0] = vve.n
            return vve.n

        def g_emit(fn, waits=()):
            k.emit("gpsimd", fn, waits=[(gcc, last_g[0])] + list(waits),
                   inc=gcc)
            last_g[0] = gcc.n
            return gcc.n

        for ht in range(4):
            buf = ht % 2
            for dq in range(4):
                k.emit("gpsimd", lambda g, ht=ht, dq=dq, buf=buf: g.dma_start(
                    wos[buf][:, dq * 4 * CH:(dq + 1) * 4 * CH].rearrange(
                        "p (c w) -> p c w", w=CH),
                    woT.ap()[dq * 512:(dq + 1) * 512,
                             ht * CH:(ht + 1) * CH].rearrange(
                        "(c p) w -> p c w", p=128)),
                    waits=([(tmm, wo_t[buf]), (sdbg, dbg_gate1)]
                           + k.ch(s_wo[buf]) if dq == 0 else ()),
                    inc=s_wo[buf], inc_k=16)
            wld = s_wo[buf].n
            k.chend(s_wo[buf])
            for ss in range(8):
                xb = ss % 2
                k.emit("sync", lambda sy, ss=ss, ht=ht, xb=xb: sy.dma_start(
                    xos[xb][:, 0:CH],
                    x_own.ap()[ss * 128:(ss + 1) * 128,
                               ht * CH:(ht + 1) * CH]),
                    waits=[(vve, xo_v[xb])] + k.ch(s_xo[xb]),
                    inc=s_xo[xb], inc_k=16)
                xld = s_xo[xb].n
                k.chend(s_xo[xb])
                xcv = v_emit(lambda v, xb=xb: v.tensor_copy(
                    xob[xb][:, 0:CH], xos[xb][:, 0:CH]),
                    waits=[(s_xo[xb], xld), (tmm, xob_t[xb])])
                xo_v[xb] = xcv
                bank = 2 + ss % 2
                wts = [(s_wo[buf], wld), (vve, ai_v)] + bank_w(bank)
                for dc in range(NCK):
                    k.emit("tensor", lambda t, ss=ss, dc=dc, bank=bank, buf=buf: t.matmul(
                        ps[bank][:, 0:CH],
                        attn_inT[:, dc * OWN + ss * 128: dc * OWN + (ss + 1) * 128],
                        wos[buf][:, dc * CH:(dc + 1) * CH],
                        start=(dc == 0), stop=False),
                        waits=wts if dc == 0 else ())
                k.emit("tensor", lambda t, bank=bank, xb=xb: t.matmul(
                    ps[bank][:, 0:CH], idb[:], xob[xb][:, 0:CH],
                    start=False, stop=True),
                    waits=[(vve, xcv)], inc=tmm)
                womm = tmm.n
                xob_t[xb] = womm
                ev = s_emit(lambda s, ss=ss, ht=ht, bank=bank: s.activation(
                    x2[:, ss * D + ht * CH: ss * D + (ht + 1) * CH],
                    ps[bank][:, 0:CH], AF.Copy,
                    accum_out=pg1[:, ss * 4 + ht: ss * 4 + ht + 1]),
                    waits=[(tmm, womm)])
                bank_guard[bank] = (sse, ev)
                s_emit(lambda s, ss=ss, ht=ht: s.activation(
                    sqscr[:, 0:CH],
                    x2[:, ss * D + ht * CH: ss * D + (ht + 1) * CH], AF.Square,
                    accum_out=qg1[:, ss * 4 + ht: ss * 4 + ht + 1]))
                v_emit(lambda v, ss=ss, ht=ht: v.tensor_reduce(
                    mng[:, ss * 4 + ht: ss * 4 + ht + 1],
                    x2[:, ss * D + ht * CH: ss * D + (ht + 1) * CH], AX.X,
                    ALU.min), waits=[(sse, ev)])
                v_emit(lambda v, ss=ss, ht=ht: v.tensor_reduce(
                    mxg[:, ss * 4 + ht: ss * 4 + ht + 1],
                    x2[:, ss * D + ht * CH: ss * D + (ht + 1) * CH], AX.X,
                    ALU.max))
                if ht == 3:
                    v_emit(lambda v, ss=ss: v.reduce_sum(
                        sr8[:, ss:ss + 1], pg1[:, ss * 4:(ss + 1) * 4],
                        AX.X), waits=[(sse, last_s[0])])
                    v_emit(lambda v, ss=ss: v.reduce_sum(
                        sr8[:, 24 + ss:25 + ss], qg1[:, ss * 4:(ss + 1) * 4],
                        AX.X))
                    v_emit(lambda v, ss=ss: v.tensor_reduce(
                        ncol[:, ss:ss + 1], mng[:, ss * 4:(ss + 1) * 4],
                        AX.X, ALU.min))
                    v_emit(lambda v, ss=ss: v.tensor_reduce(
                        sc3[:, ss:ss + 1], mxg[:, ss * 4:(ss + 1) * 4],
                        AX.X, ALU.max))
            wo_t[buf] = tmm.n
        b3_last = tmm.n
        b3_sse = last_s[0]

        up_ld = {}
        tern_half = {}
        up_t = [0, 0]
        tern_v = {}

        def emit_wfs_dma(it):
            lds = []
            for dh in range(2):
                sem = s_up if dh == 0 else s_sw[0]
                qeng = "gpsimd" if dh == 0 else "sync"
                w0 = [tern_half[(it - 1, dh)]] if it >= 1 else []
                k.emit(qeng, lambda g, it=it, dh=dh: g.dma_start(
                    wfs[:, dh * 2048:(dh + 1) * 2048].rearrange(
                        "p (c w) -> p c w", w=256),
                    wupT.ap()[dh * 1024:(dh + 1) * 1024,
                              it * 256:(it + 1) * 256].rearrange(
                        "(c p) w -> p c w", p=128)),
                    waits=w0 + k.ch(sem), inc=sem, inc_k=16)
                lds.append((sem, sem.n))
                k.chend(sem)
            up_ld[it] = lds

        def emit_tern_up(it, eng="vector", extra=()):
            buf = it % 2
            em = v_emit if eng == "vector" else g_emit
            sem = vve if eng == "vector" else gcc
            tb = tnb if eng == "vector" else tnbg
            for dh in range(2):
                em(lambda v, dh=dh, tb=tb: v.tensor_scalar(
                    tb[:, 0:2048], wfs[:, dh * 2048:(dh + 1) * 2048], -1.0,
                    thrup[:, 0:1], ALU.mult, ALU.is_ge),
                    waits=[up_ld[it][dh], (tmm, up_t[buf]),
                           (vve, sw_v)] + list(extra))
                tv = em(lambda v, dh=dh, buf=buf, tb=tb: v.scalar_tensor_tensor(
                    wub[buf][:, dh * 2048:(dh + 1) * 2048],
                    wfs[:, dh * 2048:(dh + 1) * 2048], thrup[:, 0:1],
                    tb[:, 0:2048], ALU.is_ge, ALU.subtract))
                tern_half[(it, dh)] = (sem, tv)
            tern_v[it] = (sem, tv)

        emit_wfs_dma(0)
        emit_tern_up(0)
        emit_wfs_dma(1)
        emit_tern_up(1, extra=[(tmm, b3_last)])
        emit_wfs_dma(2)

        v_emit(lambda v: v.tensor_single_scalar(
            sr8[:, 8:16], sr8[:, 0:8], 1.0 / D, ALU.mult))
        v_emit(lambda v: v.tensor_mul(sr8[:, 16:24], sr8[:, 8:16],
                                      sr8[:, 8:16]))
        v_emit(lambda v: v.tensor_single_scalar(
            sr8[:, 16:24], sr8[:, 16:24], LN_EPS, ALU.subtract))
        v_emit(lambda v: v.scalar_tensor_tensor(
            sr8[:, 32:40], sr8[:, 24:32], 1.0 / D, sr8[:, 16:24],
            ALU.mult, ALU.subtract))
        sqv = s_emit(lambda s: s.activation(
            sr8[:, 40:48], sr8[:, 32:40], AF.Sqrt),
            waits=[(vve, last_v[0])])
        v_emit(lambda v: v.reciprocal(sr8[:, 40:48], sr8[:, 40:48]),
               waits=[(sse, sqv)])
        v_emit(lambda v: v.tensor_sub(sc3[:, 0:8], sc3[:, 0:8],
                                      sr8[:, 8:16]))
        v_emit(lambda v: v.tensor_sub(ncol[:, 0:8], sr8[:, 8:16],
                                      ncol[:, 0:8]))
        v_emit(lambda v: v.tensor_tensor(sc3[:, 0:8], sc3[:, 0:8],
                                         ncol[:, 0:8], ALU.max))
        v_emit(lambda v: v.tensor_mul(sc3[:, 0:8], sc3[:, 0:8],
                                      sr8[:, 40:48]))
        v_emit(lambda v: v.tensor_single_scalar(
            sc3[:, 0:8], sc3[:, 0:8], Q_EPS, ALU.max))
        v_emit(lambda v: v.tensor_scalar(
            f1col[:, 0:8], sc3[:, 0:8], 1.0 / 127.0, swupb[:, 0:1],
            ALU.mult, ALU.mult), waits=[(vve, sw_v)])
        v_emit(lambda v: v.reciprocal(ncol[:, 0:8], sc3[:, 0:8]))
        v_emit(lambda v: v.tensor_single_scalar(
            ncol[:, 0:8], ncol[:, 0:8], 127.0, ALU.mult))
        v_emit(lambda v: v.tensor_mul(cq1[:, 0:8], ncol[:, 0:8],
                                      sr8[:, 40:48]))
        v_emit(lambda v: v.tensor_mul(cq1[:, 8:16], sr8[:, 8:16],
                                      cq1[:, 0:8]))
        fin_v = v_emit(lambda v: v.tensor_scalar(
            cq1[:, 8:16], cq1[:, 8:16], -1.0, 128.0, ALU.mult, ALU.add))

        xq_evac = [0] * 8
        scq_s = [0] * 8
        c2mm = {}

        def emit_c2_mm(it, ss):
            bank = 2 + (ss + it) % 2
            buf = it % 2
            wts = [tern_v[it], (vve, xq_evac[ss])] + bank_w(bank)
            for dc in range(NCK):
                k.emit("tensor", lambda t, ss=ss, dc=dc, it=it, bank=bank, buf=buf: t.matmul(
                    ps[bank][:, 0:256],
                    xqT[:, ss * D + dc * 128: ss * D + dc * 128 + 128],
                    wub[buf][:, dc * 256:(dc + 1) * 256],
                    start=(dc == 0), stop=(dc == NCK - 1)),
                    waits=wts if dc == 0 else (),
                    inc=tmm if dc == NCK - 1 else None)
            umm = tmm.n
            ev = s_emit(lambda s, ss=ss, it=it, bank=bank: s.activation(
                rtq[:, ss * DI + it * 256: ss * DI + (it + 1) * 256],
                ps[bank][:, 0:256], AF.Relu, scale=f1col[:, ss:ss + 1]),
                waits=[(tmm, umm), (vve, fin_v)])
            bank_guard[bank] = (sse, ev)
            return ev

        evac_mark = {}
        tp_hist = {}
        xqv_hist = {}
        for ss in range(8):
            for h2 in range(2):
                step = ss * 2 + h2
                off = h2 * 1024
                xoff = ss * D + h2 * 1024
                scv = None
                s_emit(lambda s, off=off, xoff=xoff, ss=ss: s.activation(
                    sciT[:, off:off + 1024], x2[:, xoff:xoff + 1024],
                    AF.Relu, bias=cq1[:, 8 + ss:9 + ss],
                    scale=cq1[:, ss:ss + 1]),
                    waits=[(vve, fin_v), (vve, xqv_hist.get(step - 2, 0))])
                scv = s_emit(lambda s, off=off, xoff=xoff, ss=ss: s.activation(
                    scq[:, off:off + 1024], x2[:, xoff:xoff + 1024],
                    AF.Relu, bias=cq1[:, 8 + ss:9 + ss],
                    scale=cq1[:, ss:ss + 1]))
                if h2:
                    scq_s[ss] = scv
                v_emit(lambda v, off=off: v.scalar_tensor_tensor(
                    adjb[:, off:off + 1024], scq[:, off:off + 1024], -0.5,
                    sciT[:, off:off + 1024], ALU.add, ALU.is_gt),
                    waits=[(sse, scv)])
                xqv = v_emit(lambda v, off=off: v.scalar_tensor_tensor(
                    xqtile[:, off:off + 1024], sciT[:, off:off + 1024],
                    -128.0, adjb[:, off:off + 1024], ALU.add, ALU.add),
                    waits=[(tmm, tp_hist.get(step - 2, b3_last))])
                xqv_hist[step] = xqv
                bank = step % 2
                wts = [(vve, xqv)] + bank_w(bank)
                for j in range(8):
                    dc = h2 * 8 + j
                    k.emit("tensor", lambda t, dc=dc, j=j, bank=bank: t.transpose(
                        psb[bank][:, j * 128:(j + 1) * 128],
                        xqtile[:, dc * 128:(dc + 1) * 128], idb[:]),
                        waits=wts if j == 0 else (), inc=tmm if j == 7 else None)
                tpv = tmm.n
                tp_hist[step] = tpv
                ev = v_emit(lambda v, ss=ss, h2=h2, bank=bank: v.tensor_copy(
                    xqT[:, ss * D + h2 * 1024: ss * D + (h2 + 1) * 1024],
                    psb[bank][:, 0:1024]), waits=[(tmm, tpv)])
                bank_guard[bank] = (vve, ev)
            xq_evac[ss] = last_v[0]
            emit_c2_mm(0, ss)
            emit_c2_mm(1, ss)

        up_t[0] = tmm.n
        up_t[1] = tmm.n
        nrm_v = [0] * 8
        for it in range(2, NCK):
            emit_tern_up(it)
            if it + 1 < NCK:
                emit_wfs_dma(it + 1)
            if it - 2 < 8:
                ss = it - 2
                v_emit(lambda v, ss=ss: v.tensor_single_scalar(
                    x2[:, ss * D:(ss + 1) * D], x2[:, ss * D:(ss + 1) * D],
                    sr8[:, 8 + ss:9 + ss], ALU.subtract),
                    waits=[(sse, scq_s[ss])])
                nrm_v[ss] = v_emit(lambda v, ss=ss: v.tensor_single_scalar(
                    x2[:, ss * D:(ss + 1) * D], x2[:, ss * D:(ss + 1) * D],
                    sr8[:, 40 + ss:41 + ss], ALU.mult))
            if it % 4 == 0 and it >= 4:
                g = it // 4 - 1
                for ss in range(8):
                    v_emit(lambda v, ss=ss, g=g: v.reduce_max(
                        pmax[:, ss * 4 + g: ss * 4 + g + 1],
                        rtq[:, ss * DI + g * 1024: ss * DI + (g + 1) * 1024],
                        AX.X, apply_absolute_value=True),
                        waits=[(sse, evac_mark[(4 * g + 3, ss)])])
            for ss in range(8):
                evac_mark[(it, ss)] = emit_c2_mm(it, ss)
            up_t[it % 2] = tmm.n
        for ss in range(8):
            v_emit(lambda v, ss=ss: v.reduce_max(
                pmax[:, ss * 4 + 3: ss * 4 + 4],
                rtq[:, ss * DI + 3 * 1024: ss * DI + 4 * 1024],
                AX.X, apply_absolute_value=True),
                waits=[(sse, evac_mark[(15, ss)])])
        c2_last = tmm.n
        x2_v = nrm_v[7]

        if dbg:
            k.emit("sync", lambda sy: sy.dma_start(
                dbgt["d_x2"].ap(), x2[:]), waits=[(vve, x2_v)] + k.ch(sdbg),
                inc=sdbg, inc_k=16)
            k.chend(sdbg)
        dbg_gate2 = sdbg.n


        if dbg:
            k.emit("sync", lambda sy: sy.dma_start(
                dbgt["d_rt"].ap(), rtq[:, 0:8 * DI]),
                waits=[(vve, last_v[0])] + k.ch(sdbg), inc=sdbg, inc_k=16)
            k.chend(sdbg)
        dbg_gate3 = sdbg.n

        for ss in range(8):
            v_emit(lambda v, ss=ss: v.reduce_max(
                sr8[:, ss:ss + 1], pmax[:, ss * 4:(ss + 1) * 4], AX.X))
        v_emit(lambda v: v.tensor_mul(sr8[:, 8:16], sr8[:, 0:8], sr8[:, 0:8]))
        v_emit(lambda v: v.tensor_single_scalar(
            sr8[:, 8:16], sr8[:, 8:16], Q_EPS, ALU.max))
        v_emit(lambda v: v.reciprocal(sr8[:, 16:24], sr8[:, 8:16]))
        v_emit(lambda v: v.tensor_single_scalar(
            sr8[:, 16:24], sr8[:, 16:24], 127.0, ALU.mult))
        fv2 = v_emit(lambda v: v.tensor_scalar(
            f2col[:, 0:8], sr8[:, 8:16], 1.0 / 127.0, swdnb[:, 0:1],
            ALU.mult, ALU.mult))
        sqv = s_emit(lambda s: s.activation(sc3[:, 0:8], sr8[:, 16:24],
                                            AF.Sqrt),
                     waits=[(vve, fv2)])

        dn_ld = {}
        dn_t = [0, 0]
        terndn_v = {}
        tern_prev = {}

        def emit_wfd_dma(idx):
            sb_ = idx % 2
            qeng = "gpsimd" if sb_ == 0 else "sync"
            w0 = [(vve, tern_prev[idx - 2])] if idx >= 2 else [(tmm, c2_last)]
            ht, qb = idx // 4, idx % 4
            k.emit(qeng, lambda g, ht=ht, qb=qb, sb_=sb_: g.dma_start(
                wfd[sb_][:].rearrange("p (c w) -> p c w", w=256),
                wdnT.ap()[qb * 1024:(qb + 1) * 1024,
                          ht * 256:(ht + 1) * 256].rearrange(
                    "(c p) w -> p c w", p=128)),
                waits=w0 + [(sdbg, dbg_gate3)] + k.ch(s_dn[sb_]),
                inc=s_dn[sb_], inc_k=16)
            dn_ld[idx] = s_dn[sb_].n
            k.chend(s_dn[sb_])

        def emit_tern_dn_idx(idx):
            ht, qb = divmod(idx, 4)
            buf = ht % 2
            sb_ = idx % 2
            extra = [(tmm, c2_last)] if ht < 2 else []
            v_emit(lambda v, sb_=sb_: v.tensor_scalar(
                tnb[:, 0:2048], wfd[sb_][:, 0:2048], -1.0,
                thrdn[:, 0:1], ALU.mult, ALU.is_ge),
                waits=[(s_dn[sb_], dn_ld[idx]), (tmm, dn_t[buf])] + extra)
            tv = v_emit(lambda v, qb=qb, sb_=sb_, buf=buf: v.scalar_tensor_tensor(
                wdb[buf][:, qb * 2048:(qb + 1) * 2048],
                wfd[sb_][:, 0:2048], thrdn[:, 0:1],
                tnb[:, 0:2048], ALU.is_ge, ALU.subtract))
            tern_prev[idx] = tv
            if qb == 3:
                terndn_v[ht] = tv

        nd_dn = [0]
        nt_dn = [0]

        def adv_dn(tern_upto):
            while nt_dn[0] < tern_upto:
                while nd_dn[0] < min(nt_dn[0] + 2, 32):
                    emit_wfd_dma(nd_dn[0])
                    nd_dn[0] += 1
                emit_tern_dn_idx(nt_dn[0])
                nt_dn[0] += 1

        rq_done = [0] * 8
        tp3_done = {}

        def qbase(ss):
            return ((ss + 8) % 9) * DI

        qset = [(sciT, scq, adjb), (sciT2, scq2, adjb2)]
        rq_hist = {}
        pend_cells = []
        cell_k = [0]

        def emit_c4_mm(ht, ss):
            kk = cell_k[0]
            cell_k[0] += 1
            bank = 4 + kk % 4
            buf = ht % 2
            wts = [(vve, terndn_v[ht]), (vve, rq_done[ss])] + bank_w(bank)
            for ic in range(NCI):
                k.emit("tensor", lambda t, ss=ss, ic=ic, bank=bank, buf=buf: t.matmul(
                    ps[bank][:, 0:256],
                    rtq[:, qbase(ss) + ic * 128: qbase(ss) + ic * 128 + 128],
                    wdb[buf][:, ic * 256:(ic + 1) * 256],
                    start=(ic == 0), stop=(ic == NCI - 1)),
                    waits=wts if ic == 0 else (),
                    inc=tmm if ic == NCI - 1 else None)
            pend_cells.append((ht, ss, bank, tmm.n))

        def flush_cells(keep=0):
            while len(pend_cells) > keep:
                ht, ss, bank, dmm = pend_cells.pop(0)
                yv = v_emit(lambda v, ss=ss, ht=ht, bank=bank: v.scalar_tensor_tensor(
                    x2[:, ss * D + ht * 256: ss * D + (ht + 1) * 256],
                    ps[bank][:, 0:256], f2col[:, ss:ss + 1],
                    x2[:, ss * D + ht * 256: ss * D + (ht + 1) * 256],
                    ALU.mult, ALU.add), waits=[(tmm, dmm)])
                bank_guard[bank] = (vve, yv)
                s_emit(lambda s, ss=ss, ht=ht: s.activation(
                    sqscr[:, 0:256],
                    x2[:, ss * D + ht * 256: ss * D + (ht + 1) * 256],
                    AF.Copy,
                    accum_out=pgrid[:, ss * 8 + ht: ss * 8 + ht + 1]),
                    waits=[(vve, yv)])
                s_emit(lambda s, ss=ss, ht=ht: s.activation(
                    sqscr[:, 0:256],
                    x2[:, ss * D + ht * 256: ss * D + (ht + 1) * 256],
                    AF.Square,
                    accum_out=qgrid[:, ss * 8 + ht: ss * 8 + ht + 1]))

        for ss in range(8):
            for qc in range(4):
                step = ss * 4 + qc
                sci_, scq_, adj_ = qset[(step % 4) // 2]
                off = (step % 2) * 1024
                o0 = ss * DI + qc * 1024
                wts_ci = [(sdbg, dbg_gate3),
                          (vve, rq_hist[step - 4] if step >= 4
                           else last_v[0])]
                s_emit(lambda s, o0=o0, off=off, ss=ss, sci_=sci_: s.activation(
                    sci_[:, off:off + 1024], rtq[:, o0:o0 + 1024], AF.Square,
                    scale=sc3[:, ss:ss + 1]), waits=wts_ci)
                sqv2 = s_emit(lambda s, o0=o0, off=off, ss=ss, scq_=scq_: s.activation(
                    scq_[:, off:off + 1024], rtq[:, o0:o0 + 1024], AF.Square,
                    scale=sc3[:, ss:ss + 1]))
                v_emit(lambda v, off=off, sci_=sci_, scq_=scq_, adj_=adj_: v.scalar_tensor_tensor(
                    adj_[:, off:off + 1024], scq_[:, off:off + 1024], -0.5,
                    sci_[:, off:off + 1024], ALU.add, ALU.is_gt),
                    waits=[(sse, sqv2)])
                rqv = v_emit(lambda v, off=off, sci_=sci_, adj_=adj_: v.scalar_tensor_tensor(
                    rqtile[:, off:off + 1024], sci_[:, off:off + 1024],
                    0.0, adj_[:, off:off + 1024], ALU.add, ALU.add),
                    waits=[(tmm, tp3_done.get(step - 2, b3_last))])
                rq_hist[step] = rqv
                bank = step % 2
                wts = [(vve, rqv)] + bank_w(bank)
                for j in range(8):
                    k.emit("tensor", lambda t, off=off, j=j, bank=bank: t.transpose(
                        psb[bank][:, j * 128:(j + 1) * 128],
                        rqtile[:, off + j * 128: off + (j + 1) * 128],
                        idb[:]),
                        waits=wts if j == 0 else (),
                        inc=tmm if j == 7 else None)
                tp3_done[step] = tmm.n
                ev = v_emit(lambda v, ss=ss, qc=qc, bank=bank: v.tensor_copy(
                    rtq[:, qbase(ss) + qc * 1024: qbase(ss) + (qc + 1) * 1024],
                    psb[bank][:, 0:1024]), waits=[(tmm, tp3_done[step])])
                bank_guard[bank] = (vve, ev)
            rq_done[ss] = last_v[0]
            flush_cells()
            if ss < 2:
                adv_dn((ss + 1) * 4)
            emit_c4_mm(0, ss)
            if ss >= 1:
                emit_c4_mm(1, ss - 1)
        emit_c4_mm(1, 7)
        flush_cells()
        dn_t[0] = tmm.n
        dn_t[1] = tmm.n

        adv_dn(12)
        for ht in range(2, 8):
            for ss in range(8):
                emit_c4_mm(ht, ss)
                flush_cells(keep=2)
                if ss == 2 and ht < 7:
                    adv_dn((ht + 2) * 4)
            dn_t[ht % 2] = tmm.n
        flush_cells()
        y2_v = last_v[0]
        y2_s = last_s[0]

        if dbg:
            k.emit("sync", lambda sy: sy.dma_start(
                dbgt["d_y2"].ap(), x2[:]), waits=[(vve, y2_v), (sse, y2_s)]
                + k.ch(sdbg), inc=sdbg, inc_k=16)
            k.chend(sdbg)

        ob_st = [0, 0]
        for ss in range(8):
            obp = ss % 2
            v_emit(lambda v, ss=ss: v.reduce_sum(
                sr8[:, ss:ss + 1], pgrid[:, ss * 8:(ss + 1) * 8], AX.X),
                waits=[(sse, y2_s)])
            v_emit(lambda v, ss=ss: v.reduce_sum(
                sr8[:, 24 + ss:25 + ss], qgrid[:, ss * 8:(ss + 1) * 8], AX.X))
            v_emit(lambda v, ss=ss: v.tensor_single_scalar(
                sr8[:, 8 + ss:9 + ss], sr8[:, ss:ss + 1], 1.0 / D, ALU.mult))
            v_emit(lambda v, ss=ss: v.tensor_mul(
                sr8[:, 16 + ss:17 + ss], sr8[:, 8 + ss:9 + ss],
                sr8[:, 8 + ss:9 + ss]))
            v_emit(lambda v, ss=ss: v.tensor_single_scalar(
                sr8[:, 32 + ss:33 + ss], sr8[:, 24 + ss:25 + ss], 1.0 / D,
                ALU.mult))
            v_emit(lambda v, ss=ss: v.scalar_tensor_tensor(
                sr8[:, 32 + ss:33 + ss], sr8[:, 32 + ss:33 + ss], LN_EPS,
                sr8[:, 16 + ss:17 + ss], ALU.add, ALU.subtract))
            sv = s_emit(lambda s, ss=ss: s.activation(
                sr8[:, 40 + ss:41 + ss], sr8[:, 32 + ss:33 + ss], AF.Sqrt),
                waits=[(vve, last_v[0])])
            v_emit(lambda v, ss=ss: v.reciprocal(
                sr8[:, 40 + ss:41 + ss], sr8[:, 40 + ss:41 + ss]),
                waits=[(sse, sv)])
            v_emit(lambda v, ss=ss, obp=obp: v.tensor_single_scalar(
                obuf[obp][:], x2[:, ss * D:(ss + 1) * D],
                sr8[:, 8 + ss:9 + ss], ALU.subtract),
                waits=[(sst, ob_st[obp])])
            outv = v_emit(lambda v, ss=ss, obp=obp: v.tensor_single_scalar(
                obuf[obp][:], obuf[obp][:], sr8[:, 40 + ss:41 + ss],
                ALU.mult))
            k.emit("scalar", lambda s, ss=ss, obp=obp: s.dma_start(
                out.ap()[ss * 128:(ss + 1) * 128, :], obuf[obp][:]),
                waits=[(vve, outv)] + k.ch(sst), inc=sst, inc_k=16)
            ob_st[obp] = sst.n
            k.chend(sst)

        fin = sst.n
        k.emit("sync", lambda sy: sy.nop(),
               waits=[(sst, fin), (sdbg, sdbg.n)])

        with nc.Block() as blk:
            for eng in K.ENGINES:
                stream = k.ops[eng]
                if not stream:
                    continue

                def mk(stream=stream):
                    def _(e):
                        for f in stream:
                            f(e)
                    return _
                getattr(blk, eng)(mk())
    return nc


def _core_inputs(x_b, core):
    h = core % 2
    order = LOCAL_ORDER[h]
    perm = np.concatenate([np.arange(c * CH, (c + 1) * CH) for c in order])
    xTl = np.ascontiguousarray(x_b.T[:, perm])
    own = order[:2]
    x_own_ = np.ascontiguousarray(
        np.concatenate([x_b[c * CH:(c + 1) * CH] for c in own], axis=0))
    bias = np.zeros((128, 4), np.float32)
    for (po, pc, kind, bi) in PAIRS:
        if kind != "bias":
            continue
        q_g, k_g = order[po], order[pc]
        bias[:, bi] = 0.0 if k_g < q_g else -1e9
    return xTl, x_own_, bias, own


_CONSTS = None


def _consts():
    global _CONSTS
    if _CONSTS is None:
        tri = np.zeros((128, 4 * CH), BF)
        p = np.arange(128)[:, None]
        q = np.arange(CH)[None, :]
        for kb in range(4):
            tri[:, kb * CH:(kb + 1) * CH] = np.where(
                (kb * 128 + p) <= q, 0.0, -1e9).astype(BF)
        _CONSTS = {
            "tri": tri,
            "identb": np.eye(128, dtype=BF),
            "onescb": np.ones((128, 1), BF),
            "onesrf": np.ones((1, 128), np.float32),
            "onescf": np.ones((128, 1), np.float32),
        }
    return _CONSTS


def kernel(**inputs):
    cst = _consts()
    f32c = lambda a: np.ascontiguousarray(np.asarray(a, dtype=np.float32))
    wqTn = f32c(np.asarray(inputs["L0_attn_w_q"]).T)
    wkTn = f32c(np.asarray(inputs["L0_attn_w_k"]).T)
    wvTn = f32c(np.asarray(inputs["L0_attn_w_v"]).T)
    woTn = f32c(np.asarray(inputs["L0_attn_w_o"]).T)
    memn = f32c(np.asarray(inputs["L0_attn_memories"]).reshape(M4 * D, D))
    mnTn = f32c(np.asarray(inputs["L0_attn_memory_norms"]).T)
    wupTn = f32c(np.asarray(inputs["L0_ffn_w_up"]).T)
    wdnTn = f32c(np.asarray(inputs["L0_ffn_w_down"]).T)
    gaten = f32c(np.asarray(inputs["L0_attn_gate"]).reshape(1, 1))
    xall = np.asarray(inputs["L0_attn_x"], dtype=np.float32)

    nc = build_program()
    in_maps = []
    owns = []
    for core in range(8):
        xTl, x_own_, bias, own = _core_inputs(xall[core // 2], core)
        owns.append(own)
        in_maps.append({
            "xT": xTl, "x_own": x_own_, "wqT": wqTn, "wkT": wkTn,
            "wvT": wvTn, "woT": woTn, "mem": memn, "mnT": mnTn,
            "wupT": wupTn, "wdnT": wdnTn, "gate": gaten, "tri": cst["tri"],
            "biasc": bias, "identb": cst["identb"], "onescb": cst["onescb"],
            "onesrf": cst["onesrf"], "onescf": cst["onescf"],
        })
    res = run_bass_kernel_spmd(nc, in_maps, core_ids=list(range(8)))
    outp = np.empty((4, S, D), np.float32)
    for core in range(8):
        b = core // 2
        o = np.asarray(res.results[core]["out"])
        for j, c in enumerate(owns[core]):
            outp[b, c * CH:(c + 1) * CH] = o[j * CH:(j + 1) * CH]
    return outp
